# version 02
# baseline (speedup 1.0000x reference)
import numpy as np
import ml_dtypes
import concourse.bass as bass
import concourse.mybir as mybir
from concourse.bass_utils import run_bass_kernel_spmd

F32 = mybir.dt.float32
BF16 = mybir.dt.bfloat16
AF = mybir.ActivationFunctionType
ALU = mybir.AluOpType
AX = mybir.AxisListType

NCORE = 8
T = 1024
HT = 1056
HALO = 32
DIM = 4096
H, P, G, N, CS, KCONV = 128, 64, 8, 128, 128, 4
INTER = 8192
CONV_DIM = INTER + 2 * G * N
NPROJ = INTER + CONV_DIM + H
NCH = T // CS
EPS = 1e-5
NH, HD = 32, 128
WINDOW = 1024
BF = ml_dtypes.bfloat16



class EngT:

    def __init__(self, name):
        self.name = name
        self.n = 0
        self.sem = None
        self.cmds = []

    def op(self, method, *args, **kwargs):
        prev = self.n
        self.n += 1

        def fn(eng, prev=prev, method=method, args=args, kwargs=kwargs):
            if prev > 0:
                eng.wait_ge(self.sem, prev)
            getattr(eng, method)(*args, **kwargs).then_inc(self.sem, 1)
        self.cmds.append(fn)
        return self.n

    def wait(self, other, val):
        if val <= 0:
            return

        def fn(eng, other=other, val=val):
            eng.wait_ge(other.sem, val)
        self.cmds.append(fn)

    def raw(self, fn):
        self.cmds.append(fn)


class PeT(EngT):

    def op(self, method, *args, mark=False, **kwargs):
        if mark:
            self.n += 1

            def fn(eng, method=method, args=args, kwargs=kwargs):
                getattr(eng, method)(*args, **kwargs).then_inc(self.sem, 1)
        else:
            def fn(eng, method=method, args=args, kwargs=kwargs):
                getattr(eng, method)(*args, **kwargs)
        self.cmds.append(fn)
        return self.n

    def mm(self, out, lhsT, rhs, start, stop, mark=False):
        return self.op("matmul", out, lhsT, rhs, start=start, stop=stop, mark=mark)

    def mark(self):
        self.n += 1

        def fn(eng):
            eng.drain().then_inc(self.sem, 1)
        self.cmds.append(fn)
        return self.n


class QT:

    def __init__(self, name):
        self.name = name
        self.n = 0
        self.sem = None
        self.cmds = []

    def dma(self, out, in_, **kw):
        self.n += 16

        def fn(eng, out=out, in_=in_, kw=kw):
            eng.dma_start(out=out, in_=in_, **kw).then_inc(self.sem, 16)
        self.cmds.append(fn)
        return self.n

    def dmat(self, out, in_):
        self.n += 16

        def fn(eng, out=out, in_=in_):
            eng.dma_start_transpose(out, in_).then_inc(self.sem, 16)
        self.cmds.append(fn)
        return self.n

    def wait(self, other, val):
        if val <= 0:
            return

        def fn(eng, other=other, val=val):
            eng.wait_ge(other.sem, val)
        self.cmds.append(fn)

    def raw(self, fn):
        self.cmds.append(fn)


class Ctx:
    def __init__(self, name, final_outputs=()):
        self.name = name
        self.nc = bass.Bass("TRN2", target_bir_lowering=False)
        self.pe = PeT("pe")
        self.v = EngT("v")
        self.a = EngT("a")
        self.qs = QT("qs")
        self.qg = QT("qg")
        self.qa = QT("qa")
        self._guards = []
        self.input_names = []
        self.output_names = []
        self._live_dram = {}
        self._final_outputs = set(final_outputs)
        self._uniq = 0

    def din(self, name, shape, dtype=F32):
        if name in self._live_dram:
            return self._live_dram[name]
        t = self.nc.dram_tensor(name, list(shape), dtype, kind="ExternalInput")
        self.input_names.append(name)
        ap = t.ap()
        self._live_dram[name] = ap
        return ap

    def dout(self, name, shape, dtype=F32):
        if self._final_outputs and name not in self._final_outputs:
            t = self.nc.dram_tensor(name, list(shape), dtype, kind="Internal")
        else:
            t = self.nc.dram_tensor(name, list(shape), dtype, kind="ExternalOutput")
            self.output_names.append(name)
        ap = t.ap()
        self._live_dram[name] = ap
        return ap

    def phase_mark(self):
        return len(self._guards)

    def phase_free(self, mark):
        for g in reversed(self._guards[mark:]):
            g.__exit__(None, None, None)
        del self._guards[mark:]
        streams = [self.pe, self.v, self.a, self.qs, self.qg, self.qa]
        for wtr in streams:
            for p in streams:
                if p is not wtr:
                    wtr.wait(p, p.n)

    def dram(self, name, shape, dtype=F32):
        self._uniq += 1
        name = f"{name}_{self._uniq}"
        return self.nc.dram_tensor(name, list(shape), dtype, kind="Internal").ap()

    def sb(self, name, shape, dtype):
        self._uniq += 1
        name = f"{name}_{self._uniq}"
        g = self.nc.sbuf_tensor(name, list(shape), dtype)
        t = g.__enter__()
        self._guards.append(g)
        return t

    def ps(self, name, shape, dtype=F32):
        self._uniq += 1
        name = f"{name}_{self._uniq}"
        g = self.nc.psum_tensor(name, list(shape), dtype)
        t = g.__enter__()
        self._guards.append(g)
        return t

    def barrier(self, waiter, producers=None):
        if producers is None:
            producers = [e for e in (self.pe, self.v, self.a, self.qs, self.qg, self.qa)
                         if e is not waiter]
        for p in producers:
            waiter.wait(p, p.n)

    def finish_outputs(self):
        self.qs.wait(self.qs, self.qs.n)
        self.qg.wait(self.qg, self.qg.n)
        self.qa.wait(self.qa, self.qa.n)

    def build(self):
        nc = self.nc
        streams = [self.pe, self.v, self.a, self.qs, self.qg, self.qa]
        semg = []
        for s in streams:
            g = nc.semaphore("sem_" + s.name)
            s.sem = g.__enter__()
            semg.append(g)
        blockg = nc.Block()
        block = blockg.__enter__()

        def run(dec, obj):
            @dec
            def _(eng):
                for fn in obj.cmds:
                    fn(eng)
        run(block.tensor, self.pe)
        run(block.vector, self.v)
        run(block.scalar, self.a)
        run(block.sync, self.qs)
        run(block.gpsimd, self.qg)
        run(block.scalar, self.qa)
        blockg.__exit__(None, None, None)
        for g in semg:
            g.__exit__(None, None, None)
        for g in reversed(self._guards):
            g.__exit__(None, None, None)
        return nc


def cast_np(a, dtype=np.float32):
    return np.ascontiguousarray(np.asarray(a), dtype=dtype)



def prep_common(w):
    c = {}
    c["W_inT"] = cast_np(w["L0_attn_in_proj_w"].T)
    c["conv_wR"] = cast_np(w["L0_attn_conv_w"][:, 0, :].reshape(80, 128, KCONV).transpose(1, 0, 2))
    c["conv_bR"] = cast_np(w["L0_attn_conv_b"].reshape(80, 128).T)
    c["dt_bias"] = cast_np(w["L0_attn_dt_bias"].reshape(128, 1))
    c["A_log"] = cast_np(w["L0_attn_A_log"].reshape(128, 1))
    c["Dvec"] = cast_np(w["L0_attn_D"].reshape(128, 1))
    c["norm_wR"] = cast_np(w["L0_attn_norm_w"].reshape(64, 128).T)
    c["W_outT"] = cast_np(w["L0_attn_out_proj_w"].T)
    c["F0_WT"] = cast_np(w["L0_ffn_W"].T)
    c["F0_biasR"] = cast_np(w["L0_ffn_bias"].reshape(32, 128).T)
    c["F0_AcatT"] = cast_np(w["L0_ffn_A"].reshape(256, DIM).T)
    c["F0_Bcat"] = cast_np(w["L0_ffn_Bf"].transpose(0, 2, 1).reshape(256, DIM))
    c["F0_coefR"] = cast_np(np.repeat(w["L0_ffn_coeffs"], 16).reshape(2, 128).T)
    c["F0_scalR"] = cast_np(np.repeat(w["L0_ffn_scales"], 16).reshape(2, 128).T)
    c["WqT"] = cast_np(w["L1_attn_Wq"].T)
    c["WkT"] = cast_np(w["L1_attn_Wk"].T)
    c["WvT"] = cast_np(w["L1_attn_Wv"].T)
    c["WoT"] = cast_np(w["L1_attn_Wo"].T)
    c["F1_WT"] = cast_np(w["L1_ffn_W_base"].T)
    c["F1_biasR"] = cast_np(w["L1_ffn_b_base"].reshape(32, 128).T)
    c["F1_WrT"] = cast_np(w["L1_ffn_W_router"].T)
    c["F1_AcatT"] = cast_np(w["L1_ffn_A"].reshape(128, DIM).T)
    c["F1_Bcat"] = cast_np(w["L1_ffn_B"].transpose(0, 2, 1).reshape(128, DIM))
    E8 = np.zeros((8, 128), np.float32)
    for e in range(8):
        E8[e, e * 16:(e + 1) * 16] = 1.0
    c["E8"] = cast_np(E8, BF)
    c["D_B"] = cast_np(np.broadcast_to(w["L0_attn_D"].reshape(1, 128), (128, 128)))
    c["ident_f"] = cast_np(np.eye(128))
    c["onescol"] = cast_np(np.ones((128, 1)), BF)
    c["onesrow"] = cast_np(np.ones((1, 128)), BF)
    c["ident"] = cast_np(np.eye(128), BF)
    c["tri"] = cast_np(np.triu(np.ones((128, 128))), BF)
    return c


def prep_core(ci, w):
    x = w["L0_attn_hidden_states"][0]
    d = {}
    t0 = ci * T
    halo = np.zeros((HALO, DIM), np.float32) if ci == 0 else x[t0 - HALO:t0]
    d["x_in"] = cast_np(np.concatenate([halo, x[t0:t0 + T]], 0))
    cm = np.zeros((1, 8), np.float32)
    cm[0, :ci] = 1.0
    d["carrymask"] = cm
    d["carrymaskR"] = cast_np(np.broadcast_to(cm, (128, 8)))
    pos = np.arange(t0, t0 + T, dtype=np.float64)
    inv = 1.0 / 10000.0 ** (np.arange(0, HD, 2) / HD)
    fr = pos[:, None] * inv[None, :]
    emb = np.concatenate([fr, fr], -1)
    cos = np.cos(emb)
    sin = np.sin(emb)
    sin_signed = np.concatenate([-sin[:, :HD // 2], sin[:, HD // 2:]], -1)
    d["rope_cos"] = cast_np(cos.reshape(8, 128, HD).transpose(1, 0, 2))
    d["rope_sin"] = cast_np(sin_signed.reshape(8, 128, HD).transpose(1, 0, 2))
    m = np.zeros((2, 12, 128, 512), np.float32)
    for qg_ in range(2):
        q0 = qg_ * 512
        for kb in range(12):
            kstart = q0 - 1024 + kb * 128
            kidx = np.arange(128)[:, None] + kstart
            qidx = np.arange(512)[None, :] + q0
            rel = qidx - kidx
            ok = (rel >= 0) & (rel < WINDOW)
            if ci == 0:
                ok = ok & (kidx >= 0)
            m[qg_, kb] = ok
    d["wmask"] = cast_np(m, BF)
    return d



class LnKit:

    def __init__(self, ctx, tag="", slim=False):
        self.ctx = ctx
        self.x = ctx.sb(f"ln{tag}_x", [128, DIM], F32)
        self.sq = ctx.sb(f"ln{tag}_sq", [128, DIM], F32)
        self.u = None if slim else ctx.sb(f"ln{tag}_u", [128, DIM], BF16)
        self.s1 = ctx.sb(f"ln{tag}_s1", [128, 1], F32)
        self.s2 = ctx.sb(f"ln{tag}_s2", [128, 1], F32)
        self.mneg = ctx.sb(f"ln{tag}_m", [128, 1], F32)
        self.msq = ctx.sb(f"ln{tag}_m2", [128, 1], F32)
        self.var = ctx.sb(f"ln{tag}_v", [128, 1], F32)
        self.rstd = ctx.sb(f"ln{tag}_r", [128, 1], F32)
        self.last_tp = 0

    def stats_and_u(self, xt, rows, ubuf):
        ctx = self.ctx
        v, a = ctx.v, ctx.a
        v.op("reduce_sum", self.s1[0:rows, :], xt[0:rows, :], axis=AX.X)
        a.wait(v, v.n - 1)
        a.op("activation", self.sq[0:rows, :], xt[0:rows, :], AF.Square,
             accum_out=self.s2[0:rows, 0:1])
        v.op("tensor_scalar", self.mneg[0:rows, :], self.s1[0:rows, :],
             -1.0 / DIM, None, ALU.mult)
        v.op("tensor_tensor", self.msq[0:rows, :], self.mneg[0:rows, :],
             self.mneg[0:rows, :], ALU.mult)
        v.wait(a, a.n)
        v.op("tensor_scalar", self.var[0:rows, :], self.s2[0:rows, :],
             1.0 / DIM, EPS, ALU.mult, ALU.add)
        v.op("tensor_tensor", self.var[0:rows, :], self.var[0:rows, :],
             self.msq[0:rows, :], ALU.subtract)
        a.wait(v, v.n)
        a.op("activation", self.rstd[0:rows, :], self.var[0:rows, :], AF.Sqrt)
        v.wait(a, a.n)
        v.op("reciprocal", self.rstd[0:rows, :], self.rstd[0:rows, :])
        v.op("tensor_scalar", ubuf[0:rows, :], xt[0:rows, :],
             self.mneg[0:rows, 0:1], self.rstd[0:rows, 0:1], ALU.add, ALU.mult)
        return v.n

    def tile_from_hbm(self, x_h, row0, rows, uT=None, ucol=None, u_keep=None):
        ctx = self.ctx
        xt, ub = self.x, self.u
        ctx.qs.wait(ctx.v, ctx.v.n)
        ctx.qs.wait(ctx.a, ctx.a.n)
        ld = ctx.qs.dma(xt[0:rows, :], x_h[row0:row0 + rows, :])
        ctx.v.wait(ctx.qs, ld)
        ctx.a.wait(ctx.qs, ld)
        if self.last_tp:
            ctx.v.wait(ctx.qs, self.last_tp)
        done = self.stats_and_u(xt, rows, ub)
        if u_keep is not None:
            ctx.v.op("tensor_copy", u_keep[0:rows, :], ub[0:rows, :])
        if uT is not None:
            ctx.qs.wait(ctx.v, done)
            tp = ctx.qs.dmat(uT[:, :, ucol:ucol + rows],
                             ub[0:rows, :].rearrange("t (a r) -> t a r", a=32))
            self.last_tp = tp
        return xt


def emit_weight_gemm(ctx, wT_h, uT, n_k, do_tiles, n_cols, evac, wname,
                     extra_mms=None, pre_tile=None):
    pe, qg, v = ctx.pe, ctx.qg, ctx.v
    wb = [ctx.sb(f"{wname}0", [128, n_k, 128], BF16),
          ctx.sb(f"{wname}1", [128, n_k, 128], BF16)]
    psl = [ctx.ps(f"{wname}_psA", [128, 512], F32),
           ctx.ps(f"{wname}_psB", [128, 512], F32),
           ctx.ps(f"{wname}_psC", [128, 512], F32),
           ctx.ps(f"{wname}_psD", [128, 512], F32)]
    nsplit = [(i, min(512, n_cols - i)) for i in range(0, n_cols, 512)]
    assert len(nsplit) <= 2
    pe_done_at = {}
    v_done_at = {}
    for d in range(do_tiles):
        b = wb[d % 2]
        if d >= 2:
            qg.wait(pe, pe_done_at[d - 2])
        ld = qg.dma(b[:, :, :],
                    wT_h[:, d * 128:(d + 1) * 128].rearrange("(a k) m -> k a m", k=128))
        if pre_tile is not None:
            ld = pre_tile(d) or ld
        pe.raw(lambda eng, ld=ld: eng.wait_ge(qg.sem, ld))
        if d >= 4:
            pe.raw(lambda eng, val=v_done_at[d - 4]: eng.wait_ge(v.sem, val))
        pss = [psl[(2 * d) % 4], psl[(2 * d + 1) % 4]]
        for ci, (c0, cw) in enumerate(nsplit):
            ps = pss[ci]
            for k in range(n_k):
                pe.mm(ps[:, 0:cw], b[:, k, :], uT[:, k, c0:c0 + cw],
                      start=(k == 0), stop=(extra_mms is None and k == n_k - 1))
            if extra_mms is not None:
                extra_mms(d, ps, c0, cw)
        pe.mark()
        pe_done_at[d] = pe.n
        evac(d, pss, nsplit)
        v_done_at[d] = v.n
    return




def emit_post_ln_residual(ctx, ln, src_fm, x_old_h, x_new_h):
    v, a, qs = ctx.v, ctx.a, ctx.qs
    tm = ctx.sb("plr_tm_" + ctx.name, [128, DIM], BF16)
    xold = ctx.sb("plr_xold_" + ctx.name, [128, DIM], F32)
    nrm = ln.sq
    for i in range(8):
        qs.wait(v, v.n)
        tp = qs.dmat(tm[:, :], src_fm[:, i * 128:(i + 1) * 128].opt())
        ld = qs.dma(xold[:, :], x_old_h[i * 128:(i + 1) * 128, :])
        v.wait(qs, tp)
        v.wait(qs, ld)
        a.wait(qs, tp)
        a.wait(v, v.n)
        ln.stats_and_u(tm, 128, nrm)
        v.op("tensor_tensor", nrm[:, :], nrm[:, :], xold[:, :], ALU.add)
        qs.wait(v, v.n)
        qs.dma(x_new_h[i * 128:(i + 1) * 128, :], nrm[:, :])


def build_phaseB(ctx=None):
    if ctx is None:
        ctx = Ctx("phaseB")
    x1 = ctx.din("x1", (T, DIM))
    WT = ctx.din("F0_WT", (DIM, DIM))
    biasR = ctx.din("F0_biasR", (128, 32))
    AcatT = ctx.din("F0_AcatT", (DIM, 256))
    Bcat = ctx.din("F0_Bcat", (256, DIM))
    coefR = ctx.din("F0_coefR", (128, 2))
    scalR = ctx.din("F0_scalR", (128, 2))
    x2 = ctx.dout("x2", (T, DIM))
    ffn_h = ctx.dram("ffn_h", (DIM, T), BF16)

    uT = ctx.sb("uT", [128, 32, T], BF16)
    ln = LnKit(ctx, "b")
    bias_s = ctx.sb("bias_s", [128, 32], F32)
    coef_s = ctx.sb("coef_s", [128, 2], F32)
    scal_s = ctx.sb("scal_s", [128, 2], F32)
    AcT_s = ctx.sb("AcT_s", [128, 32, 256], BF16)
    bcb = [ctx.sb("bcb0", [128, 2, 128], BF16), ctx.sb("bcb1", [128, 2, 128], BF16)]
    xa = ctx.sb("xa", [128, 2, T], BF16)
    stage = ctx.sb("stage", [128, T], BF16)
    psX = ctx.ps("psX", [128, 512], F32)
    psY = ctx.ps("psY", [128, 512], F32)

    pe, v, a, qs, qg = ctx.pe, ctx.v, ctx.a, ctx.qs, ctx.qg

    qg.dma(bias_s[:, :], biasR[:, :])
    qg.dma(coef_s[:, :], coefR[:, :])
    qg.dma(scal_s[:, :], scalR[:, :])
    qg.dma(AcT_s[:, :, :], AcatT[:, :].rearrange("(a k) m -> k a m", k=128))

    for i in range(8):
        ln.tile_from_hbm(x1, i * 128, 128, uT=uT, ucol=i * 128)

    ctx.barrier(pe, [ctx.qs, ctx.v, ctx.qg])
    for er in range(2):
        for ci in range(2):
            ps = psX if ci == 0 else psY
            for k in range(32):
                pe.mm(ps[:, :], AcT_s[:, k, er * 128:(er + 1) * 128],
                      uT[:, k, ci * 512:(ci + 1) * 512], start=(k == 0), stop=(k == 31))
            pe.mark()
            v.wait(pe, pe.n)
            v.op("tensor_scalar", xa[:, er, ci * 512:(ci + 1) * 512],
                 ps[:, :], coef_s[:, er:er + 1], scal_s[:, er:er + 1],
                 ALU.mult, ALU.mult)
    pe.raw(lambda eng, val=v.n: eng.wait_ge(v.sem, val))

    def pre_tile(d):
        return qg.dma(bcb[d % 2][:, :, :],
                      Bcat[:, d * 128:(d + 1) * 128].rearrange("(c k) m -> k c m", k=128))

    def extra(d, ps, c0, cw):
        pe.mm(ps[:, 0:cw], bcb[d % 2][:, 0, :], xa[:, 0, c0:c0 + cw], start=False, stop=False)
        pe.mm(ps[:, 0:cw], bcb[d % 2][:, 1, :], xa[:, 1, c0:c0 + cw], start=False, stop=True)

    def evac(d, pss, nsplit):
        v.wait(pe, pe.n)
        for ci, (c0, cw) in enumerate(nsplit):
            v.op("tensor_scalar", stage[:, c0:c0 + cw], pss[ci][:, 0:cw],
                 bias_s[:, d:d + 1], None, ALU.add)
        qs.wait(v, v.n)
        st = qs.dma(ffn_h[d * 128:(d + 1) * 128, :], stage[:, :])
        v.wait(qs, st)

    emit_weight_gemm(ctx, WT, uT, 32, 32, T, evac, "w0", extra_mms=extra,
                     pre_tile=pre_tile)

    emit_post_ln_residual(ctx, ln, ffn_h, x1, x2)
    ctx.finish_outputs()
    return ctx




def build_phaseD(ctx=None, out_name="x4"):
    if ctx is None:
        ctx = Ctx("phaseD")
    x3 = ctx.din("x3", (T, DIM))
    WT = ctx.din("F1_WT", (DIM, DIM))
    biasR = ctx.din("F1_biasR", (128, 32))
    WrT = ctx.din("F1_WrT", (DIM, 8))
    AcatT = ctx.din("F1_AcatT", (DIM, 128))
    Bcat = ctx.din("F1_Bcat", (128, DIM))
    E8 = ctx.din("E8", (8, 128), BF16)
    ident = ctx.din("ident", (128, 128), BF16)
    x4 = ctx.dout(out_name, (T, DIM))
    ffn_h = ctx.dram("ffn_h", (DIM, T), BF16)

    uT = ctx.sb("uT", [128, 32, T], BF16)
    ln = LnKit(ctx, "d")
    bias_s = ctx.sb("bias_s", [128, 32], F32)
    WrT_s = ctx.sb("WrT_s", [128, 32, 8], BF16)
    AcT_s = ctx.sb("AcT_s", [128, 32, 128], BF16)
    Bc_s = ctx.sb("Bc_s", [128, 32, 128], BF16)
    E8_s = ctx.sb("E8_s", [8, 128], BF16)
    id_s = ctx.sb("id_s", [128, 128], BF16)
    lg = ctx.sb("lg", [128, 8], F32)
    m1 = ctx.sb("m1", [128, 1], F32)
    m2 = ctx.sb("m2", [128, 1], F32)
    ge1 = ctx.sb("ge1", [128, 8], F32)
    lg2 = ctx.sb("lg2", [128, 8], F32)
    ex = ctx.sb("ex", [128, 8], F32)
    den = ctx.sb("den", [128, 1], F32)
    rec = ctx.sb("rec", [128, 1], F32)
    wtm = ctx.sb("wtm", [128, 8], BF16)
    w_fm = ctx.sb("w_fm", [8, T], BF16)
    wex = ctx.sb("wex", [128, T], BF16)
    hpr = ctx.sb("hpr", [128, T], BF16)
    stage = ctx.sb("stage", [128, T], BF16)
    psR = ctx.ps("psR", [128, 8], F32)
    psT = ctx.ps("psT", [8, 128], BF16)
    psX = ctx.ps("psX", [128, 512], F32)
    psY = ctx.ps("psY", [128, 512], F32)

    pe, v, a, qs, qg = ctx.pe, ctx.v, ctx.a, ctx.qs, ctx.qg

    qg.dma(bias_s[:, :], biasR[:, :])
    qg.dma(WrT_s[:, :, :], WrT[:, :].rearrange("(a k) m -> k a m", k=128))
    qg.dma(AcT_s[:, :, :], AcatT[:, :].rearrange("(a k) m -> k a m", k=128))
    qg.dma(Bc_s[:, :, :], Bcat[:, :].rearrange("k (a m) -> k a m", m=128))
    qg.dma(E8_s[:, :], E8[:, :])
    qg.dma(id_s[:, :], ident[:, :])

    for i in range(8):
        ln.tile_from_hbm(x3, i * 128, 128, uT=uT, ucol=i * 128)

    ctx.barrier(pe, [ctx.qs, ctx.v, ctx.qg])
    for i in range(8):
        for k in range(32):
            pe.mm(psR[:, :], uT[:, k, i * 128:(i + 1) * 128], WrT_s[:, k, :],
                  start=(k == 0), stop=(k == 31))
        pe.mark()
        v.wait(pe, pe.n)
        v.op("tensor_copy", lg[:, :], psR[:, :])
        v.op("reduce_max", m1[:, :], lg[:, :], axis=AX.X)
        v.op("tensor_scalar", ge1[:, :], lg[:, :], m1[:, 0:1], None, ALU.is_ge)
        v.op("scalar_tensor_tensor", lg2[:, :], ge1[:, :], -1e30, lg[:, :],
             ALU.mult, ALU.add)
        v.op("reduce_max", m2[:, :], lg2[:, :], axis=AX.X)
        v.op("tensor_scalar", m1[:, :], m1[:, :], -1.0, None, ALU.mult)
        a.wait(v, v.n)
        a.op("activation", ex[:, :], lg[:, :], AF.Exp, bias=m1[:, 0:1])
        v.wait(a, a.n)
        v.op("tensor_scalar", ge1[:, :], lg[:, :], m2[:, 0:1], None, ALU.is_ge)
        v.op("tensor_tensor", ex[:, :], ex[:, :], ge1[:, :], ALU.mult)
        v.op("reduce_sum", den[:, :], ex[:, :], axis=AX.X)
        v.op("reciprocal", rec[:, :], den[:, :])
        v.op("tensor_scalar", wtm[:, :], ex[:, :], rec[:, 0:1], None, ALU.mult)
        pe.raw(lambda eng, val=v.n: eng.wait_ge(v.sem, val))
        pe.op("transpose", psT[:, :], wtm[:, :], id_s[:, :])
        pe.mark()
        v.wait(pe, pe.n)
        v.op("tensor_copy", w_fm[:, i * 128:(i + 1) * 128], psT[0:8, :])
    pe.raw(lambda eng, val=v.n: eng.wait_ge(v.sem, val))
    for ci in range(2):
        ps = psX if ci == 0 else psY
        pe.mm(ps[:, :], E8_s[:, :], w_fm[0:8, ci * 512:(ci + 1) * 512],
              start=True, stop=True)
        pe.mark()
        v.wait(pe, pe.n)
        v.op("tensor_scalar", wex[:, ci * 512:(ci + 1) * 512], ps[:, :],
             2.0, None, ALU.mult)
    for ci in range(2):
        ps = psX if ci == 0 else psY
        pe.raw(lambda eng, val=v.n: eng.wait_ge(v.sem, val))
        for k in range(32):
            pe.mm(ps[:, :], AcT_s[:, k, :], uT[:, k, ci * 512:(ci + 1) * 512],
                  start=(k == 0), stop=(k == 31))
        pe.mark()
        v.wait(pe, pe.n)
        v.op("tensor_tensor", hpr[:, ci * 512:(ci + 1) * 512], ps[:, :],
             wex[:, ci * 512:(ci + 1) * 512], ALU.mult)
    pe.raw(lambda eng, val=v.n: eng.wait_ge(v.sem, val))

    def extra(d, ps, c0, cw):
        pe.mm(ps[:, 0:cw], Bc_s[:, d, :], hpr[:, c0:c0 + cw], start=False, stop=True)

    def evac(d, pss, nsplit):
        v.wait(pe, pe.n)
        for ci, (c0, cw) in enumerate(nsplit):
            v.op("tensor_scalar", stage[:, c0:c0 + cw], pss[ci][:, 0:cw],
                 bias_s[:, d:d + 1], None, ALU.add)
        qs.wait(v, v.n)
        st = qs.dma(ffn_h[d * 128:(d + 1) * 128, :], stage[:, :])
        v.wait(qs, st)

    emit_weight_gemm(ctx, WT, uT, 32, 32, T, evac, "w1", extra_mms=extra)

    emit_post_ln_residual(ctx, ln, ffn_h, x3, x4)
    ctx.finish_outputs()
    return ctx




def emit_roleswap_gemm(ctx, wT_h, uT, out_h, wname, rope=None):
    pe, v, qs, qg = ctx.pe, ctx.v, ctx.qs, ctx.qg
    if not hasattr(ctx, "_rs_bufs"):
        ctx._rs_bufs = dict(
            wb=[ctx.sb("rs_wb0", [128, 32, 512], BF16),
                ctx.sb("rs_wb1", [128, 32, 512], BF16)],
            pss=[ctx.ps("rs_psA", [128, 512], F32),
                 ctx.ps("rs_psB", [128, 512], F32)],
            stage=ctx.sb("rs_st", [128, 512], BF16),
            t1=ctx.sb("rs_t1", [128, 512], F32),
            t2=ctx.sb("rs_t2", [128, 512], F32))
    wb = ctx._rs_bufs["wb"]
    pss = ctx._rs_bufs["pss"]
    stage = ctx._rs_bufs["stage"]
    t1 = ctx._rs_bufs["t1"]
    t2 = ctx._rs_bufs["t2"]
    qg.wait(pe, pe.n)
    qg.wait(v, v.n)
    pe_done = {}
    v_done = {}
    for ci in range(8):
        b = wb[ci % 2]
        if ci >= 2:
            qg.wait(pe, pe_done[ci - 2])
        ld = qg.dma(b[:, :, :],
                    wT_h[:, ci * 512:(ci + 1) * 512].rearrange("(a k) m -> k a m", k=128))
        pe.raw(lambda eng, ld=ld: eng.wait_ge(qg.sem, ld))
        for ti in range(8):
            ps = pss[ti % 2]
            if (ci, ti) != (0, 0) and (ci, ti) != (0, 1):
                prev = v_done.get((ci, ti - 2)) or v_done.get((ci - 1, ti + 6))
                if prev:
                    pe.raw(lambda eng, val=prev: eng.wait_ge(v.sem, val))
            for k in range(32):
                pe.mm(ps[:, :], uT[:, k, ti * 128:(ti + 1) * 128], b[:, k, :],
                      start=(k == 0), stop=(k == 31))
            pe.mark()
            pe_done[ci] = pe.n
            v.wait(pe, pe.n)
            if rope is None:
                v.op("tensor_copy", stage[:, :], ps[:, :])
            else:
                cos_s, sin_s = rope
                pv = ps[:, :].rearrange("t (h v f) -> t h v f", h=4, v=2)
                cl = cos_s[:, ti, 0:64].unsqueeze(1).broadcast_to([128, 4, 64])
                ch = cos_s[:, ti, 64:128].unsqueeze(1).broadcast_to([128, 4, 64])
                sl = sin_s[:, ti, 0:64].unsqueeze(1).broadcast_to([128, 4, 64])
                sh = sin_s[:, ti, 64:128].unsqueeze(1).broadcast_to([128, 4, 64])
                t1v = t1[:, :].rearrange("t (h v f) -> t h v f", h=4, v=2)
                v.op("tensor_tensor", t1v[:, :, 0, :], pv[:, :, 0, :], cl, ALU.mult)
                v.op("tensor_tensor", t1v[:, :, 1, :], pv[:, :, 1, :], ch, ALU.mult)
                t2v = t2[:, :].rearrange("t (h v f) -> t h v f", h=4, v=2)
                v.op("tensor_tensor", t2v[:, :, 0, :], pv[:, :, 1, :], sl, ALU.mult)
                v.op("tensor_tensor", t2v[:, :, 1, :], pv[:, :, 0, :], sh, ALU.mult)
                v.op("tensor_tensor", stage[:, :], t1[:, :], t2[:, :], ALU.add)
            qs.wait(v, v.n)
            st = qs.dma(out_h[ti * 128:(ti + 1) * 128, ci * 512:(ci + 1) * 512],
                        stage[:, :])
            v.wait(qs, st)
            v_done[(ci, ti)] = v.n


def build_phaseC1(ctx=None):
    if ctx is None:
        ctx = Ctx("phaseC1")
    x2 = ctx.din("x2", (T, DIM))
    WqT = ctx.din("WqT", (DIM, DIM))
    WkT = ctx.din("WkT", (DIM, DIM))
    WvT = ctx.din("WvT", (DIM, DIM))
    cosR = ctx.din("rope_cos", (128, 8, HD))
    sinR = ctx.din("rope_sin", (128, 8, HD))
    q_h = ctx.dout("q_tm", (T, DIM), BF16)
    k_h = ctx.dout("k_tm", (T, DIM), BF16)
    v_h = ctx.dout("v_tm", (T, DIM), BF16)

    uT = ctx.sb("uT", [128, 32, T], BF16)
    ln = LnKit(ctx, "c")
    cos_s = ctx.sb("cos_s", [128, 8, HD], F32)
    sin_s = ctx.sb("sin_s", [128, 8, HD], F32)

    qg = ctx.qg
    qg.dma(cos_s[:, :, :], cosR[:, :, :])
    qg.dma(sin_s[:, :, :], sinR[:, :, :])

    for i in range(8):
        ln.tile_from_hbm(x2, i * 128, 128, uT=uT, ucol=i * 128)
    ctx.barrier(ctx.pe, [ctx.qs, ctx.v, ctx.qg])

    emit_roleswap_gemm(ctx, WqT, uT, q_h, "wq", rope=(cos_s, sin_s))
    emit_roleswap_gemm(ctx, WkT, uT, k_h, "wk", rope=(cos_s, sin_s))
    emit_roleswap_gemm(ctx, WvT, uT, v_h, "wv", rope=None)
    ctx.finish_outputs()
    return ctx


def build_phaseC2(ctx=None):
    if ctx is None:
        ctx = Ctx("phaseC2")
    x2 = ctx.din("x2", (T, DIM))
    q_h = ctx.din("q_tm", (T, DIM), BF16)
    k_h = ctx.din("k_tm", (T, DIM), BF16)
    v_h = ctx.din("v_tm", (T, DIM), BF16)
    kp_h = ctx.din("kprev_tm", (T, DIM), BF16)
    vp_h = ctx.din("vprev_tm", (T, DIM), BF16)
    wmask = ctx.din("wmask", (2, 12, 128, 512), BF16)
    WoT = ctx.din("WoT", (DIM, DIM))
    onescol = ctx.din("onescol", (128, 1), BF16)
    onesrow = ctx.din("onesrow", (1, 128), BF16)
    x3 = ctx.dout("x3", (T, DIM))
    attn_h = ctx.dram("attn_h", (DIM, T), BF16)
    wo_h = ctx.dram("wo_h", (DIM, T), BF16)

    ln = LnKit(ctx, "c2", slim=True)
    mask_s = ctx.sb("mask_s", [128, 2, 12, 512], BF16)
    ones_s = ctx.sb("ones_s", [128, 1], BF16)
    onesr_s = ctx.sb("onesr_s", [1, 128], BF16)
    qh_s = [ctx.sb("qh0", [128, T], BF16)]
    kw_s = [ctx.sb("kw0", [128, 2048], BF16)]
    vw_s = [ctx.sb("vw0", [128, 16, 128], BF16)]
    probs = ctx.sb("probs", [128, 12, 512], BF16)
    sum_f = ctx.sb("sum_f", [1, 512], F32)
    rec_f = ctx.sb("rec_f", [1, 512], F32)
    rec_b = ctx.sb("rec_b", [1, 512], BF16)
    recB = ctx.sb("recB", [128, 512], BF16)
    ast = ctx.sb("ast", [128, 512], BF16)
    psS = ctx.ps("psS", [128, 512], F32)
    psSum = ctx.ps("psSum", [1, 512], F32)
    psAV = ctx.ps("psAV", [128, 512], F32)
    psRB = ctx.ps("psRB", [128, 512], F32)

    pe, v, a, qs, qg = ctx.pe, ctx.v, ctx.a, ctx.qs, ctx.qg

    qg.dma(mask_s[:, :, :, :], wmask[:, :, :, :].rearrange("g b k q -> k g b q"))
    qg.dma(ones_s[:, :], onescol[:, :])
    qg.dma(onesr_s[:, :], onesrow[:, :])

    SC = 1.0 / (HD ** 0.5)
    ld_done = {}
    v_done_head = {}
    for h in range(32):
        qb, kb_, vb = qh_s[0], kw_s[0], vw_s[0]
        if h >= 1:
            qs.wait(v, v_done_head[h - 1])
            qs.wait(pe, pe.n)
        l1 = qs.dmat(qb[:, :], q_h[:, h * 128:(h + 1) * 128].opt())
        l2 = qs.dmat(kb_[:, 0:1024], kp_h[:, h * 128:(h + 1) * 128].opt())
        l3 = qs.dmat(kb_[:, 1024:2048], k_h[:, h * 128:(h + 1) * 128].opt())
        l4 = qs.dma(vb[:, 0:8, :], vp_h[:, h * 128:(h + 1) * 128].rearrange("(b r) d -> r b d", r=128))
        l5 = qs.dma(vb[:, 8:16, :], v_h[:, h * 128:(h + 1) * 128].rearrange("(b r) d -> r b d", r=128))
        pe.raw(lambda eng, val=l5: eng.wait_ge(qs.sem, val))
        for qg_i in range(2):
            for kb in range(12):
                koff = qg_i * 512 + kb * 128
                pe.raw(lambda eng, val=a.n: eng.wait_ge(a.sem, val))
                pe.mm(psS[:, :], kb_[:, koff:koff + 128], qb[:, qg_i * 512:(qg_i + 1) * 512],
                      start=True, stop=True)
                pe.mark()
                a.wait(pe, pe.n)
                if kb > 0 or qg_i > 0 or h > 0:
                    a.wait(v, v.n)
                a.op("activation", probs[:, kb, :], psS[:, :], AF.Exp, scale=SC)
                v.wait(a, a.n)
                v.op("tensor_tensor", probs[:, kb, :], probs[:, kb, :],
                     mask_s[:, qg_i, kb, :], ALU.mult)
            pv = v.n
            pe.raw(lambda eng, val=pv: eng.wait_ge(v.sem, val))
            for kb in range(12):
                pe.mm(psSum[:, :], ones_s[:, 0:1],
                      probs[:, kb, :], start=(kb == 0), stop=(kb == 11))
            for kb in range(12):
                koff = qg_i * 512 + kb * 128
                bidx = qg_i * 4 + kb
                pe.mm(psAV[:, :], vb[:, bidx, :],
                      probs[:, kb, :], start=(kb == 0), stop=(kb == 11))
            pe.mark()
            v.wait(pe, pe.n)
            v.op("tensor_copy", sum_f[:, :], psSum[0:1, :])
            v.op("reciprocal", rec_f[:, :], sum_f[:, :])
            v.op("tensor_copy", rec_b[:, :], rec_f[:, :])
            pe.raw(lambda eng, val=v.n: eng.wait_ge(v.sem, val))
            pe.mm(psRB[:, :], onesr_s[0:1, :], rec_b[0:1, :], start=True, stop=True)
            pe.mark()
            v.wait(pe, pe.n)
            v.op("tensor_copy", recB[:, :], psRB[:, :])
            v.op("tensor_tensor", ast[:, :], psAV[:, :], recB[:, :], ALU.mult)
            qs.wait(v, v.n)
            st = qs.dma(attn_h[h * 128:(h + 1) * 128, qg_i * 512:(qg_i + 1) * 512],
                        ast[:, :])
            v.wait(qs, st)
        v_done_head[h] = v.n

    attn_s = ctx.sb("attn_s", [128, 32, T], BF16)
    la = qs.dma(attn_s[:, :, :], attn_h[:, :].rearrange("(a k) t -> k a t", k=128))
    pe.raw(lambda eng, val=la: eng.wait_ge(qs.sem, val))
    stage2 = ctx.sb("stage2", [128, T], BF16)

    def evac(d, pss2, nsplit):
        v.wait(pe, pe.n)
        for ci, (c0, cw) in enumerate(nsplit):
            v.op("tensor_copy", stage2[:, c0:c0 + cw], pss2[ci][:, 0:cw])
        qs.wait(v, v.n)
        st = qs.dma(wo_h[d * 128:(d + 1) * 128, :], stage2[:, :])
        v.wait(qs, st)

    emit_weight_gemm(ctx, WoT, attn_s, 32, 32, T, evac, "wo")
    emit_post_ln_residual(ctx, ln, wo_h, x2, x3)
    ctx.finish_outputs()
    return ctx




def build_phaseA1(ctx=None):
    if ctx is None:
        ctx = Ctx("phaseA1")
    x_in = ctx.din("x_in", (HT, DIM))
    WT = ctx.din("W_inT", (DIM, NPROJ))
    convw = ctx.din("conv_wR", (128, 80, KCONV))
    convb = ctx.din("conv_bR", (128, 80))
    dtb = ctx.din("dt_bias", (128, 1))
    alog = ctx.din("A_log", (128, 1))
    normw = ctx.din("norm_wR", (128, 64))
    ident = ctx.din("ident", (128, 128), BF16)
    onesr = ctx.din("onesrow", (1, 128), BF16)

    xtm_h = ctx.dout("xtm", (T, INTER), BF16)
    gate_h = ctx.dout("gate", (INTER, T), BF16)
    bc_h = ctx.dout("bc", (2 * G * N, T), BF16)
    dt_h = ctx.dout("dt", (128, T))
    Rloc_h = ctx.dout("Rloc", (NCH, 128, INTER))
    Z_h = ctx.dout("Z", (128, INTER))
    cumA_h = ctx.dout("cumA", (128, 1))
    decflat_h = ctx.dram("decflat", (1, 1024))

    uT = ctx.sb("uT", [128, 32, HT], BF16)
    ln = LnKit(ctx, "a")
    wb = [ctx.sb("wb0", [128, 32, 128], BF16), ctx.sb("wb1", [128, 32, 128], BF16)]
    ident_s = ctx.sb("ident_s", [128, 128], BF16)
    onesr_s = ctx.sb("onesr_s", [1, 128], BF16)
    convw_s = ctx.sb("convw_s", [128, 80, KCONV], F32)
    convb_s = ctx.sb("convb_s", [128, 80], F32)
    dtb_s = ctx.sb("dtb_s", [128, 1], F32)
    alog_s = ctx.sb("alog_s", [128, 1], F32)
    normw_s = ctx.sb("normw_s", [128, 64], F32)
    A_col = ctx.sb("A_col", [128, 1], F32)
    hbc_s = ctx.sb("hbc_s", [128, HT], BF16)
    diag_s = ctx.sb("diag_s", [128, 4, 128], BF16)
    act_s = ctx.sb("act_s", [128, T], BF16)
    dt_s = ctx.sb("dt_s", [128, T], F32)
    xb_s = ctx.sb("xb_s", [128, T], F32)
    a_s = ctx.sb("a_s", [128, T], F32)
    Acum_s = ctx.sb("Acum_s", [128, T], F32)
    zeros_s = ctx.sb("zeros_s", [128, CS], F32)
    dec_s = ctx.sb("dec_s", [128, T], F32)
    sp1_s = dec_s
    ddt_s = ctx.sb("ddt_s", [128, T], F32)
    ddtT_s = ctx.sb("ddtT_s", [128, NCH, 128], F32)
    dec8_s = ctx.sb("dec8_s", [128, NCH], F32)
    decB_s = ctx.sb("decB_s", [128, 1024], F32)
    Btm_s = ctx.sb("Btm_s", [128, NCH, 128], BF16)
    xtmg_s = ctx.sb("xtmg_s", [128, NCH, 8, 128], BF16)
    xsl_s = ctx.sb("xsl_s", [128, NCH, 8, 128], BF16)
    Rg_s = ctx.sb("Rg_s", [128, 1024], F32)
    Rt_s = ctx.sb("Rt_s", [128, 1024], F32)
    cumA_s = ctx.sb("cumA_s", [128, 1], F32)
    stage_s = None

    psI = [ctx.ps("psIa", [128, 512], F32), ctx.ps("psIb", [128, 512], F32)]
    psIh = ctx.ps("psIh", [128, 32], F32)
    psC = [ctx.ps("psCa", [128, 512], F32), ctx.ps("psCb", [128, 512], F32)]
    psT = ctx.ps("psT", [128, 128], BF16)
    psS = ctx.ps("psS", [128, 1024], F32)

    pe, v, a, qs, qg = ctx.pe, ctx.v, ctx.a, ctx.qs, ctx.qg

    qg.dma(ident_s[:, :], ident[:, :])
    qg.dma(onesr_s[:, :], onesr[:, :])
    qg.dma(convw_s[:, :, :], convw[:, :, :])
    qg.dma(convb_s[:, :], convb[:, :])
    qg.dma(dtb_s[:, :], dtb[:, :])
    qg.dma(alog_s[:, :], alog[:, :])
    lc = qg.dma(normw_s[:, :], normw[:, :])

    a.wait(qg, lc)
    a.op("activation", A_col[:, :], alog_s[:, :], AF.Exp)
    v.wait(a, a.n)
    v.op("tensor_scalar", A_col[:, :], A_col[:, :], -1.0, None, ALU.mult)
    v.op("memset", zeros_s[:, :], 0.0)

    ln.tile_from_hbm(x_in, 0, HALO, uT=uT, ucol=0)
    for i in range(8):
        ln.tile_from_hbm(x_in, HALO + i * 128, 128, uT=uT, ucol=HALO + i * 128)
    ctx.barrier(pe, [ctx.qs, ctx.v, ctx.qg])

    NS = [(0, 32, psIh), (32, 512, psI[0]), (544, 512, psI[1])]
    _ip_state = {"marks": {}, "seq": []}

    def inproj_tile(d, halo_too=True):
        seq = _ip_state["seq"]
        if len(seq) >= 2:
            qg.wait(pe, _ip_state["marks"][seq[-2]])
        b = wb[len(seq) % 2]
        ld = qg.dma(b[:, :, :],
                    WT[:, d * 128:(d + 1) * 128].rearrange("(a k) m -> k a m", k=128))
        pe.raw(lambda eng, val=ld: eng.wait_ge(qg.sem, val))
        pe.raw(lambda eng, val=v.n: eng.wait_ge(v.sem, val))
        pe.raw(lambda eng, val=a.n: eng.wait_ge(a.sem, val))
        for (c0, cw, ps) in NS:
            if c0 == 0 and not halo_too:
                continue
            for k in range(32):
                pe.mm(ps[:, 0:cw], b[:, k, :], uT[:, k, c0:c0 + cw],
                      start=(k == 0), stop=(k == 31))
        pe.mark()
        seq.append(d)
        _ip_state["marks"][d] = pe.n

    def conv_and_silu(d, cidx):
        for k in range(KCONV):
            v.op("tensor_scalar", diag_s[:, k, :], ident_s[:, :],
                 convw_s[:, cidx:cidx + 1, k:k + 1].rearrange("p a b -> p (a b)"),
                 None, ALU.mult)
        pe.raw(lambda eng, val=v.n: eng.wait_ge(v.sem, val))
        pe.raw(lambda eng, val=a.n: eng.wait_ge(a.sem, val))
        for ci in range(2):
            for k in range(KCONV):
                pe.mm(psC[ci][:, :], diag_s[:, k, :],
                      hbc_s[:, 29 + k + ci * 512: 29 + k + ci * 512 + 512],
                      start=(k == 0), stop=(k == KCONV - 1))
        pe.mark()
        a.wait(pe, pe.n)
        for ci in range(2):
            a.op("activation", act_s[:, ci * 512:(ci + 1) * 512], psC[ci][:, :],
                 AF.Silu, bias=convb_s[:, cidx:cidx + 1])

    inproj_tile(144, halo_too=False)
    v.wait(pe, pe.n)
    for ci in range(2):
        v.op("tensor_scalar", xb_s[:, ci * 512:(ci + 1) * 512], psI[ci][:, :],
             dtb_s[:, 0:1], None, ALU.add)
    a.wait(v, v.n)
    a.op("activation", sp1_s[:, :], xb_s[:, :], AF.Abs)
    a.op("activation", sp1_s[:, :], sp1_s[:, :], AF.Exp, scale=-1.0)
    a.op("activation", sp1_s[:, :], sp1_s[:, :], AF.Ln, bias=1.0)
    v.wait(a, a.n)
    v.op("tensor_scalar", dt_s[:, :], xb_s[:, :], 0.0, None, ALU.max)
    v.op("tensor_tensor", dt_s[:, :], dt_s[:, :], sp1_s[:, :], ALU.add)
    qs.wait(v, v.n)
    qs.dma(dt_h[:, :], dt_s[:, :])
    v.op("tensor_scalar", a_s[:, :], dt_s[:, :], A_col[:, 0:1], None, ALU.mult)
    for c in range(NCH):
        v.op("tensor_tensor_scan", Acum_s[:, c * CS:(c + 1) * CS],
             a_s[:, c * CS:(c + 1) * CS], zeros_s[:, :], 0.0, ALU.add, ALU.add)
    v.op("reduce_sum", cumA_s[:, :],
         Acum_s[:, :].rearrange("p (c l) -> p c l", c=NCH)[:, :, CS - 1:CS].squeeze(2),
         axis=AX.X)
    qs.wait(v, v.n)
    qs.dma(cumA_h[:, :], cumA_s[:, :])
    for c in range(NCH):
        v.op("tensor_scalar", dec_s[:, c * CS:(c + 1) * CS],
             Acum_s[:, c * CS:(c + 1) * CS],
             Acum_s[:, c * CS + CS - 1:c * CS + CS], None, ALU.subtract)
    a.wait(v, v.n)
    a.op("activation", dec_s[:, :], dec_s[:, :], AF.Exp, scale=-1.0)
    v.wait(a, a.n)
    v.op("tensor_tensor", ddt_s[:, :], dec_s[:, :], dt_s[:, :], ALU.mult)
    v.op("tensor_copy", act_s[:, :], ddt_s[:, :])
    pe.raw(lambda eng, val=v.n: eng.wait_ge(v.sem, val))
    for c in range(NCH):
        pe.op("transpose", psT[:, :], act_s[:, c * CS:(c + 1) * CS], ident_s[:, :])
        pe.mark()
        v.wait(pe, pe.n)
        v.op("tensor_copy", ddtT_s[:, c, :], psT[:, :])
        pe.raw(lambda eng, val=v.n: eng.wait_ge(v.sem, val))
    a.op("activation", dec8_s[:, :],
         Acum_s[:, :].rearrange("p (c l) -> p c l", c=NCH)[:, :, CS - 1:CS].squeeze(2),
         AF.Exp)
    qs.wait(a, a.n)
    f1 = qs.dma(decflat_h[0, :].rearrange("(h c) -> h c", h=128), dec8_s[:, :])
    qs.wait(qs, f1)
    f2 = qs.dma(decB_s[:, :], decflat_h[0:1, :].broadcast_to([128, 1024]))
    v.wait(qs, f2)

    for d in range(128, 144):
        inproj_tile(d)
        v.wait(pe, pe.n)
        v.op("tensor_copy", hbc_s[:, 0:32], psIh[:, :])
        v.op("tensor_copy", hbc_s[:, 32:544], psI[0][:, :])
        v.op("tensor_copy", hbc_s[:, 544:1056], psI[1][:, :])
        cidx = d - 64
        conv_and_silu(d, cidx)
        qs.wait(a, a.n)
        st = qs.dma(bc_h[(d - 128) * 128:(d - 127) * 128, :], act_s[:, 0:T])
        a.wait(qs, st)
        if d < 136:
            pass

    for g in range(8):
        lb = qs.dma(hbc_s[:, 0:T], bc_h[g * 128:(g + 1) * 128, :])
        qs.wait(qs, lb)
        tb = qs.dmat(Btm_s[:, :, :], hbc_s[:, 0:T].rearrange("n (c s) -> n c s", c=NCH))
        pe.raw(lambda eng, val=tb: eng.wait_ge(qs.sem, val))
        for j in range(8):
            d = 64 + g * 8 + j
            inproj_tile(d)
            v.wait(pe, pe.n)
            v.op("tensor_copy", hbc_s[:, 0:32], psIh[:, :])
            v.op("tensor_copy", hbc_s[:, 32:544], psI[0][:, :])
            v.op("tensor_copy", hbc_s[:, 544:1056], psI[1][:, :])
            conv_and_silu(d, d - 64)
            qs.wait(a, a.n)
            tx = qs.dmat(xtmg_s[:, :, j, :], act_s[:, :].rearrange("r (c t) -> r c t", c=NCH))
            qs.wait(qs, tx)
            st = qs.dma(xtm_h[:, (d - 64) * 128:(d - 63) * 128]
                        .rearrange("(c tl) ch -> tl c ch", c=NCH), xtmg_s[:, :, j, :])
            v.wait(qs, st)
            for c in range(NCH):
                hv = ddtT_s[:, c, g * 16 + j * 2: g * 16 + j * 2 + 2]
                v.op("tensor_tensor", xsl_s[:, c, j, :].rearrange("p (h q) -> p h q", h=2),
                     xtmg_s[:, c, j, :].rearrange("p (h q) -> p h q", h=2),
                     hv.unsqueeze(2).broadcast_to([128, 2, 64]), ALU.mult)
        v.op("memset", Rg_s[:, :], 0.0)
        for c in range(NCH):
            pe.raw(lambda eng, val=v.n: eng.wait_ge(v.sem, val))
            for j in range(8):
                pe.mm(psS[:, j * 128:(j + 1) * 128], Btm_s[:, c, :],
                      xsl_s[:, c, j, :], start=True, stop=True)
            pe.mark()
            qs.wait(v, v.n)
            sr = qs.dma(Rloc_h[c, :, g * 1024:(g + 1) * 1024], Rg_s[:, :])
            v.wait(qs, sr)
            v.wait(pe, pe.n)
            dv = decB_s[:, :].rearrange("p (h c) -> p h c", h=128)[:, g * 16:(g + 1) * 16, c:c + 1]
            v.op("tensor_tensor", Rt_s[:, :].rearrange("p (h q) -> p h q", h=16),
                 Rg_s[:, :].rearrange("p (h q) -> p h q", h=16),
                 dv.broadcast_to([128, 16, 64]), ALU.mult)
            v.op("tensor_tensor", Rg_s[:, :], Rt_s[:, :], psS[:, :], ALU.add)
        qs.wait(v, v.n)
        sz = qs.dma(Z_h[:, g * 1024:(g + 1) * 1024], Rg_s[:, :])
        v.wait(qs, sz)

    for d in range(64):
        inproj_tile(d, halo_too=False)
        a.wait(pe, pe.n)
        for ci in range(2):
            a.op("activation", act_s[:, ci * 512:(ci + 1) * 512], psI[ci][:, :], AF.Silu)
        v.wait(a, a.n)
        v.op("tensor_scalar", act_s[:, 0:T], act_s[:, 0:T],
             normw_s[:, d:d + 1], None, ALU.mult)
        qs.wait(v, v.n)
        st = qs.dma(gate_h[d * 128:(d + 1) * 128, :], act_s[:, 0:T])
        v.wait(qs, st)
        a.wait(qs, st)

    ctx.finish_outputs()
    return ctx




def build_phaseA2a(ctx=None):
    if ctx is None:
        ctx = Ctx("phaseA2a")
    xtm_h = ctx.din("xtm", (T, INTER), BF16)
    gate_h = ctx.din("gate", (INTER, T), BF16)
    bc_h = ctx.din("bc", (2 * G * N, T), BF16)
    dt_h = ctx.din("dt", (128, T))
    alog = ctx.din("A_log", (128, 1))
    Rloc_h = ctx.din("Rloc", (NCH, 128, INTER))
    Zall_h = ctx.din("Z_all", (8, 128, INTER))
    cumAall = ctx.din("cumA_all", (128, 8))
    maskR = ctx.din("carrymaskR", (128, 8))
    D_B = ctx.din("D_B", (128, 128))
    tri = ctx.din("tri", (128, 128), BF16)
    ident = ctx.din("ident", (128, 128), BF16)
    identf = ctx.din("ident_f", (128, 128))
    z_h = ctx.dout("z", (INTER, T), BF16)
    acumflat_h = ctx.dram("acumflat", (NCH, 128 * 128))
    wjflat_h = ctx.dram("wjflat", (1, 1024))
    eprflat_h = ctx.dram("eprflat", (1, 1024))

    dt_s = ctx.sb("dt_s", [128, T], F32)
    a_s = ctx.sb("a_s", [128, T], F32)
    Acum_s = ctx.sb("Acum_s", [128, T], F32)
    zeros_s = ctx.sb("zeros_s", [128, CS], F32)
    alog_s = ctx.sb("alog_s", [128, 1], F32)
    A_col = ctx.sb("A_col", [128, 1], F32)
    AcumT_s = ctx.sb("AcumT_s", [128, NCH, 128], F32)
    exppT_s = ctx.sb("exppT_s", [128, NCH, 128], F32)
    dtT_s = ctx.sb("dtT_s", [128, NCH, 128], F32)
    dtb16 = ctx.sb("dtb16", [128, T], BF16)
    B_s = ctx.sb("B_s", [128, G, T], BF16)
    C_s = ctx.sb("C_s", [128, G, T], BF16)
    tri_s = ctx.sb("tri_s", [128, 128], BF16)
    ident_s = ctx.sb("ident_s", [128, 128], BF16)
    identf_s = ctx.sb("identf_s", [128, 128], F32)
    DB_s = ctx.sb("DB_s", [128, 128], F32)
    maskR_s = ctx.sb("maskR_s", [128, 8], F32)
    cumA_s = ctx.sb("cumA_s", [128, 8], F32)
    Tall_s = ctx.sb("Tall_s", [128, 8], F32)
    Tsel_s = ctx.sb("Tsel_s", [128, 1], F32)
    wj_s = ctx.sb("wj_s", [128, 8], F32)
    zer8 = ctx.sb("zer8", [128, 8], F32)
    wjB_s = ctx.sb("wjB_s", [128, 1024], F32)
    EprB_s = ctx.sb("EprB_s", [128, 1024], F32)
    Epr_s = ctx.sb("Epr_s", [128, 8], F32)
    carry_all = ctx.sb("carry_all", [128, G, 1024], BF16)
    carry_f = ctx.sb("carry_f", [128, 1024], F32)
    zj_s = ctx.sb("zj_s", [128, 1024], F32)
    AROW_s = ctx.sb("AROW_s", [128, 2048], F32)
    Dsl_s = ctx.sb("Dsl_s", [128, 2048], F32)
    Lsl_s = ctx.sb("Lsl_s", [128, 2048], BF16)
    gm_s = ctx.sb("gm_s", [128, 128], BF16)
    M_s = ctx.sb("M_s", [128, 2048], BF16)
    xtm_c = ctx.sb("xtm_c", [128, 1024], BF16)
    xdt_s = ctx.sb("xdt_s", [128, 1024], BF16)
    Rloc_b = ctx.sb("Rloc_b", [128, 1024], F32)
    Rf_s = ctx.sb("Rf_s", [128, 1024], F32)
    rfb_s = ctx.sb("rfb_s", [128, 1024], BF16)
    yt1_s = ctx.sb("yt1_s", [128, 1024], F32)
    y1_s = ctx.sb("y1_s", [128, 1024], F32)
    ycs = ctx.sb("ycs", [128, G, 1024], BF16)
    sqsc = None
    ssqc = ctx.sb("ssqc", [128, 8], F32)
    ssq = ctx.sb("ssq", [128, 1], F32)
    rstd = ctx.sb("rstd", [128, 1], F32)
    ytn_g = ctx.sb("ytn_g", [128, 1024], BF16)
    yfm_s = ctx.sb("yfm_s", [128, 64, 128], BF16)
    ggc_s = ctx.sb("ggc_s", [128, 64, 128], BF16)
    zfm_s = ctx.sb("zfm_s", [128, 64, 128], BF16)

    psT = ctx.ps("psT", [128, 128], F32)
    psG = ctx.ps("psG", [128, 128], F32)
    psY = ctx.ps("psY", [128, 1024], F32)
    psO = ctx.ps("psO", [128, 1024], F32)

    pe, v, a, qs, qg = ctx.pe, ctx.v, ctx.a, ctx.qs, ctx.qg

    qg.dma(tri_s[:, :], tri[:, :])
    qg.dma(ident_s[:, :], ident[:, :])
    qg.dma(identf_s[:, :], identf[:, :])
    qg.dma(DB_s[:, :], D_B[:, :])
    qg.dma(maskR_s[:, :], maskR[:, :])
    qg.dma(cumA_s[:, :], cumAall[:, :])
    qg.dma(alog_s[:, :], alog[:, :])
    qg.dma(B_s[:, :, :], bc_h[0:G * N, :].rearrange("(g n) t -> n g t", n=128))
    lBC = qg.dma(C_s[:, :, :], bc_h[G * N:2 * G * N, :].rearrange("(g n) t -> n g t", n=128))
    ldt = qs.dma(dt_s[:, :], dt_h[:, :])

    a.wait(qg, lBC)
    a.op("activation", A_col[:, :], alog_s[:, :], AF.Exp)
    v.wait(qs, ldt)
    v.wait(a, a.n)
    v.op("memset", zeros_s[:, :], 0.0)
    v.op("memset", zer8[:, :], 0.0)
    v.op("tensor_scalar", A_col[:, :], A_col[:, :], -1.0, None, ALU.mult)
    v.op("tensor_scalar", a_s[:, :], dt_s[:, :], A_col[:, 0:1], None, ALU.mult)
    for c in range(NCH):
        v.op("tensor_tensor_scan", Acum_s[:, c * CS:(c + 1) * CS],
             a_s[:, c * CS:(c + 1) * CS], zeros_s[:, :], 0.0, ALU.add, ALU.add)
    v.op("tensor_copy", dtb16[:, :], dt_s[:, :])
    pe.raw(lambda eng, val=v.n: eng.wait_ge(v.sem, val))
    for c in range(NCH):
        pe.op("transpose", psT[:, :], Acum_s[:, c * CS:(c + 1) * CS], identf_s[:, :])
        pe.mark()
        v.wait(pe, pe.n)
        v.op("tensor_copy", AcumT_s[:, c, :], psT[:, :])
        pe.raw(lambda eng, val=v.n: eng.wait_ge(v.sem, val))
        pe.op("transpose", psT[:, :], dt_s[:, c * CS:(c + 1) * CS], identf_s[:, :])
        pe.mark()
        v.wait(pe, pe.n)
        v.op("tensor_copy", dtT_s[:, c, :], psT[:, :])
        pe.raw(lambda eng, val=v.n: eng.wait_ge(v.sem, val))
    a.wait(v, v.n)
    for c in range(NCH):
        a.op("activation", exppT_s[:, c, :], AcumT_s[:, c, :], AF.Exp)
    qs.wait(v, v.n)
    fl = 0
    for c in range(NCH):
        fl = qs.dma(acumflat_h[c, :].rearrange("(h l) -> h l", h=128),
                    Acum_s[:, c * CS:(c + 1) * CS])
    qg.wait(qs, fl)

    v.op("tensor_tensor_scan", Tall_s[:, :], cumA_s[:, :], zer8[:, :], 0.0,
         ALU.add, ALU.add)
    v.op("tensor_tensor", wj_s[:, :], cumA_s[:, :], maskR_s[:, :], ALU.mult)
    v.op("reduce_sum", Tsel_s[:, :], wj_s[:, :], axis=AX.X)
    v.op("tensor_scalar", wj_s[:, :], Tall_s[:, :], Tsel_s[:, 0:1], None, ALU.subtract)
    v.op("tensor_scalar", wj_s[:, :], wj_s[:, :], 0.0, None, ALU.max)
    a.wait(v, v.n)
    a.op("activation", wj_s[:, :], wj_s[:, :], AF.Exp, scale=-1.0)
    v.wait(a, a.n)
    v.op("tensor_tensor", wj_s[:, :], wj_s[:, :], maskR_s[:, :], ALU.mult)
    qs.wait(v, v.n)
    f1 = qs.dma(wjflat_h[0, :].rearrange("(h j) -> h j", h=128), wj_s[:, :])
    qs.wait(qs, f1)
    f2 = qs.dma(wjB_s[:, :], wjflat_h[0:1, :].broadcast_to([128, 1024]))
    totv = Acum_s[:, :].rearrange("p (c l) -> p c l", c=NCH)[:, :, CS - 1:CS].squeeze(2)
    v.op("tensor_tensor_scan", Epr_s[:, :], totv, zer8[:, :], 0.0, ALU.add, ALU.add)
    a.wait(v, v.n)
    a.op("activation", Epr_s[:, 1:8], Epr_s[:, 0:7], AF.Exp)
    v.wait(a, a.n)
    v.op("memset", Epr_s[:, 0:1], 1.0)
    qs.wait(v, v.n)
    f3 = qs.dma(eprflat_h[0, :].rearrange("(h c) -> h c", h=128), Epr_s[:, :])
    qs.wait(qs, f3)
    f4 = qs.dma(EprB_s[:, :], eprflat_h[0:1, :].broadcast_to([128, 1024]))
    v.wait(qs, f4)

    for g in range(G):
        v.op("memset", carry_f[:, :], 0.0)
        for j in range(8):
            lz = qs.dma(zj_s[:, :], Zall_h[j, :, g * 1024:(g + 1) * 1024])
            v.wait(qs, lz)
            wv_ = wjB_s[:, :].rearrange("p (h j) -> p h j", h=128)[:, g * 16:(g + 1) * 16, j:j + 1]
            v.op("tensor_tensor", zj_s[:, :].rearrange("p (h q) -> p h q", h=16),
                 zj_s[:, :].rearrange("p (h q) -> p h q", h=16),
                 wv_.broadcast_to([128, 16, 64]), ALU.mult)
            v.op("tensor_tensor", carry_f[:, :], carry_f[:, :], zj_s[:, :], ALU.add)
            qs.wait(v, v.n)
        v.op("tensor_copy", carry_all[:, g, :], carry_f[:, :])

    for c in range(NCH):
        for g in range(G):
            qs.wait(v, v.n)
            qg.wait(v, v.n)
            lx = qs.dma(xtm_c[:, :], xtm_h[c * 128:(c + 1) * 128, g * 1024:(g + 1) * 1024])
            lr = qs.dma(Rloc_b[:, :], Rloc_h[c, :, g * 1024:(g + 1) * 1024])
            la = qg.dma(AROW_s[:, :],
                        acumflat_h[c:c + 1, g * 2048:(g + 1) * 2048].broadcast_to([128, 2048]))
            v.wait(qg, la)
            acv = AcumT_s[:, c, g * 16:(g + 1) * 16].unsqueeze(2).broadcast_to([128, 16, 128])
            v.op("tensor_tensor", Dsl_s[:, :].rearrange("p (h l) -> p h l", h=16),
                 AROW_s[:, :].rearrange("p (h l) -> p h l", h=16), acv, ALU.subtract)
            v.op("tensor_scalar", Dsl_s[:, :], Dsl_s[:, :], 0.0, None, ALU.min)
            a.wait(v, v.n)
            a.op("activation", Lsl_s[:, :], Dsl_s[:, :], AF.Exp)
            pe.raw(lambda eng, val=v.n: eng.wait_ge(v.sem, val))
            pe.mm(psG[:, :], B_s[:, g, c * CS:(c + 1) * CS],
                  C_s[:, g, c * CS:(c + 1) * CS], start=True, stop=True)
            pe.mark()
            v.wait(pe, pe.n)
            v.op("tensor_tensor", gm_s[:, :], psG[:, :], tri_s[:, :], ALU.mult)
            v.wait(a, a.n)
            gmv = gm_s[:, :].unsqueeze(1).broadcast_to([128, 16, 128])
            v.op("tensor_tensor", M_s[:, :].rearrange("p (h l) -> p h l", h=16),
                 gmv, Lsl_s[:, :].rearrange("p (h l) -> p h l", h=16), ALU.mult)
            v.wait(qs, lx)
            dtv = dtT_s[:, c, g * 16:(g + 1) * 16].unsqueeze(2).broadcast_to([128, 16, 64])
            v.op("tensor_tensor", xdt_s[:, :].rearrange("p (h q) -> p h q", h=16),
                 xtm_c[:, :].rearrange("p (h q) -> p h q", h=16), dtv, ALU.mult)
            v.wait(qs, lr)
            epv = EprB_s[:, :].rearrange("p (h cc) -> p h cc", h=128)[:, g * 16:(g + 1) * 16, c:c + 1]
            v.op("tensor_tensor", Rf_s[:, :].rearrange("p (h q) -> p h q", h=16),
                 carry_all[:, g, :].rearrange("p (h q) -> p h q", h=16),
                 epv.broadcast_to([128, 16, 64]), ALU.mult)
            v.op("tensor_tensor", Rf_s[:, :], Rf_s[:, :], Rloc_b[:, :], ALU.add)
            v.op("tensor_copy", rfb_s[:, :], Rf_s[:, :])
            pe.raw(lambda eng, val=v.n: eng.wait_ge(v.sem, val))
            for hl in range(16):
                pe.mm(psY[:, hl * 64:(hl + 1) * 64], M_s[:, hl * 128:(hl + 1) * 128],
                      xdt_s[:, hl * 64:(hl + 1) * 64], start=True, stop=True)
            for hl in range(16):
                pe.mm(psO[:, hl * 64:(hl + 1) * 64], C_s[:, g, c * CS:(c + 1) * CS],
                      rfb_s[:, hl * 64:(hl + 1) * 64], start=True, stop=True)
            pe.mark()
            v.wait(pe, pe.n)
            epx = exppT_s[:, c, g * 16:(g + 1) * 16].unsqueeze(2).broadcast_to([128, 16, 64])
            v.op("tensor_tensor", yt1_s[:, :].rearrange("p (h q) -> p h q", h=16),
                 psO[:, :].rearrange("p (h q) -> p h q", h=16), epx, ALU.mult)
            v.op("tensor_tensor", y1_s[:, :], yt1_s[:, :], psY[:, :], ALU.add)
            for hl in range(16):
                hglob = g * 16 + hl
                v.op("scalar_tensor_tensor", ycs[:, g, hl * 64:(hl + 1) * 64],
                     xtm_c[:, hl * 64:(hl + 1) * 64], DB_s[:, hglob:hglob + 1],
                     y1_s[:, hl * 64:(hl + 1) * 64], ALU.mult, ALU.add)
            pe.raw(lambda eng, val=v.n: eng.wait_ge(v.sem, val))
        a.wait(v, v.n)
        for g in range(G):
            a.op("activation", yt1_s[:, :], ycs[:, g, :], AF.Square,
                 accum_out=ssqc[:, g:g + 1])
        v.wait(a, a.n)
        v.op("reduce_sum", ssq[:, :], ssqc[:, :], axis=AX.X)
        v.op("tensor_scalar", ssq[:, :], ssq[:, :], 1.0 / INTER, EPS, ALU.mult, ALU.add)
        a.wait(v, v.n)
        a.op("activation", rstd[:, :], ssq[:, :], AF.Sqrt)
        v.wait(a, a.n)
        v.op("reciprocal", rstd[:, :], rstd[:, :])
        qs.wait(v, v.n)
        lg_ = qs.dma(ggc_s[:, :, :],
                     gate_h[:, c * 128:(c + 1) * 128].rearrange("(a r) t -> r a t", r=128))
        tx = 0
        for g in range(G):
            v.op("tensor_scalar", ytn_g[:, :], ycs[:, g, :], rstd[:, 0:1], None, ALU.mult)
            qs.wait(v, v.n)
            tx = qs.dmat(yfm_s[:, g * 8:(g + 1) * 8, :],
                         ytn_g[:, :].rearrange("t (a r) -> t a r", a=8))
            v.wait(qs, tx)
        v.wait(qs, lg_)
        v.op("tensor_tensor", zfm_s[:, :, :], yfm_s[:, :, :], ggc_s[:, :, :], ALU.mult)
        qs.wait(v, v.n)
        sz = qs.dma(z_h[:, c * 128:(c + 1) * 128].rearrange("(a r) t -> r a t", r=128),
                    zfm_s[:, :, :])
        v.wait(qs, sz)

    ctx.finish_outputs()
    return ctx


def build_phaseA2b(ctx=None):
    if ctx is None:
        ctx = Ctx("phaseA2b")
    x_in = ctx.din("x_in", (HT, DIM))
    z_h = ctx.din("z", (INTER, T), BF16)
    WT = ctx.din("W_outT", (INTER, DIM))
    x1 = ctx.dout("x1", (T, DIM))
    wo_h = ctx.dram("wo_h", (DIM, T), BF16)

    zT_s = ctx.sb("zT_s", [128, 64, 512], BF16)
    wb = [ctx.sb("wb0", [128, 64, 128], BF16), ctx.sb("wb1", [128, 64, 128], BF16)]
    stage = ctx.sb("stage", [128, 512], BF16)
    ln = LnKit(ctx, "a2")
    psA = ctx.ps("psA", [128, 512], F32)
    psB = ctx.ps("psB", [128, 512], F32)

    pe, v, a, qs, qg = ctx.pe, ctx.v, ctx.a, ctx.qs, ctx.qg

    marks = {}
    vdone = {}
    for th in range(2):
        lz = qs.dma(zT_s[:, :, :],
                    z_h[:, th * 512:(th + 1) * 512].rearrange("(a k) t -> k a t", k=128))
        qs.wait(pe, pe.n)
        pe.raw(lambda eng, val=lz: eng.wait_ge(qs.sem, val))
        for d in range(32):
            key = (th, d)
            if d >= 2:
                qg.wait(pe, marks[(th, d - 2)])
            ld = qg.dma(wb[d % 2][:, :, :],
                        WT[:, d * 128:(d + 1) * 128].rearrange("(a k) m -> k a m", k=128))
            pe.raw(lambda eng, val=ld: eng.wait_ge(qg.sem, val))
            if d >= 2:
                pe.raw(lambda eng, val=vdone[(th, d - 2)]: eng.wait_ge(v.sem, val))
            ps = psA if d % 2 == 0 else psB
            for k in range(64):
                pe.mm(ps[:, :], wb[d % 2][:, k, :], zT_s[:, k, :],
                      start=(k == 0), stop=(k == 63))
            pe.mark()
            marks[key] = pe.n
            v.wait(pe, pe.n)
            v.op("tensor_copy", stage[:, :], ps[:, :])
            qs.wait(v, v.n)
            st = qs.dma(wo_h[d * 128:(d + 1) * 128, th * 512:(th + 1) * 512], stage[:, :])
            v.wait(qs, st)
            vdone[key] = v.n

    emit_post_ln_residual(ctx, ln, wo_h, x_in[HALO:HALO + T, :], x1)
    ctx.finish_outputs()
    return ctx




def build_mid():
    ctx = Ctx("mid", final_outputs={"x2", "q_tm", "k_tm", "v_tm"})
    m = ctx.phase_mark()
    build_phaseA2a(ctx)
    ctx.phase_free(m)
    m = ctx.phase_mark()
    build_phaseA2b(ctx)
    ctx.phase_free(m)
    m = ctx.phase_mark()
    build_phaseB(ctx)
    ctx.phase_free(m)
    build_phaseC1(ctx)
    return ctx


def build_tail():
    ctx = Ctx("tail", final_outputs={"x4"})
    m = ctx.phase_mark()
    build_phaseC2(ctx)
    ctx.phase_free(m)
    build_phaseD(ctx)
    return ctx



_BUILT = {}


def _get(name):
    if name not in _BUILT:
        fn = globals().get("build_phase" + name) or globals()["build_" + name]
        ctx = fn()
        nc = ctx.build()
        _BUILT[name] = (ctx, nc)
    return _BUILT[name]


def kernel(**inputs):
    com = prep_common(inputs)
    cores = [prep_core(c, inputs) for c in range(NCORE)]
    live = [dict() for _ in range(NCORE)]

    def resolve(ctx, c):
        m = {}
        for n in ctx.input_names:
            if n in live[c]:
                m[n] = live[c][n]
            elif n == "kprev_tm":
                m[n] = live[c - 1]["k_tm"] if c > 0 else np.zeros((T, DIM), BF)
            elif n == "vprev_tm":
                m[n] = live[c - 1]["v_tm"] if c > 0 else np.zeros((T, DIM), BF)
            elif n == "Z_all":
                m[n] = np.ascontiguousarray(np.stack([live[j]["Z"] for j in range(NCORE)], 0))
            elif n == "cumA_all":
                m[n] = np.ascontiguousarray(
                    np.concatenate([live[j]["cumA"] for j in range(NCORE)], 1))
            elif n in com:
                m[n] = com[n]
            else:
                m[n] = cores[c][n]
        return m

    def launch(name, renames=None):
        ctx, nc = _get(name)
        in_maps = [resolve(ctx, c) for c in range(NCORE)]
        res = run_bass_kernel_spmd(nc, in_maps, core_ids=list(range(NCORE)))
        for c in range(NCORE):
            for on in ctx.output_names:
                key = renames.get(on, on) if renames else on
                live[c][key] = np.asarray(res.results[c][on])
        return res

    launch("A1")
    launch("mid")
    launch("tail")
    out = np.concatenate([live[c]["x4"] for c in range(NCORE)], 0)
    return out.reshape(1, 8192, DIM).astype(np.float32)


# version 03
# speedup vs baseline: 1.0832x; 1.0832x over previous
import numpy as np
import ml_dtypes
import concourse.bass as bass
import concourse.mybir as mybir
from concourse.bass_utils import run_bass_kernel_spmd

F32 = mybir.dt.float32
BF16 = mybir.dt.bfloat16
AF = mybir.ActivationFunctionType
ALU = mybir.AluOpType
AX = mybir.AxisListType

NCORE = 8
T = 1024
HT = 1056
HALO = 32
DIM = 4096
H, P, G, N, CS, KCONV = 128, 64, 8, 128, 128, 4
INTER = 8192
CONV_DIM = INTER + 2 * G * N
NPROJ = INTER + CONV_DIM + H
NCH = T // CS
EPS = 1e-5
NH, HD = 32, 128
WINDOW = 1024
BF = ml_dtypes.bfloat16



class EngT:

    def __init__(self, name):
        self.name = name
        self.n = 0
        self.sem = None
        self.cmds = []

    def op(self, method, *args, **kwargs):
        prev = self.n
        self.n += 1

        def fn(eng, prev=prev, method=method, args=args, kwargs=kwargs):
            if prev > 0:
                eng.wait_ge(self.sem, prev)
            getattr(eng, method)(*args, **kwargs).then_inc(self.sem, 1)
        self.cmds.append(fn)
        return self.n

    def wait(self, other, val):
        if val <= 0:
            return

        def fn(eng, other=other, val=val):
            eng.wait_ge(other.sem, val)
        self.cmds.append(fn)

    def raw(self, fn):
        self.cmds.append(fn)


class PeT(EngT):

    def op(self, method, *args, mark=False, **kwargs):
        if mark:
            self.n += 1

            def fn(eng, method=method, args=args, kwargs=kwargs):
                getattr(eng, method)(*args, **kwargs).then_inc(self.sem, 1)
        else:
            def fn(eng, method=method, args=args, kwargs=kwargs):
                getattr(eng, method)(*args, **kwargs)
        self.cmds.append(fn)
        return self.n

    def mm(self, out, lhsT, rhs, start, stop, mark=False):
        return self.op("matmul", out, lhsT, rhs, start=start, stop=stop, mark=mark)

    def mark(self):
        self.n += 1

        def fn(eng):
            eng.drain().then_inc(self.sem, 1)
        self.cmds.append(fn)
        return self.n


class QT:

    def __init__(self, name):
        self.name = name
        self.n = 0
        self.sem = None
        self.cmds = []

    def dma(self, out, in_, **kw):
        self.n += 16

        def fn(eng, out=out, in_=in_, kw=kw):
            eng.dma_start(out=out, in_=in_, **kw).then_inc(self.sem, 16)
        self.cmds.append(fn)
        return self.n

    def dmat(self, out, in_):
        self.n += 16

        def fn(eng, out=out, in_=in_):
            eng.dma_start_transpose(out, in_).then_inc(self.sem, 16)
        self.cmds.append(fn)
        return self.n

    def wait(self, other, val):
        if val <= 0:
            return

        def fn(eng, other=other, val=val):
            eng.wait_ge(other.sem, val)
        self.cmds.append(fn)

    def raw(self, fn):
        self.cmds.append(fn)


class Ctx:
    def __init__(self, name, final_outputs=()):
        self.name = name
        self.nc = bass.Bass("TRN2", target_bir_lowering=False)
        self.pe = PeT("pe")
        self.v = EngT("v")
        self.a = EngT("a")
        self.qs = QT("qs")
        self.qg = QT("qg")
        self.qa = QT("qa")
        self._guards = []
        self.input_names = []
        self.output_names = []
        self._live_dram = {}
        self._final_outputs = set(final_outputs)
        self._uniq = 0

    def din(self, name, shape, dtype=F32):
        if name in self._live_dram:
            return self._live_dram[name]
        t = self.nc.dram_tensor(name, list(shape), dtype, kind="ExternalInput")
        self.input_names.append(name)
        ap = t.ap()
        self._live_dram[name] = ap
        return ap

    def dout(self, name, shape, dtype=F32):
        if self._final_outputs and name not in self._final_outputs:
            t = self.nc.dram_tensor(name, list(shape), dtype, kind="Internal")
        else:
            t = self.nc.dram_tensor(name, list(shape), dtype, kind="ExternalOutput")
            self.output_names.append(name)
        ap = t.ap()
        self._live_dram[name] = ap
        return ap

    def phase_mark(self):
        return len(self._guards)

    def phase_free(self, mark):
        for g in reversed(self._guards[mark:]):
            g.__exit__(None, None, None)
        del self._guards[mark:]
        streams = [self.pe, self.v, self.a, self.qs, self.qg, self.qa]
        for wtr in streams:
            for p in streams:
                if p is not wtr:
                    wtr.wait(p, p.n)

    def dram(self, name, shape, dtype=F32):
        self._uniq += 1
        name = f"{name}_{self._uniq}"
        return self.nc.dram_tensor(name, list(shape), dtype, kind="Internal").ap()

    def sb(self, name, shape, dtype):
        self._uniq += 1
        name = f"{name}_{self._uniq}"
        g = self.nc.sbuf_tensor(name, list(shape), dtype)
        t = g.__enter__()
        self._guards.append(g)
        return t

    def ps(self, name, shape, dtype=F32):
        self._uniq += 1
        name = f"{name}_{self._uniq}"
        g = self.nc.psum_tensor(name, list(shape), dtype)
        t = g.__enter__()
        self._guards.append(g)
        return t

    def barrier(self, waiter, producers=None):
        if producers is None:
            producers = [e for e in (self.pe, self.v, self.a, self.qs, self.qg, self.qa)
                         if e is not waiter]
        for p in producers:
            waiter.wait(p, p.n)

    def finish_outputs(self):
        self.qs.wait(self.qs, self.qs.n)
        self.qg.wait(self.qg, self.qg.n)
        self.qa.wait(self.qa, self.qa.n)

    def build(self):
        nc = self.nc
        streams = [self.pe, self.v, self.a, self.qs, self.qg, self.qa]
        semg = []
        for s in streams:
            g = nc.semaphore("sem_" + s.name)
            s.sem = g.__enter__()
            semg.append(g)
        blockg = nc.Block()
        block = blockg.__enter__()

        def run(dec, obj):
            @dec
            def _(eng):
                for fn in obj.cmds:
                    fn(eng)
        run(block.tensor, self.pe)
        run(block.vector, self.v)
        run(block.scalar, self.a)
        run(block.sync, self.qs)
        run(block.gpsimd, self.qg)
        run(block.scalar, self.qa)
        blockg.__exit__(None, None, None)
        for g in semg:
            g.__exit__(None, None, None)
        for g in reversed(self._guards):
            g.__exit__(None, None, None)
        return nc


def cast_np(a, dtype=np.float32):
    return np.ascontiguousarray(np.asarray(a), dtype=dtype)



def prep_common(w):
    c = {}
    c["W_inT"] = cast_np(w["L0_attn_in_proj_w"].T)
    c["conv_wR"] = cast_np(w["L0_attn_conv_w"][:, 0, :].reshape(80, 128, KCONV).transpose(1, 0, 2))
    c["conv_bR"] = cast_np(w["L0_attn_conv_b"].reshape(80, 128).T)
    c["dt_bias"] = cast_np(w["L0_attn_dt_bias"].reshape(128, 1))
    c["A_log"] = cast_np(w["L0_attn_A_log"].reshape(128, 1))
    c["Dvec"] = cast_np(w["L0_attn_D"].reshape(128, 1))
    c["norm_wR"] = cast_np(w["L0_attn_norm_w"].reshape(64, 128).T)
    c["W_outT"] = cast_np(w["L0_attn_out_proj_w"].T)
    c["F0_WT"] = cast_np(w["L0_ffn_W"].T)
    c["F0_biasR"] = cast_np(w["L0_ffn_bias"].reshape(32, 128).T)
    c["F0_AcatT"] = cast_np(w["L0_ffn_A"].reshape(256, DIM).T)
    c["F0_Bcat"] = cast_np(w["L0_ffn_Bf"].transpose(0, 2, 1).reshape(256, DIM))
    c["F0_coefR"] = cast_np(np.repeat(w["L0_ffn_coeffs"], 16).reshape(2, 128).T)
    c["F0_scalR"] = cast_np(np.repeat(w["L0_ffn_scales"], 16).reshape(2, 128).T)
    c["WqT"] = cast_np(w["L1_attn_Wq"].T)
    c["WkT"] = cast_np(w["L1_attn_Wk"].T)
    c["WvT"] = cast_np(w["L1_attn_Wv"].T)
    c["WoT"] = cast_np(w["L1_attn_Wo"].T)
    c["F1_WT"] = cast_np(w["L1_ffn_W_base"].T)
    c["F1_biasR"] = cast_np(w["L1_ffn_b_base"].reshape(32, 128).T)
    c["F1_WrT"] = cast_np(w["L1_ffn_W_router"].T)
    c["F1_AcatT"] = cast_np(w["L1_ffn_A"].reshape(128, DIM).T)
    c["F1_Bcat"] = cast_np(w["L1_ffn_B"].transpose(0, 2, 1).reshape(128, DIM))
    E8 = np.zeros((8, 128), np.float32)
    for e in range(8):
        E8[e, e * 16:(e + 1) * 16] = 1.0
    c["E8"] = cast_np(E8, BF)
    c["D_B"] = cast_np(np.broadcast_to(w["L0_attn_D"].reshape(1, 128), (128, 128)))
    c["ident_f"] = cast_np(np.eye(128))
    c["onescol"] = cast_np(np.ones((128, 1)), BF)
    c["onesrow"] = cast_np(np.ones((1, 128)), BF)
    c["ident"] = cast_np(np.eye(128), BF)
    c["tri"] = cast_np(np.triu(np.ones((128, 128))), BF)
    return c


def prep_core(ci, w):
    x = w["L0_attn_hidden_states"][0]
    d = {}
    t0 = ci * T
    halo = np.zeros((HALO, DIM), np.float32) if ci == 0 else x[t0 - HALO:t0]
    d["x_in"] = cast_np(np.concatenate([halo, x[t0:t0 + T]], 0))
    cm = np.zeros((1, 8), np.float32)
    cm[0, :ci] = 1.0
    d["carrymask"] = cm
    d["carrymaskR"] = cast_np(np.broadcast_to(cm, (128, 8)))
    pos = np.arange(t0, t0 + T, dtype=np.float64)
    inv = 1.0 / 10000.0 ** (np.arange(0, HD, 2) / HD)
    fr = pos[:, None] * inv[None, :]
    emb = np.concatenate([fr, fr], -1)
    cos = np.cos(emb)
    sin = np.sin(emb)
    sin_signed = np.concatenate([-sin[:, :HD // 2], sin[:, HD // 2:]], -1)
    d["rope_cos"] = cast_np(cos.reshape(8, 128, HD).transpose(1, 0, 2))
    d["rope_sin"] = cast_np(sin_signed.reshape(8, 128, HD).transpose(1, 0, 2))
    m = np.zeros((2, 12, 128, 512), np.float32)
    for qg_ in range(2):
        q0 = qg_ * 512
        for kb in range(12):
            kstart = q0 - 1024 + kb * 128
            kidx = np.arange(128)[:, None] + kstart
            qidx = np.arange(512)[None, :] + q0
            rel = qidx - kidx
            ok = (rel >= 0) & (rel < WINDOW)
            if ci == 0:
                ok = ok & (kidx >= 0)
            m[qg_, kb] = ok
    d["wmask"] = cast_np(m, BF)
    return d



class LnKit:

    def __init__(self, ctx, tag="", slim=False):
        self.ctx = ctx
        self.x = ctx.sb(f"ln{tag}_x", [128, DIM], F32)
        self.sq = ctx.sb(f"ln{tag}_sq", [128, DIM], F32)
        self.u = None if slim else ctx.sb(f"ln{tag}_u", [128, DIM], BF16)
        self.s1 = ctx.sb(f"ln{tag}_s1", [128, 1], F32)
        self.s2 = ctx.sb(f"ln{tag}_s2", [128, 1], F32)
        self.mneg = ctx.sb(f"ln{tag}_m", [128, 1], F32)
        self.msq = ctx.sb(f"ln{tag}_m2", [128, 1], F32)
        self.var = ctx.sb(f"ln{tag}_v", [128, 1], F32)
        self.rstd = ctx.sb(f"ln{tag}_r", [128, 1], F32)
        self.last_tp = 0

    def stats_and_u(self, xt, rows, ubuf):
        ctx = self.ctx
        v, a = ctx.v, ctx.a
        v.op("reduce_sum", self.s1[0:rows, :], xt[0:rows, :], axis=AX.X)
        a.wait(v, v.n - 1)
        a.op("activation", self.sq[0:rows, :], xt[0:rows, :], AF.Square,
             accum_out=self.s2[0:rows, 0:1])
        v.op("tensor_scalar", self.mneg[0:rows, :], self.s1[0:rows, :],
             -1.0 / DIM, None, ALU.mult)
        v.op("tensor_tensor", self.msq[0:rows, :], self.mneg[0:rows, :],
             self.mneg[0:rows, :], ALU.mult)
        v.wait(a, a.n)
        v.op("tensor_scalar", self.var[0:rows, :], self.s2[0:rows, :],
             1.0 / DIM, EPS, ALU.mult, ALU.add)
        v.op("tensor_tensor", self.var[0:rows, :], self.var[0:rows, :],
             self.msq[0:rows, :], ALU.subtract)
        a.wait(v, v.n)
        a.op("activation", self.rstd[0:rows, :], self.var[0:rows, :], AF.Sqrt)
        v.wait(a, a.n)
        v.op("reciprocal", self.rstd[0:rows, :], self.rstd[0:rows, :])
        v.op("tensor_scalar", ubuf[0:rows, :], xt[0:rows, :],
             self.mneg[0:rows, 0:1], self.rstd[0:rows, 0:1], ALU.add, ALU.mult)
        return v.n

    def tile_from_hbm(self, x_h, row0, rows, uT=None, ucol=None, u_keep=None):
        ctx = self.ctx
        xt, ub = self.x, self.u
        ctx.qs.wait(ctx.v, ctx.v.n)
        ctx.qs.wait(ctx.a, ctx.a.n)
        ld = ctx.qs.dma(xt[0:rows, :], x_h[row0:row0 + rows, :])
        ctx.v.wait(ctx.qs, ld)
        ctx.a.wait(ctx.qs, ld)
        if self.last_tp:
            ctx.v.wait(ctx.qs, self.last_tp)
        done = self.stats_and_u(xt, rows, ub)
        if u_keep is not None:
            ctx.v.op("tensor_copy", u_keep[0:rows, :], ub[0:rows, :])
        if uT is not None:
            ctx.qs.wait(ctx.v, done)
            tp = ctx.qs.dmat(uT[:, :, ucol:ucol + rows],
                             ub[0:rows, :].rearrange("t (a r) -> t a r", a=32))
            self.last_tp = tp
        return xt


def emit_weight_gemm(ctx, wT_h, uT, n_k, do_tiles, n_cols, evac, wname,
                     extra_mms=None, pre_tile=None):
    pe, qg, v = ctx.pe, ctx.qg, ctx.v
    wb = [ctx.sb(f"{wname}0", [128, n_k, 128], BF16),
          ctx.sb(f"{wname}1", [128, n_k, 128], BF16)]
    psl = [ctx.ps(f"{wname}_psA", [128, 512], F32),
           ctx.ps(f"{wname}_psB", [128, 512], F32),
           ctx.ps(f"{wname}_psC", [128, 512], F32),
           ctx.ps(f"{wname}_psD", [128, 512], F32)]
    nsplit = [(i, min(512, n_cols - i)) for i in range(0, n_cols, 512)]
    assert len(nsplit) <= 2
    pe_done_at = {}
    v_done_at = {}
    for d in range(do_tiles):
        b = wb[d % 2]
        if d >= 2:
            qg.wait(pe, pe_done_at[d - 2])
        ld = qg.dma(b[:, :, :],
                    wT_h[:, d * 128:(d + 1) * 128].rearrange("(a k) m -> k a m", k=128))
        if pre_tile is not None:
            ld = pre_tile(d) or ld
        pe.raw(lambda eng, ld=ld: eng.wait_ge(qg.sem, ld))
        if d >= 2:
            pe.raw(lambda eng, val=v_done_at[d - 2]: eng.wait_ge(v.sem, val))
        pss = [psl[(2 * d) % 4], psl[(2 * d + 1) % 4]]
        for ci, (c0, cw) in enumerate(nsplit):
            ps = pss[ci]
            for k in range(n_k):
                pe.mm(ps[:, 0:cw], b[:, k, :], uT[:, k, c0:c0 + cw],
                      start=(k == 0), stop=(extra_mms is None and k == n_k - 1))
            if extra_mms is not None:
                extra_mms(d, ps, c0, cw)
        pe.mark()
        pe_done_at[d] = pe.n
        evac(d, pss, nsplit)
        v_done_at[d] = v.n
    return




def emit_post_ln_residual(ctx, ln, src_fm, x_old_h, x_new_h):
    v, a, qs = ctx.v, ctx.a, ctx.qs
    tm = ctx.sb("plr_tm_" + ctx.name, [128, DIM], BF16)
    xold = ctx.sb("plr_xold_" + ctx.name, [128, DIM], F32)
    nrm = ln.sq
    for i in range(8):
        qs.wait(v, v.n)
        tp = qs.dmat(tm[:, :], src_fm[:, i * 128:(i + 1) * 128].opt())
        ld = qs.dma(xold[:, :], x_old_h[i * 128:(i + 1) * 128, :])
        v.wait(qs, tp)
        v.wait(qs, ld)
        a.wait(qs, tp)
        a.wait(v, v.n)
        ln.stats_and_u(tm, 128, nrm)
        v.op("tensor_tensor", nrm[:, :], nrm[:, :], xold[:, :], ALU.add)
        qs.wait(v, v.n)
        qs.dma(x_new_h[i * 128:(i + 1) * 128, :], nrm[:, :])


def build_phaseB(ctx=None):
    if ctx is None:
        ctx = Ctx("phaseB")
    x1 = ctx.din("x1", (T, DIM))
    WT = ctx.din("F0_WT", (DIM, DIM))
    biasR = ctx.din("F0_biasR", (128, 32))
    AcatT = ctx.din("F0_AcatT", (DIM, 256))
    Bcat = ctx.din("F0_Bcat", (256, DIM))
    coefR = ctx.din("F0_coefR", (128, 2))
    scalR = ctx.din("F0_scalR", (128, 2))
    x2 = ctx.dout("x2", (T, DIM))
    ffn_h = ctx.dram("ffn_h", (DIM, T), BF16)

    uT = ctx.sb("uT", [128, 32, T], BF16)
    ln = LnKit(ctx, "b")
    bias_s = ctx.sb("bias_s", [128, 32], F32)
    coef_s = ctx.sb("coef_s", [128, 2], F32)
    scal_s = ctx.sb("scal_s", [128, 2], F32)
    AcT_s = ctx.sb("AcT_s", [128, 32, 256], BF16)
    bcb = [ctx.sb("bcb0", [128, 2, 128], BF16), ctx.sb("bcb1", [128, 2, 128], BF16)]
    xa = ctx.sb("xa", [128, 2, T], BF16)
    stage = ctx.sb("stage", [128, T], BF16)
    psX = ctx.ps("psX", [128, 512], F32)
    psY = ctx.ps("psY", [128, 512], F32)

    pe, v, a, qs, qg = ctx.pe, ctx.v, ctx.a, ctx.qs, ctx.qg

    qg.dma(bias_s[:, :], biasR[:, :])
    qg.dma(coef_s[:, :], coefR[:, :])
    qg.dma(scal_s[:, :], scalR[:, :])
    qg.dma(AcT_s[:, :, :], AcatT[:, :].rearrange("(a k) m -> k a m", k=128))

    for i in range(8):
        ln.tile_from_hbm(x1, i * 128, 128, uT=uT, ucol=i * 128)

    ctx.barrier(pe, [ctx.qs, ctx.v, ctx.qg])
    for er in range(2):
        for ci in range(2):
            ps = psX if ci == 0 else psY
            for k in range(32):
                pe.mm(ps[:, :], AcT_s[:, k, er * 128:(er + 1) * 128],
                      uT[:, k, ci * 512:(ci + 1) * 512], start=(k == 0), stop=(k == 31))
            pe.mark()
            v.wait(pe, pe.n)
            v.op("tensor_scalar", xa[:, er, ci * 512:(ci + 1) * 512],
                 ps[:, :], coef_s[:, er:er + 1], scal_s[:, er:er + 1],
                 ALU.mult, ALU.mult)
    pe.raw(lambda eng, val=v.n: eng.wait_ge(v.sem, val))

    def pre_tile(d):
        return qg.dma(bcb[d % 2][:, :, :],
                      Bcat[:, d * 128:(d + 1) * 128].rearrange("(c k) m -> k c m", k=128))

    def extra(d, ps, c0, cw):
        pe.mm(ps[:, 0:cw], bcb[d % 2][:, 0, :], xa[:, 0, c0:c0 + cw], start=False, stop=False)
        pe.mm(ps[:, 0:cw], bcb[d % 2][:, 1, :], xa[:, 1, c0:c0 + cw], start=False, stop=True)

    def evac(d, pss, nsplit):
        v.wait(pe, pe.n)
        for ci, (c0, cw) in enumerate(nsplit):
            v.op("tensor_scalar", stage[:, c0:c0 + cw], pss[ci][:, 0:cw],
                 bias_s[:, d:d + 1], None, ALU.add)
        qs.wait(v, v.n)
        st = qs.dma(ffn_h[d * 128:(d + 1) * 128, :], stage[:, :])
        v.wait(qs, st)

    emit_weight_gemm(ctx, WT, uT, 32, 32, T, evac, "w0", extra_mms=extra,
                     pre_tile=pre_tile)

    emit_post_ln_residual(ctx, ln, ffn_h, x1, x2)
    ctx.finish_outputs()
    return ctx




def build_phaseD(ctx=None, out_name="x4"):
    if ctx is None:
        ctx = Ctx("phaseD")
    x3 = ctx.din("x3", (T, DIM))
    WT = ctx.din("F1_WT", (DIM, DIM))
    biasR = ctx.din("F1_biasR", (128, 32))
    WrT = ctx.din("F1_WrT", (DIM, 8))
    AcatT = ctx.din("F1_AcatT", (DIM, 128))
    Bcat = ctx.din("F1_Bcat", (128, DIM))
    E8 = ctx.din("E8", (8, 128), BF16)
    ident = ctx.din("ident", (128, 128), BF16)
    x4 = ctx.dout(out_name, (T, DIM))
    ffn_h = ctx.dram("ffn_h", (DIM, T), BF16)

    uT = ctx.sb("uT", [128, 32, T], BF16)
    ln = LnKit(ctx, "d")
    bias_s = ctx.sb("bias_s", [128, 32], F32)
    WrT_s = ctx.sb("WrT_s", [128, 32, 8], BF16)
    AcT_s = ctx.sb("AcT_s", [128, 32, 128], BF16)
    Bc_s = ctx.sb("Bc_s", [128, 32, 128], BF16)
    E8_s = ctx.sb("E8_s", [8, 128], BF16)
    id_s = ctx.sb("id_s", [128, 128], BF16)
    lg = ctx.sb("lg", [128, 8], F32)
    m1 = ctx.sb("m1", [128, 1], F32)
    m2 = ctx.sb("m2", [128, 1], F32)
    ge1 = ctx.sb("ge1", [128, 8], F32)
    lg2 = ctx.sb("lg2", [128, 8], F32)
    ex = ctx.sb("ex", [128, 8], F32)
    den = ctx.sb("den", [128, 1], F32)
    rec = ctx.sb("rec", [128, 1], F32)
    wtm = ctx.sb("wtm", [128, 8], BF16)
    w_fm = ctx.sb("w_fm", [8, T], BF16)
    wex = ctx.sb("wex", [128, T], BF16)
    hpr = ctx.sb("hpr", [128, T], BF16)
    stage = ctx.sb("stage", [128, T], BF16)
    psR = ctx.ps("psR", [128, 8], F32)
    psT = ctx.ps("psT", [8, 128], BF16)
    psX = ctx.ps("psX", [128, 512], F32)
    psY = ctx.ps("psY", [128, 512], F32)

    pe, v, a, qs, qg = ctx.pe, ctx.v, ctx.a, ctx.qs, ctx.qg

    qg.dma(bias_s[:, :], biasR[:, :])
    qg.dma(WrT_s[:, :, :], WrT[:, :].rearrange("(a k) m -> k a m", k=128))
    qg.dma(AcT_s[:, :, :], AcatT[:, :].rearrange("(a k) m -> k a m", k=128))
    qg.dma(Bc_s[:, :, :], Bcat[:, :].rearrange("k (a m) -> k a m", m=128))
    qg.dma(E8_s[:, :], E8[:, :])
    qg.dma(id_s[:, :], ident[:, :])

    for i in range(8):
        ln.tile_from_hbm(x3, i * 128, 128, uT=uT, ucol=i * 128)

    ctx.barrier(pe, [ctx.qs, ctx.v, ctx.qg])
    for i in range(8):
        for k in range(32):
            pe.mm(psR[:, :], uT[:, k, i * 128:(i + 1) * 128], WrT_s[:, k, :],
                  start=(k == 0), stop=(k == 31))
        pe.mark()
        v.wait(pe, pe.n)
        v.op("tensor_copy", lg[:, :], psR[:, :])
        v.op("reduce_max", m1[:, :], lg[:, :], axis=AX.X)
        v.op("tensor_scalar", ge1[:, :], lg[:, :], m1[:, 0:1], None, ALU.is_ge)
        v.op("scalar_tensor_tensor", lg2[:, :], ge1[:, :], -1e30, lg[:, :],
             ALU.mult, ALU.add)
        v.op("reduce_max", m2[:, :], lg2[:, :], axis=AX.X)
        v.op("tensor_scalar", m1[:, :], m1[:, :], -1.0, None, ALU.mult)
        a.wait(v, v.n)
        a.op("activation", ex[:, :], lg[:, :], AF.Exp, bias=m1[:, 0:1])
        v.wait(a, a.n)
        v.op("tensor_scalar", ge1[:, :], lg[:, :], m2[:, 0:1], None, ALU.is_ge)
        v.op("tensor_tensor", ex[:, :], ex[:, :], ge1[:, :], ALU.mult)
        v.op("reduce_sum", den[:, :], ex[:, :], axis=AX.X)
        v.op("reciprocal", rec[:, :], den[:, :])
        v.op("tensor_scalar", wtm[:, :], ex[:, :], rec[:, 0:1], None, ALU.mult)
        pe.raw(lambda eng, val=v.n: eng.wait_ge(v.sem, val))
        pe.op("transpose", psT[:, :], wtm[:, :], id_s[:, :])
        pe.mark()
        v.wait(pe, pe.n)
        v.op("tensor_copy", w_fm[:, i * 128:(i + 1) * 128], psT[0:8, :])
    pe.raw(lambda eng, val=v.n: eng.wait_ge(v.sem, val))
    for ci in range(2):
        ps = psX if ci == 0 else psY
        pe.mm(ps[:, :], E8_s[:, :], w_fm[0:8, ci * 512:(ci + 1) * 512],
              start=True, stop=True)
        pe.mark()
        v.wait(pe, pe.n)
        v.op("tensor_scalar", wex[:, ci * 512:(ci + 1) * 512], ps[:, :],
             2.0, None, ALU.mult)
    for ci in range(2):
        ps = psX if ci == 0 else psY
        pe.raw(lambda eng, val=v.n: eng.wait_ge(v.sem, val))
        for k in range(32):
            pe.mm(ps[:, :], AcT_s[:, k, :], uT[:, k, ci * 512:(ci + 1) * 512],
                  start=(k == 0), stop=(k == 31))
        pe.mark()
        v.wait(pe, pe.n)
        v.op("tensor_tensor", hpr[:, ci * 512:(ci + 1) * 512], ps[:, :],
             wex[:, ci * 512:(ci + 1) * 512], ALU.mult)
    pe.raw(lambda eng, val=v.n: eng.wait_ge(v.sem, val))

    def extra(d, ps, c0, cw):
        pe.mm(ps[:, 0:cw], Bc_s[:, d, :], hpr[:, c0:c0 + cw], start=False, stop=True)

    def evac(d, pss, nsplit):
        v.wait(pe, pe.n)
        for ci, (c0, cw) in enumerate(nsplit):
            v.op("tensor_scalar", stage[:, c0:c0 + cw], pss[ci][:, 0:cw],
                 bias_s[:, d:d + 1], None, ALU.add)
        qs.wait(v, v.n)
        st = qs.dma(ffn_h[d * 128:(d + 1) * 128, :], stage[:, :])
        v.wait(qs, st)

    emit_weight_gemm(ctx, WT, uT, 32, 32, T, evac, "w1", extra_mms=extra)

    emit_post_ln_residual(ctx, ln, ffn_h, x3, x4)
    ctx.finish_outputs()
    return ctx




def emit_roleswap_gemm(ctx, wT_h, uT, out_h, wname, rope=None):
    pe, v, qs, qg = ctx.pe, ctx.v, ctx.qs, ctx.qg
    if not hasattr(ctx, "_rs_bufs"):
        ctx._rs_bufs = dict(
            wb=[ctx.sb("rs_wb0", [128, 32, 512], BF16),
                ctx.sb("rs_wb1", [128, 32, 512], BF16)],
            pss=[ctx.ps("rs_psA", [128, 512], F32),
                 ctx.ps("rs_psB", [128, 512], F32)],
            stage=ctx.sb("rs_st", [128, 512], BF16),
            t1=ctx.sb("rs_t1", [128, 512], F32),
            t2=ctx.sb("rs_t2", [128, 512], F32))
    wb = ctx._rs_bufs["wb"]
    pss = ctx._rs_bufs["pss"]
    stage = ctx._rs_bufs["stage"]
    t1 = ctx._rs_bufs["t1"]
    t2 = ctx._rs_bufs["t2"]
    qg.wait(pe, pe.n)
    qg.wait(v, v.n)
    pe_done = {}
    v_done = {}
    for ci in range(8):
        b = wb[ci % 2]
        if ci >= 2:
            qg.wait(pe, pe_done[ci - 2])
        ld = qg.dma(b[:, :, :],
                    wT_h[:, ci * 512:(ci + 1) * 512].rearrange("(a k) m -> k a m", k=128))
        pe.raw(lambda eng, ld=ld: eng.wait_ge(qg.sem, ld))
        for ti in range(8):
            ps = pss[ti % 2]
            if (ci, ti) != (0, 0) and (ci, ti) != (0, 1):
                prev = v_done.get((ci, ti - 2)) or v_done.get((ci - 1, ti + 6))
                if prev:
                    pe.raw(lambda eng, val=prev: eng.wait_ge(v.sem, val))
            for k in range(32):
                pe.mm(ps[:, :], uT[:, k, ti * 128:(ti + 1) * 128], b[:, k, :],
                      start=(k == 0), stop=(k == 31))
            pe.mark()
            pe_done[ci] = pe.n
            v.wait(pe, pe.n)
            if rope is None:
                v.op("tensor_copy", stage[:, :], ps[:, :])
            else:
                cos_s, sin_s = rope
                pv = ps[:, :].rearrange("t (h v f) -> t h v f", h=4, v=2)
                cl = cos_s[:, ti, 0:64].unsqueeze(1).broadcast_to([128, 4, 64])
                ch = cos_s[:, ti, 64:128].unsqueeze(1).broadcast_to([128, 4, 64])
                sl = sin_s[:, ti, 0:64].unsqueeze(1).broadcast_to([128, 4, 64])
                sh = sin_s[:, ti, 64:128].unsqueeze(1).broadcast_to([128, 4, 64])
                t1v = t1[:, :].rearrange("t (h v f) -> t h v f", h=4, v=2)
                v.op("tensor_tensor", t1v[:, :, 0, :], pv[:, :, 0, :], cl, ALU.mult)
                v.op("tensor_tensor", t1v[:, :, 1, :], pv[:, :, 1, :], ch, ALU.mult)
                t2v = t2[:, :].rearrange("t (h v f) -> t h v f", h=4, v=2)
                v.op("tensor_tensor", t2v[:, :, 0, :], pv[:, :, 1, :], sl, ALU.mult)
                v.op("tensor_tensor", t2v[:, :, 1, :], pv[:, :, 0, :], sh, ALU.mult)
                v.op("tensor_tensor", stage[:, :], t1[:, :], t2[:, :], ALU.add)
            qs.wait(v, v.n)
            st = qs.dma(out_h[ti * 128:(ti + 1) * 128, ci * 512:(ci + 1) * 512],
                        stage[:, :])
            v.wait(qs, st)
            v_done[(ci, ti)] = v.n


def build_phaseC1(ctx=None):
    if ctx is None:
        ctx = Ctx("phaseC1")
    x2 = ctx.din("x2", (T, DIM))
    WqT = ctx.din("WqT", (DIM, DIM))
    WkT = ctx.din("WkT", (DIM, DIM))
    WvT = ctx.din("WvT", (DIM, DIM))
    cosR = ctx.din("rope_cos", (128, 8, HD))
    sinR = ctx.din("rope_sin", (128, 8, HD))
    q_h = ctx.dout("q_tm", (T, DIM), BF16)
    k_h = ctx.dout("k_tm", (T, DIM), BF16)
    v_h = ctx.dout("v_tm", (T, DIM), BF16)

    uT = ctx.sb("uT", [128, 32, T], BF16)
    ln = LnKit(ctx, "c")
    cos_s = ctx.sb("cos_s", [128, 8, HD], F32)
    sin_s = ctx.sb("sin_s", [128, 8, HD], F32)

    qg = ctx.qg
    qg.dma(cos_s[:, :, :], cosR[:, :, :])
    qg.dma(sin_s[:, :, :], sinR[:, :, :])

    for i in range(8):
        ln.tile_from_hbm(x2, i * 128, 128, uT=uT, ucol=i * 128)
    ctx.barrier(ctx.pe, [ctx.qs, ctx.v, ctx.qg])

    emit_roleswap_gemm(ctx, WqT, uT, q_h, "wq", rope=(cos_s, sin_s))
    emit_roleswap_gemm(ctx, WkT, uT, k_h, "wk", rope=(cos_s, sin_s))
    emit_roleswap_gemm(ctx, WvT, uT, v_h, "wv", rope=None)
    ctx.finish_outputs()
    return ctx


def build_phaseC2(ctx=None):
    if ctx is None:
        ctx = Ctx("phaseC2")
    x2 = ctx.din("x2", (T, DIM))
    q_h = ctx.din("q_tm", (T, DIM), BF16)
    k_h = ctx.din("k_tm", (T, DIM), BF16)
    v_h = ctx.din("v_tm", (T, DIM), BF16)
    kp_h = ctx.din("kprev_tm", (T, DIM), BF16)
    vp_h = ctx.din("vprev_tm", (T, DIM), BF16)
    wmask = ctx.din("wmask", (2, 12, 128, 512), BF16)
    WoT = ctx.din("WoT", (DIM, DIM))
    onescol = ctx.din("onescol", (128, 1), BF16)
    onesrow = ctx.din("onesrow", (1, 128), BF16)
    x3 = ctx.dout("x3", (T, DIM))
    attn_h = ctx.dram("attn_h", (DIM, T), BF16)
    wo_h = ctx.dram("wo_h", (DIM, T), BF16)

    ln = LnKit(ctx, "c2", slim=True)
    mask_s = ctx.sb("mask_s", [128, 2, 12, 512], BF16)
    ones_s = ctx.sb("ones_s", [128, 1], BF16)
    onesr_s = ctx.sb("onesr_s", [1, 128], BF16)
    qh_s = [ctx.sb("qh0", [128, T], BF16)]
    kw_s = [ctx.sb("kw0", [128, 2048], BF16)]
    vw_s = [ctx.sb("vw0", [128, 16, 128], BF16)]
    probs = ctx.sb("probs", [128, 12, 512], BF16)
    sum_f = ctx.sb("sum_f", [1, 512], F32)
    rec_f = ctx.sb("rec_f", [1, 512], F32)
    rec_b = ctx.sb("rec_b", [1, 512], BF16)
    recB = ctx.sb("recB", [128, 512], BF16)
    ast = ctx.sb("ast", [128, 512], BF16)
    psS = ctx.ps("psS", [128, 512], F32)
    psSum = ctx.ps("psSum", [1, 512], F32)
    psAV = ctx.ps("psAV", [128, 512], F32)
    psRB = ctx.ps("psRB", [128, 512], F32)

    pe, v, a, qs, qg = ctx.pe, ctx.v, ctx.a, ctx.qs, ctx.qg

    qg.dma(mask_s[:, :, :, :], wmask[:, :, :, :].rearrange("g b k q -> k g b q"))
    qg.dma(ones_s[:, :], onescol[:, :])
    qg.dma(onesr_s[:, :], onesrow[:, :])

    SC = 1.0 / (HD ** 0.5)
    ld_done = {}
    v_done_head = {}
    for h in range(32):
        qb, kb_, vb = qh_s[0], kw_s[0], vw_s[0]
        if h >= 1:
            qs.wait(v, v_done_head[h - 1])
            qs.wait(pe, pe.n)
        l1 = qs.dmat(qb[:, :], q_h[:, h * 128:(h + 1) * 128].opt())
        l2 = qs.dmat(kb_[:, 0:1024], kp_h[:, h * 128:(h + 1) * 128].opt())
        l3 = qs.dmat(kb_[:, 1024:2048], k_h[:, h * 128:(h + 1) * 128].opt())
        l4 = qs.dma(vb[:, 0:8, :], vp_h[:, h * 128:(h + 1) * 128].rearrange("(b r) d -> r b d", r=128))
        l5 = qs.dma(vb[:, 8:16, :], v_h[:, h * 128:(h + 1) * 128].rearrange("(b r) d -> r b d", r=128))
        pe.raw(lambda eng, val=l5: eng.wait_ge(qs.sem, val))
        for qg_i in range(2):
            for kb in range(12):
                koff = qg_i * 512 + kb * 128
                pe.raw(lambda eng, val=a.n: eng.wait_ge(a.sem, val))
                pe.mm(psS[:, :], kb_[:, koff:koff + 128], qb[:, qg_i * 512:(qg_i + 1) * 512],
                      start=True, stop=True)
                pe.mark()
                a.wait(pe, pe.n)
                if kb > 0 or qg_i > 0 or h > 0:
                    a.wait(v, v.n)
                a.op("activation", probs[:, kb, :], psS[:, :], AF.Exp, scale=SC)
                v.wait(a, a.n)
                v.op("tensor_tensor", probs[:, kb, :], probs[:, kb, :],
                     mask_s[:, qg_i, kb, :], ALU.mult)
            pv = v.n
            pe.raw(lambda eng, val=pv: eng.wait_ge(v.sem, val))
            for kb in range(12):
                pe.mm(psSum[:, :], ones_s[:, 0:1],
                      probs[:, kb, :], start=(kb == 0), stop=(kb == 11))
            for kb in range(12):
                koff = qg_i * 512 + kb * 128
                bidx = qg_i * 4 + kb
                pe.mm(psAV[:, :], vb[:, bidx, :],
                      probs[:, kb, :], start=(kb == 0), stop=(kb == 11))
            pe.mark()
            v.wait(pe, pe.n)
            v.op("tensor_copy", sum_f[:, :], psSum[0:1, :])
            v.op("reciprocal", rec_f[:, :], sum_f[:, :])
            v.op("tensor_copy", rec_b[:, :], rec_f[:, :])
            pe.raw(lambda eng, val=v.n: eng.wait_ge(v.sem, val))
            pe.mm(psRB[:, :], onesr_s[0:1, :], rec_b[0:1, :], start=True, stop=True)
            pe.mark()
            v.wait(pe, pe.n)
            v.op("tensor_copy", recB[:, :], psRB[:, :])
            v.op("tensor_tensor", ast[:, :], psAV[:, :], recB[:, :], ALU.mult)
            qs.wait(v, v.n)
            st = qs.dma(attn_h[h * 128:(h + 1) * 128, qg_i * 512:(qg_i + 1) * 512],
                        ast[:, :])
            v.wait(qs, st)
        v_done_head[h] = v.n

    attn_s = ctx.sb("attn_s", [128, 32, T], BF16)
    la = qs.dma(attn_s[:, :, :], attn_h[:, :].rearrange("(a k) t -> k a t", k=128))
    pe.raw(lambda eng, val=la: eng.wait_ge(qs.sem, val))
    stage2 = ctx.sb("stage2", [128, T], BF16)

    def evac(d, pss2, nsplit):
        v.wait(pe, pe.n)
        for ci, (c0, cw) in enumerate(nsplit):
            v.op("tensor_copy", stage2[:, c0:c0 + cw], pss2[ci][:, 0:cw])
        qs.wait(v, v.n)
        st = qs.dma(wo_h[d * 128:(d + 1) * 128, :], stage2[:, :])
        v.wait(qs, st)

    emit_weight_gemm(ctx, WoT, attn_s, 32, 32, T, evac, "wo")
    emit_post_ln_residual(ctx, ln, wo_h, x2, x3)
    ctx.finish_outputs()
    return ctx




def build_phaseA1(ctx=None):
    if ctx is None:
        ctx = Ctx("phaseA1")
    x_in = ctx.din("x_in", (HT, DIM))
    WT = ctx.din("W_inT", (DIM, NPROJ))
    convw = ctx.din("conv_wR", (128, 80, KCONV))
    convb = ctx.din("conv_bR", (128, 80))
    dtb = ctx.din("dt_bias", (128, 1))
    alog = ctx.din("A_log", (128, 1))
    normw = ctx.din("norm_wR", (128, 64))
    ident = ctx.din("ident", (128, 128), BF16)
    onesr = ctx.din("onesrow", (1, 128), BF16)

    xtm_h = ctx.dout("xtm", (T, INTER), BF16)
    gate_h = ctx.dout("gate", (INTER, T), BF16)
    bc_h = ctx.dout("bc", (2 * G * N, T), BF16)
    dt_h = ctx.dout("dt", (128, T))
    Rloc_h = ctx.dout("Rloc", (NCH, 128, INTER))
    Z_h = ctx.dout("Z", (128, INTER))
    cumA_h = ctx.dout("cumA", (128, 1))
    decflat_h = ctx.dram("decflat", (1, 1024))

    uT = ctx.sb("uT", [128, 32, HT], BF16)
    ln = LnKit(ctx, "a")
    wb = [ctx.sb("wb0", [128, 32, 128], BF16), ctx.sb("wb1", [128, 32, 128], BF16)]
    ident_s = ctx.sb("ident_s", [128, 128], BF16)
    onesr_s = ctx.sb("onesr_s", [1, 128], BF16)
    convw_s = ctx.sb("convw_s", [128, 80, KCONV], F32)
    convb_s = ctx.sb("convb_s", [128, 80], F32)
    dtb_s = ctx.sb("dtb_s", [128, 1], F32)
    alog_s = ctx.sb("alog_s", [128, 1], F32)
    normw_s = ctx.sb("normw_s", [128, 64], F32)
    A_col = ctx.sb("A_col", [128, 1], F32)
    hbc_s = ctx.sb("hbc_s", [128, HT], BF16)
    diag_s = ctx.sb("diag_s", [128, 4, 128], BF16)
    act_s = ctx.sb("act_s", [128, T], BF16)
    dt_s = ctx.sb("dt_s", [128, T], F32)
    xb_s = ctx.sb("xb_s", [128, T], F32)
    a_s = ctx.sb("a_s", [128, T], F32)
    Acum_s = ctx.sb("Acum_s", [128, T], F32)
    zeros_s = ctx.sb("zeros_s", [128, CS], F32)
    dec_s = ctx.sb("dec_s", [128, T], F32)
    sp1_s = dec_s
    ddt_s = ctx.sb("ddt_s", [128, T], F32)
    ddtT_s = ctx.sb("ddtT_s", [128, NCH, 128], F32)
    dec8_s = ctx.sb("dec8_s", [128, NCH], F32)
    decB_s = ctx.sb("decB_s", [128, 1024], F32)
    Btm_s = ctx.sb("Btm_s", [128, NCH, 128], BF16)
    xtmg_s = ctx.sb("xtmg_s", [128, NCH, 8, 128], BF16)
    xsl_s = ctx.sb("xsl_s", [128, NCH, 8, 128], BF16)
    Rg_s = ctx.sb("Rg_s", [128, 1024], F32)
    Rt_s = ctx.sb("Rt_s", [128, 1024], F32)
    cumA_s = ctx.sb("cumA_s", [128, 1], F32)
    stage_s = None

    psI = [ctx.ps("psIa", [128, 512], F32), ctx.ps("psIb", [128, 512], F32)]
    psI2 = [ctx.ps("psIa2", [128, 512], F32), ctx.ps("psIb2", [128, 512], F32)]
    psIh = ctx.ps("psIh", [128, 32], F32)
    psC0 = ctx.ps("psCa", [128, 512], F32)
    psC = [psC0, psC0]
    psS = ctx.ps("psS", [128, 1024], F32)
    psT = psC0[:, 0:64].bitcast(BF16)

    pe, v, a, qs, qg = ctx.pe, ctx.v, ctx.a, ctx.qs, ctx.qg

    qg.dma(ident_s[:, :], ident[:, :])
    qg.dma(onesr_s[:, :], onesr[:, :])
    qg.dma(convw_s[:, :, :], convw[:, :, :])
    qg.dma(convb_s[:, :], convb[:, :])
    qg.dma(dtb_s[:, :], dtb[:, :])
    qg.dma(alog_s[:, :], alog[:, :])
    lc = qg.dma(normw_s[:, :], normw[:, :])

    a.wait(qg, lc)
    a.op("activation", A_col[:, :], alog_s[:, :], AF.Exp)
    v.wait(a, a.n)
    v.op("tensor_scalar", A_col[:, :], A_col[:, :], -1.0, None, ALU.mult)
    v.op("memset", zeros_s[:, :], 0.0)

    ln.tile_from_hbm(x_in, 0, HALO, uT=uT, ucol=0)
    for i in range(8):
        ln.tile_from_hbm(x_in, HALO + i * 128, 128, uT=uT, ucol=HALO + i * 128)
    ctx.barrier(pe, [ctx.qs, ctx.v, ctx.qg])

    NSsets = [[(32, 512, psI[0]), (544, 512, psI[1]), (0, 32, psIh)],
              [(32, 512, psI2[0]), (544, 512, psI2[1]), (0, 32, psIh)]]
    _ip_state = {"marks": {}, "seq": [], "rdr": {}}

    def inproj_tile(d, halo_too=True):
        seq = _ip_state["seq"]
        if len(seq) >= 2:
            qg.wait(pe, _ip_state["marks"][seq[-2]])
        b = wb[len(seq) % 2]
        ld = qg.dma(b[:, :, :],
                    WT[:, d * 128:(d + 1) * 128].rearrange("(a k) m -> k a m", k=128))
        pe.raw(lambda eng, val=ld: eng.wait_ge(qg.sem, val))
        if len(seq) >= 2:
            pv, pa = _ip_state["rdr"].get(seq[-2], (0, 0))
            pe.raw(lambda eng, val=pv: eng.wait_ge(v.sem, val))
            pe.raw(lambda eng, val=pa: eng.wait_ge(a.sem, val))
        NS = NSsets[len(seq) % 2]
        for (c0, cw, ps) in NS:
            if c0 == 0:
                if not halo_too:
                    continue
                if len(seq) >= 1 and seq[-1] in _ip_state["rdr"]:
                    pv, pa = _ip_state["rdr"][seq[-1]]
                    pe.raw(lambda eng, val=pv: eng.wait_ge(v.sem, val))
            for k in range(32):
                pe.mm(ps[:, 0:cw], b[:, k, :], uT[:, k, c0:c0 + cw],
                      start=(k == 0), stop=(k == 31))
        pe.mark()
        seq.append(d)
        _ip_state["marks"][d] = pe.n
        return NS

    def inproj_done(d):
        _ip_state["rdr"][d] = (v.n, a.n)

    def conv_and_silu(d, cidx):
        for k in range(KCONV):
            v.op("tensor_scalar", diag_s[:, k, :], ident_s[:, :],
                 convw_s[:, cidx:cidx + 1, k:k + 1].rearrange("p a b -> p (a b)"),
                 None, ALU.mult)
        pe.raw(lambda eng, val=v.n: eng.wait_ge(v.sem, val))
        for ci in range(2):
            pe.raw(lambda eng, val=a.n: eng.wait_ge(a.sem, val))
            for k in range(KCONV):
                pe.mm(psC[ci][:, :], diag_s[:, k, :],
                      hbc_s[:, 29 + k + ci * 512: 29 + k + ci * 512 + 512],
                      start=(k == 0), stop=(k == KCONV - 1))
            pe.mark()
            a.wait(pe, pe.n)
            a.op("activation", act_s[:, ci * 512:(ci + 1) * 512], psC[ci][:, :],
                 AF.Silu, bias=convb_s[:, cidx:cidx + 1])

    NS_ = inproj_tile(144, halo_too=False)
    v.wait(pe, pe.n)
    for ci in range(2):
        v.op("tensor_scalar", xb_s[:, ci * 512:(ci + 1) * 512], NS_[ci][2][:, :],
             dtb_s[:, 0:1], None, ALU.add)
    inproj_done(144)
    a.wait(v, v.n)
    a.op("activation", sp1_s[:, :], xb_s[:, :], AF.Abs)
    a.op("activation", sp1_s[:, :], sp1_s[:, :], AF.Exp, scale=-1.0)
    a.op("activation", sp1_s[:, :], sp1_s[:, :], AF.Ln, bias=1.0)
    v.wait(a, a.n)
    v.op("tensor_scalar", dt_s[:, :], xb_s[:, :], 0.0, None, ALU.max)
    v.op("tensor_tensor", dt_s[:, :], dt_s[:, :], sp1_s[:, :], ALU.add)
    qs.wait(v, v.n)
    qs.dma(dt_h[:, :], dt_s[:, :])
    v.op("tensor_scalar", a_s[:, :], dt_s[:, :], A_col[:, 0:1], None, ALU.mult)
    for c in range(NCH):
        v.op("tensor_tensor_scan", Acum_s[:, c * CS:(c + 1) * CS],
             a_s[:, c * CS:(c + 1) * CS], zeros_s[:, :], 0.0, ALU.add, ALU.add)
    v.op("reduce_sum", cumA_s[:, :],
         Acum_s[:, :].rearrange("p (c l) -> p c l", c=NCH)[:, :, CS - 1:CS].squeeze(2),
         axis=AX.X)
    qs.wait(v, v.n)
    qs.dma(cumA_h[:, :], cumA_s[:, :])
    for c in range(NCH):
        v.op("tensor_scalar", dec_s[:, c * CS:(c + 1) * CS],
             Acum_s[:, c * CS:(c + 1) * CS],
             Acum_s[:, c * CS + CS - 1:c * CS + CS], None, ALU.subtract)
    a.wait(v, v.n)
    a.op("activation", dec_s[:, :], dec_s[:, :], AF.Exp, scale=-1.0)
    v.wait(a, a.n)
    v.op("tensor_tensor", ddt_s[:, :], dec_s[:, :], dt_s[:, :], ALU.mult)
    v.op("tensor_copy", act_s[:, :], ddt_s[:, :])
    pe.raw(lambda eng, val=v.n: eng.wait_ge(v.sem, val))
    for c in range(NCH):
        pe.op("transpose", psT, act_s[:, c * CS:(c + 1) * CS], ident_s[:, :])
        pe.mark()
        v.wait(pe, pe.n)
        v.op("tensor_copy", ddtT_s[:, c, :], psT)
        pe.raw(lambda eng, val=v.n: eng.wait_ge(v.sem, val))
    a.op("activation", dec8_s[:, :],
         Acum_s[:, :].rearrange("p (c l) -> p c l", c=NCH)[:, :, CS - 1:CS].squeeze(2),
         AF.Exp)
    qs.wait(a, a.n)
    f1 = qs.dma(decflat_h[0, :].rearrange("(h c) -> h c", h=128), dec8_s[:, :])
    qs.wait(qs, f1)
    f2 = qs.dma(decB_s[:, :], decflat_h[0:1, :].broadcast_to([128, 1024]))
    v.wait(qs, f2)

    for d in range(128, 144):
        NS_ = inproj_tile(d)
        v.wait(pe, pe.n)
        v.op("tensor_copy", hbc_s[:, 0:32], NS_[2][2][:, :])
        v.op("tensor_copy", hbc_s[:, 32:544], NS_[0][2][:, :])
        v.op("tensor_copy", hbc_s[:, 544:1056], NS_[1][2][:, :])
        inproj_done(d)
        cidx = d - 64
        conv_and_silu(d, cidx)
        qs.wait(a, a.n)
        st = qs.dma(bc_h[(d - 128) * 128:(d - 127) * 128, :], act_s[:, 0:T])
        a.wait(qs, st)
        if d < 136:
            pass

    for g in range(8):
        lb = qs.dma(hbc_s[:, 0:T], bc_h[g * 128:(g + 1) * 128, :])
        qs.wait(qs, lb)
        tb = qs.dmat(Btm_s[:, :, :], hbc_s[:, 0:T].rearrange("n (c s) -> n c s", c=NCH))
        pe.raw(lambda eng, val=tb: eng.wait_ge(qs.sem, val))
        for j in range(8):
            d = 64 + g * 8 + j
            NS_ = inproj_tile(d)
            v.wait(pe, pe.n)
            v.op("tensor_copy", hbc_s[:, 0:32], NS_[2][2][:, :])
            v.op("tensor_copy", hbc_s[:, 32:544], NS_[0][2][:, :])
            v.op("tensor_copy", hbc_s[:, 544:1056], NS_[1][2][:, :])
            inproj_done(d)
            conv_and_silu(d, d - 64)
            qs.wait(a, a.n)
            tx = qs.dmat(xtmg_s[:, :, j, :], act_s[:, :].rearrange("r (c t) -> r c t", c=NCH))
            qs.wait(qs, tx)
            st = qs.dma(xtm_h[:, (d - 64) * 128:(d - 63) * 128]
                        .rearrange("(c tl) ch -> tl c ch", c=NCH), xtmg_s[:, :, j, :])
            v.wait(qs, st)
            for c in range(NCH):
                hv = ddtT_s[:, c, g * 16 + j * 2: g * 16 + j * 2 + 2]
                v.op("tensor_tensor", xsl_s[:, c, j, :].rearrange("p (h q) -> p h q", h=2),
                     xtmg_s[:, c, j, :].rearrange("p (h q) -> p h q", h=2),
                     hv.unsqueeze(2).broadcast_to([128, 2, 64]), ALU.mult)
        v.op("memset", Rg_s[:, :], 0.0)
        for c in range(NCH):
            pe.raw(lambda eng, val=v.n: eng.wait_ge(v.sem, val))
            for j in range(8):
                pe.mm(psS[:, j * 128:(j + 1) * 128], Btm_s[:, c, :],
                      xsl_s[:, c, j, :], start=True, stop=True)
            pe.mark()
            qs.wait(v, v.n)
            sr = qs.dma(Rloc_h[c, :, g * 1024:(g + 1) * 1024], Rg_s[:, :])
            v.wait(qs, sr)
            v.wait(pe, pe.n)
            dv = decB_s[:, :].rearrange("p (h c) -> p h c", h=128)[:, g * 16:(g + 1) * 16, c:c + 1]
            v.op("tensor_tensor", Rt_s[:, :].rearrange("p (h q) -> p h q", h=16),
                 Rg_s[:, :].rearrange("p (h q) -> p h q", h=16),
                 dv.broadcast_to([128, 16, 64]), ALU.mult)
            v.op("tensor_tensor", Rg_s[:, :], Rt_s[:, :], psS[:, :], ALU.add)
        qs.wait(v, v.n)
        sz = qs.dma(Z_h[:, g * 1024:(g + 1) * 1024], Rg_s[:, :])
        v.wait(qs, sz)

    for d in range(64):
        NS_ = inproj_tile(d, halo_too=False)
        a.wait(pe, pe.n)
        for ci in range(2):
            a.op("activation", act_s[:, ci * 512:(ci + 1) * 512], NS_[ci][2][:, :], AF.Silu)
        v.wait(a, a.n)
        v.op("tensor_scalar", act_s[:, 0:T], act_s[:, 0:T],
             normw_s[:, d:d + 1], None, ALU.mult)
        qs.wait(v, v.n)
        st = qs.dma(gate_h[d * 128:(d + 1) * 128, :], act_s[:, 0:T])
        v.wait(qs, st)
        a.wait(qs, st)
        inproj_done(d)

    ctx.finish_outputs()
    return ctx




def build_phaseA2a(ctx=None):
    if ctx is None:
        ctx = Ctx("phaseA2a")
    xtm_h = ctx.din("xtm", (T, INTER), BF16)
    gate_h = ctx.din("gate", (INTER, T), BF16)
    bc_h = ctx.din("bc", (2 * G * N, T), BF16)
    dt_h = ctx.din("dt", (128, T))
    alog = ctx.din("A_log", (128, 1))
    Rloc_h = ctx.din("Rloc", (NCH, 128, INTER))
    Zall_h = ctx.din("Z_all", (8, 128, INTER))
    cumAall = ctx.din("cumA_all", (128, 8))
    maskR = ctx.din("carrymaskR", (128, 8))
    D_B = ctx.din("D_B", (128, 128))
    tri = ctx.din("tri", (128, 128), BF16)
    ident = ctx.din("ident", (128, 128), BF16)
    identf = ctx.din("ident_f", (128, 128))
    z_h = ctx.dout("z", (INTER, T), BF16)
    acumflat_h = ctx.dram("acumflat", (NCH, 128 * 128))
    wjflat_h = ctx.dram("wjflat", (1, 1024))
    eprflat_h = ctx.dram("eprflat", (1, 1024))

    dt_s = ctx.sb("dt_s", [128, T], F32)
    a_s = ctx.sb("a_s", [128, T], F32)
    Acum_s = ctx.sb("Acum_s", [128, T], F32)
    zeros_s = ctx.sb("zeros_s", [128, CS], F32)
    alog_s = ctx.sb("alog_s", [128, 1], F32)
    A_col = ctx.sb("A_col", [128, 1], F32)
    AcumT_s = ctx.sb("AcumT_s", [128, NCH, 128], F32)
    exppT_s = ctx.sb("exppT_s", [128, NCH, 128], F32)
    dtT_s = ctx.sb("dtT_s", [128, NCH, 128], F32)
    dtb16 = ctx.sb("dtb16", [128, T], BF16)
    B_s = ctx.sb("B_s", [128, G, T], BF16)
    C_s = ctx.sb("C_s", [128, G, T], BF16)
    tri_s = ctx.sb("tri_s", [128, 128], BF16)
    ident_s = ctx.sb("ident_s", [128, 128], BF16)
    identf_s = ctx.sb("identf_s", [128, 128], F32)
    DB_s = ctx.sb("DB_s", [128, 128], F32)
    maskR_s = ctx.sb("maskR_s", [128, 8], F32)
    cumA_s = ctx.sb("cumA_s", [128, 8], F32)
    Tall_s = ctx.sb("Tall_s", [128, 8], F32)
    Tsel_s = ctx.sb("Tsel_s", [128, 1], F32)
    wj_s = ctx.sb("wj_s", [128, 8], F32)
    zer8 = ctx.sb("zer8", [128, 8], F32)
    wjB_s = ctx.sb("wjB_s", [128, 1024], F32)
    EprB_s = ctx.sb("EprB_s", [128, 1024], F32)
    Epr_s = ctx.sb("Epr_s", [128, 8], F32)
    carry_all = ctx.sb("carry_all", [128, G, 1024], BF16)
    carry_f = ctx.sb("carry_f", [128, 1024], F32)
    zj_s = ctx.sb("zj_s", [128, 1024], F32)
    AROW_s = ctx.sb("AROW_s", [128, 2048], F32)
    Dsl_s = ctx.sb("Dsl_s", [128, 2048], F32)
    Lsl_s = ctx.sb("Lsl_s", [128, 2048], BF16)
    gm_s = ctx.sb("gm_s", [128, 128], BF16)
    M_s = ctx.sb("M_s", [128, 2048], BF16)
    xtm_c = ctx.sb("xtm_c", [128, 1024], BF16)
    xdt_s = ctx.sb("xdt_s", [128, 1024], BF16)
    Rloc_b = ctx.sb("Rloc_b", [128, 1024], F32)
    Rf_s = ctx.sb("Rf_s", [128, 1024], F32)
    rfb_s = ctx.sb("rfb_s", [128, 1024], BF16)
    yt1_s = ctx.sb("yt1_s", [128, 1024], F32)
    y1_s = ctx.sb("y1_s", [128, 1024], F32)
    ycs = ctx.sb("ycs", [128, G, 1024], BF16)
    sqsc = None
    ssqc = ctx.sb("ssqc", [128, 8], F32)
    ssq = ctx.sb("ssq", [128, 1], F32)
    rstd = ctx.sb("rstd", [128, 1], F32)
    ytn_g = ctx.sb("ytn_g", [128, 1024], BF16)
    yfm_s = ctx.sb("yfm_s", [128, 64, 128], BF16)
    ggc_s = ctx.sb("ggc_s", [128, 64, 128], BF16)
    zfm_s = ctx.sb("zfm_s", [128, 64, 128], BF16)

    psT = ctx.ps("psT", [128, 128], F32)
    psG = ctx.ps("psG", [128, 128], F32)
    psY = ctx.ps("psY", [128, 1024], F32)
    psO = ctx.ps("psO", [128, 1024], F32)

    pe, v, a, qs, qg = ctx.pe, ctx.v, ctx.a, ctx.qs, ctx.qg

    qg.dma(tri_s[:, :], tri[:, :])
    qg.dma(ident_s[:, :], ident[:, :])
    qg.dma(identf_s[:, :], identf[:, :])
    qg.dma(DB_s[:, :], D_B[:, :])
    qg.dma(maskR_s[:, :], maskR[:, :])
    qg.dma(cumA_s[:, :], cumAall[:, :])
    qg.dma(alog_s[:, :], alog[:, :])
    qg.dma(B_s[:, :, :], bc_h[0:G * N, :].rearrange("(g n) t -> n g t", n=128))
    lBC = qg.dma(C_s[:, :, :], bc_h[G * N:2 * G * N, :].rearrange("(g n) t -> n g t", n=128))
    ldt = qs.dma(dt_s[:, :], dt_h[:, :])

    a.wait(qg, lBC)
    a.op("activation", A_col[:, :], alog_s[:, :], AF.Exp)
    v.wait(qs, ldt)
    v.wait(a, a.n)
    v.op("memset", zeros_s[:, :], 0.0)
    v.op("memset", zer8[:, :], 0.0)
    v.op("tensor_scalar", A_col[:, :], A_col[:, :], -1.0, None, ALU.mult)
    v.op("tensor_scalar", a_s[:, :], dt_s[:, :], A_col[:, 0:1], None, ALU.mult)
    for c in range(NCH):
        v.op("tensor_tensor_scan", Acum_s[:, c * CS:(c + 1) * CS],
             a_s[:, c * CS:(c + 1) * CS], zeros_s[:, :], 0.0, ALU.add, ALU.add)
    v.op("tensor_copy", dtb16[:, :], dt_s[:, :])
    pe.raw(lambda eng, val=v.n: eng.wait_ge(v.sem, val))
    for c in range(NCH):
        pe.op("transpose", psT[:, :], Acum_s[:, c * CS:(c + 1) * CS], identf_s[:, :])
        pe.mark()
        v.wait(pe, pe.n)
        v.op("tensor_copy", AcumT_s[:, c, :], psT[:, :])
        pe.raw(lambda eng, val=v.n: eng.wait_ge(v.sem, val))
        pe.op("transpose", psT[:, :], dt_s[:, c * CS:(c + 1) * CS], identf_s[:, :])
        pe.mark()
        v.wait(pe, pe.n)
        v.op("tensor_copy", dtT_s[:, c, :], psT[:, :])
        pe.raw(lambda eng, val=v.n: eng.wait_ge(v.sem, val))
    a.wait(v, v.n)
    for c in range(NCH):
        a.op("activation", exppT_s[:, c, :], AcumT_s[:, c, :], AF.Exp)
    qs.wait(v, v.n)
    fl = 0
    for c in range(NCH):
        fl = qs.dma(acumflat_h[c, :].rearrange("(h l) -> h l", h=128),
                    Acum_s[:, c * CS:(c + 1) * CS])
    qg.wait(qs, fl)

    v.op("tensor_tensor_scan", Tall_s[:, :], cumA_s[:, :], zer8[:, :], 0.0,
         ALU.add, ALU.add)
    v.op("tensor_tensor", wj_s[:, :], cumA_s[:, :], maskR_s[:, :], ALU.mult)
    v.op("reduce_sum", Tsel_s[:, :], wj_s[:, :], axis=AX.X)
    v.op("tensor_scalar", wj_s[:, :], Tall_s[:, :], Tsel_s[:, 0:1], None, ALU.subtract)
    v.op("tensor_scalar", wj_s[:, :], wj_s[:, :], 0.0, None, ALU.max)
    a.wait(v, v.n)
    a.op("activation", wj_s[:, :], wj_s[:, :], AF.Exp, scale=-1.0)
    v.wait(a, a.n)
    v.op("tensor_tensor", wj_s[:, :], wj_s[:, :], maskR_s[:, :], ALU.mult)
    qs.wait(v, v.n)
    f1 = qs.dma(wjflat_h[0, :].rearrange("(h j) -> h j", h=128), wj_s[:, :])
    qs.wait(qs, f1)
    f2 = qs.dma(wjB_s[:, :], wjflat_h[0:1, :].broadcast_to([128, 1024]))
    totv = Acum_s[:, :].rearrange("p (c l) -> p c l", c=NCH)[:, :, CS - 1:CS].squeeze(2)
    v.op("tensor_tensor_scan", Epr_s[:, :], totv, zer8[:, :], 0.0, ALU.add, ALU.add)
    a.wait(v, v.n)
    a.op("activation", Epr_s[:, 1:8], Epr_s[:, 0:7], AF.Exp)
    v.wait(a, a.n)
    v.op("memset", Epr_s[:, 0:1], 1.0)
    qs.wait(v, v.n)
    f3 = qs.dma(eprflat_h[0, :].rearrange("(h c) -> h c", h=128), Epr_s[:, :])
    qs.wait(qs, f3)
    f4 = qs.dma(EprB_s[:, :], eprflat_h[0:1, :].broadcast_to([128, 1024]))
    v.wait(qs, f4)

    for g in range(G):
        v.op("memset", carry_f[:, :], 0.0)
        for j in range(8):
            lz = qs.dma(zj_s[:, :], Zall_h[j, :, g * 1024:(g + 1) * 1024])
            v.wait(qs, lz)
            wv_ = wjB_s[:, :].rearrange("p (h j) -> p h j", h=128)[:, g * 16:(g + 1) * 16, j:j + 1]
            v.op("tensor_tensor", zj_s[:, :].rearrange("p (h q) -> p h q", h=16),
                 zj_s[:, :].rearrange("p (h q) -> p h q", h=16),
                 wv_.broadcast_to([128, 16, 64]), ALU.mult)
            v.op("tensor_tensor", carry_f[:, :], carry_f[:, :], zj_s[:, :], ALU.add)
            qs.wait(v, v.n)
        v.op("tensor_copy", carry_all[:, g, :], carry_f[:, :])

    for c in range(NCH):
        for g in range(G):
            qs.wait(v, v.n)
            qg.wait(v, v.n)
            lx = qs.dma(xtm_c[:, :], xtm_h[c * 128:(c + 1) * 128, g * 1024:(g + 1) * 1024])
            lr = qs.dma(Rloc_b[:, :], Rloc_h[c, :, g * 1024:(g + 1) * 1024])
            la = qg.dma(AROW_s[:, :],
                        acumflat_h[c:c + 1, g * 2048:(g + 1) * 2048].broadcast_to([128, 2048]))
            v.wait(qg, la)
            acv = AcumT_s[:, c, g * 16:(g + 1) * 16].unsqueeze(2).broadcast_to([128, 16, 128])
            v.op("tensor_tensor", Dsl_s[:, :].rearrange("p (h l) -> p h l", h=16),
                 AROW_s[:, :].rearrange("p (h l) -> p h l", h=16), acv, ALU.subtract)
            v.op("tensor_scalar", Dsl_s[:, :], Dsl_s[:, :], 0.0, None, ALU.min)
            a.wait(v, v.n)
            a.op("activation", Lsl_s[:, :], Dsl_s[:, :], AF.Exp)
            pe.raw(lambda eng, val=v.n: eng.wait_ge(v.sem, val))
            pe.mm(psG[:, :], B_s[:, g, c * CS:(c + 1) * CS],
                  C_s[:, g, c * CS:(c + 1) * CS], start=True, stop=True)
            pe.mark()
            v.wait(pe, pe.n)
            v.op("tensor_tensor", gm_s[:, :], psG[:, :], tri_s[:, :], ALU.mult)
            v.wait(a, a.n)
            gmv = gm_s[:, :].unsqueeze(1).broadcast_to([128, 16, 128])
            v.op("tensor_tensor", M_s[:, :].rearrange("p (h l) -> p h l", h=16),
                 gmv, Lsl_s[:, :].rearrange("p (h l) -> p h l", h=16), ALU.mult)
            v.wait(qs, lx)
            dtv = dtT_s[:, c, g * 16:(g + 1) * 16].unsqueeze(2).broadcast_to([128, 16, 64])
            v.op("tensor_tensor", xdt_s[:, :].rearrange("p (h q) -> p h q", h=16),
                 xtm_c[:, :].rearrange("p (h q) -> p h q", h=16), dtv, ALU.mult)
            v.wait(qs, lr)
            epv = EprB_s[:, :].rearrange("p (h cc) -> p h cc", h=128)[:, g * 16:(g + 1) * 16, c:c + 1]
            v.op("tensor_tensor", Rf_s[:, :].rearrange("p (h q) -> p h q", h=16),
                 carry_all[:, g, :].rearrange("p (h q) -> p h q", h=16),
                 epv.broadcast_to([128, 16, 64]), ALU.mult)
            v.op("tensor_tensor", Rf_s[:, :], Rf_s[:, :], Rloc_b[:, :], ALU.add)
            v.op("tensor_copy", rfb_s[:, :], Rf_s[:, :])
            pe.raw(lambda eng, val=v.n: eng.wait_ge(v.sem, val))
            for hl in range(16):
                pe.mm(psY[:, hl * 64:(hl + 1) * 64], M_s[:, hl * 128:(hl + 1) * 128],
                      xdt_s[:, hl * 64:(hl + 1) * 64], start=True, stop=True)
            for hl in range(16):
                pe.mm(psO[:, hl * 64:(hl + 1) * 64], C_s[:, g, c * CS:(c + 1) * CS],
                      rfb_s[:, hl * 64:(hl + 1) * 64], start=True, stop=True)
            pe.mark()
            v.wait(pe, pe.n)
            epx = exppT_s[:, c, g * 16:(g + 1) * 16].unsqueeze(2).broadcast_to([128, 16, 64])
            v.op("tensor_tensor", yt1_s[:, :].rearrange("p (h q) -> p h q", h=16),
                 psO[:, :].rearrange("p (h q) -> p h q", h=16), epx, ALU.mult)
            v.op("tensor_tensor", y1_s[:, :], yt1_s[:, :], psY[:, :], ALU.add)
            for hl in range(16):
                hglob = g * 16 + hl
                v.op("scalar_tensor_tensor", ycs[:, g, hl * 64:(hl + 1) * 64],
                     xtm_c[:, hl * 64:(hl + 1) * 64], DB_s[:, hglob:hglob + 1],
                     y1_s[:, hl * 64:(hl + 1) * 64], ALU.mult, ALU.add)
            pe.raw(lambda eng, val=v.n: eng.wait_ge(v.sem, val))
        a.wait(v, v.n)
        for g in range(G):
            a.op("activation", yt1_s[:, :], ycs[:, g, :], AF.Square,
                 accum_out=ssqc[:, g:g + 1])
        v.wait(a, a.n)
        v.op("reduce_sum", ssq[:, :], ssqc[:, :], axis=AX.X)
        v.op("tensor_scalar", ssq[:, :], ssq[:, :], 1.0 / INTER, EPS, ALU.mult, ALU.add)
        a.wait(v, v.n)
        a.op("activation", rstd[:, :], ssq[:, :], AF.Sqrt)
        v.wait(a, a.n)
        v.op("reciprocal", rstd[:, :], rstd[:, :])
        qs.wait(v, v.n)
        lg_ = qs.dma(ggc_s[:, :, :],
                     gate_h[:, c * 128:(c + 1) * 128].rearrange("(a r) t -> r a t", r=128))
        tx = 0
        for g in range(G):
            v.op("tensor_scalar", ytn_g[:, :], ycs[:, g, :], rstd[:, 0:1], None, ALU.mult)
            qs.wait(v, v.n)
            tx = qs.dmat(yfm_s[:, g * 8:(g + 1) * 8, :],
                         ytn_g[:, :].rearrange("t (a r) -> t a r", a=8))
            v.wait(qs, tx)
        v.wait(qs, lg_)
        v.op("tensor_tensor", zfm_s[:, :, :], yfm_s[:, :, :], ggc_s[:, :, :], ALU.mult)
        qs.wait(v, v.n)
        sz = qs.dma(z_h[:, c * 128:(c + 1) * 128].rearrange("(a r) t -> r a t", r=128),
                    zfm_s[:, :, :])
        v.wait(qs, sz)

    ctx.finish_outputs()
    return ctx


def build_phaseA2b(ctx=None):
    if ctx is None:
        ctx = Ctx("phaseA2b")
    x_in = ctx.din("x_in", (HT, DIM))
    z_h = ctx.din("z", (INTER, T), BF16)
    WT = ctx.din("W_outT", (INTER, DIM))
    x1 = ctx.dout("x1", (T, DIM))
    wo_h = ctx.dram("wo_h", (DIM, T), BF16)

    zT_s = ctx.sb("zT_s", [128, 64, 512], BF16)
    wb = [ctx.sb("wb0", [128, 64, 128], BF16), ctx.sb("wb1", [128, 64, 128], BF16)]
    stage = ctx.sb("stage", [128, 512], BF16)
    ln = LnKit(ctx, "a2")
    psA = ctx.ps("psA", [128, 512], F32)
    psB = ctx.ps("psB", [128, 512], F32)

    pe, v, a, qs, qg = ctx.pe, ctx.v, ctx.a, ctx.qs, ctx.qg

    marks = {}
    vdone = {}
    for th in range(2):
        lz = qs.dma(zT_s[:, :, :],
                    z_h[:, th * 512:(th + 1) * 512].rearrange("(a k) t -> k a t", k=128))
        qs.wait(pe, pe.n)
        pe.raw(lambda eng, val=lz: eng.wait_ge(qs.sem, val))
        for d in range(32):
            key = (th, d)
            if d >= 2:
                qg.wait(pe, marks[(th, d - 2)])
            ld = qg.dma(wb[d % 2][:, :, :],
                        WT[:, d * 128:(d + 1) * 128].rearrange("(a k) m -> k a m", k=128))
            pe.raw(lambda eng, val=ld: eng.wait_ge(qg.sem, val))
            if d >= 2:
                pe.raw(lambda eng, val=vdone[(th, d - 2)]: eng.wait_ge(v.sem, val))
            ps = psA if d % 2 == 0 else psB
            for k in range(64):
                pe.mm(ps[:, :], wb[d % 2][:, k, :], zT_s[:, k, :],
                      start=(k == 0), stop=(k == 63))
            pe.mark()
            marks[key] = pe.n
            v.wait(pe, pe.n)
            v.op("tensor_copy", stage[:, :], ps[:, :])
            qs.wait(v, v.n)
            st = qs.dma(wo_h[d * 128:(d + 1) * 128, th * 512:(th + 1) * 512], stage[:, :])
            v.wait(qs, st)
            vdone[key] = v.n

    emit_post_ln_residual(ctx, ln, wo_h, x_in[HALO:HALO + T, :], x1)
    ctx.finish_outputs()
    return ctx




def build_mid1():
    ctx = Ctx("mid1", final_outputs={"x1"})
    m = ctx.phase_mark()
    build_phaseA2a(ctx)
    ctx.phase_free(m)
    build_phaseA2b(ctx)
    return ctx


def build_mid2():
    ctx = Ctx("mid2", final_outputs={"x2", "q_tm", "k_tm", "v_tm"})
    m = ctx.phase_mark()
    build_phaseB(ctx)
    ctx.phase_free(m)
    build_phaseC1(ctx)
    return ctx


def build_tail():
    ctx = Ctx("tail", final_outputs={"x4"})
    m = ctx.phase_mark()
    build_phaseC2(ctx)
    ctx.phase_free(m)
    build_phaseD(ctx)
    return ctx



_BUILT = {}


def _get(name):
    if name not in _BUILT:
        fn = globals().get("build_phase" + name) or globals()["build_" + name]
        ctx = fn()
        nc = ctx.build()
        _BUILT[name] = (ctx, nc)
    return _BUILT[name]


def kernel(**inputs):
    com = prep_common(inputs)
    cores = [prep_core(c, inputs) for c in range(NCORE)]
    live = [dict() for _ in range(NCORE)]

    def resolve(ctx, c):
        m = {}
        for n in ctx.input_names:
            if n in live[c]:
                m[n] = live[c][n]
            elif n == "kprev_tm":
                m[n] = live[c - 1]["k_tm"] if c > 0 else np.zeros((T, DIM), BF)
            elif n == "vprev_tm":
                m[n] = live[c - 1]["v_tm"] if c > 0 else np.zeros((T, DIM), BF)
            elif n == "Z_all":
                m[n] = np.ascontiguousarray(np.stack([live[j]["Z"] for j in range(NCORE)], 0))
            elif n == "cumA_all":
                m[n] = np.ascontiguousarray(
                    np.concatenate([live[j]["cumA"] for j in range(NCORE)], 1))
            elif n in com:
                m[n] = com[n]
            else:
                m[n] = cores[c][n]
        return m

    def launch(name, renames=None):
        ctx, nc = _get(name)
        in_maps = [resolve(ctx, c) for c in range(NCORE)]
        res = run_bass_kernel_spmd(nc, in_maps, core_ids=list(range(NCORE)))
        for c in range(NCORE):
            for on in ctx.output_names:
                key = renames.get(on, on) if renames else on
                live[c][key] = np.asarray(res.results[c][on])
        return res

    launch("A1")
    launch("mid1")
    launch("mid2")
    launch("tail")
    out = np.concatenate([live[c]["x4"] for c in range(NCORE)], 0)
    return out.reshape(1, 8192, DIM).astype(np.float32)


# version 04
# speedup vs baseline: 1.1102x; 1.0249x over previous
import numpy as np
import ml_dtypes
import concourse.bass as bass
import concourse.mybir as mybir
from concourse.bass_utils import run_bass_kernel_spmd

F32 = mybir.dt.float32
BF16 = mybir.dt.bfloat16
AF = mybir.ActivationFunctionType
ALU = mybir.AluOpType
AX = mybir.AxisListType

NCORE = 8
T = 1024
HT = 1056
HALO = 32
DIM = 4096
H, P, G, N, CS, KCONV = 128, 64, 8, 128, 128, 4
INTER = 8192
CONV_DIM = INTER + 2 * G * N
NPROJ = INTER + CONV_DIM + H
NCH = T // CS
EPS = 1e-5
NH, HD = 32, 128
WINDOW = 1024
BF = ml_dtypes.bfloat16



class EngT:

    def __init__(self, name):
        self.name = name
        self.n = 0
        self.sem = None
        self.cmds = []

    def op(self, method, *args, **kwargs):
        prev = self.n
        self.n += 1

        def fn(eng, prev=prev, method=method, args=args, kwargs=kwargs):
            if prev > 0:
                eng.wait_ge(self.sem, prev)
            getattr(eng, method)(*args, **kwargs).then_inc(self.sem, 1)
        self.cmds.append(fn)
        return self.n

    def wait(self, other, val):
        if val <= 0:
            return

        def fn(eng, other=other, val=val):
            eng.wait_ge(other.sem, val)
        self.cmds.append(fn)

    def raw(self, fn):
        self.cmds.append(fn)


class PeT(EngT):

    def op(self, method, *args, mark=False, **kwargs):
        if mark:
            self.n += 1

            def fn(eng, method=method, args=args, kwargs=kwargs):
                getattr(eng, method)(*args, **kwargs).then_inc(self.sem, 1)
        else:
            def fn(eng, method=method, args=args, kwargs=kwargs):
                getattr(eng, method)(*args, **kwargs)
        self.cmds.append(fn)
        return self.n

    def mm(self, out, lhsT, rhs, start, stop, mark=False):
        return self.op("matmul", out, lhsT, rhs, start=start, stop=stop, mark=mark)

    def mark(self):
        self.n += 1

        def fn(eng):
            eng.drain().then_inc(self.sem, 1)
        self.cmds.append(fn)
        return self.n


class QT:

    def __init__(self, name):
        self.name = name
        self.n = 0
        self.sem = None
        self.cmds = []

    def dma(self, out, in_, **kw):
        self.n += 16

        def fn(eng, out=out, in_=in_, kw=kw):
            eng.dma_start(out=out, in_=in_, **kw).then_inc(self.sem, 16)
        self.cmds.append(fn)
        return self.n

    def dmat(self, out, in_):
        self.n += 16

        def fn(eng, out=out, in_=in_):
            eng.dma_start_transpose(out, in_).then_inc(self.sem, 16)
        self.cmds.append(fn)
        return self.n

    def wait(self, other, val):
        if val <= 0:
            return

        def fn(eng, other=other, val=val):
            eng.wait_ge(other.sem, val)
        self.cmds.append(fn)

    def raw(self, fn):
        self.cmds.append(fn)


class Ctx:
    def __init__(self, name, final_outputs=()):
        self.name = name
        self.nc = bass.Bass("TRN2", target_bir_lowering=False)
        self.pe = PeT("pe")
        self.v = EngT("v")
        self.a = EngT("a")
        self.qs = QT("qs")
        self.qg = QT("qg")
        self.qa = QT("qa")
        self._guards = []
        self.input_names = []
        self.output_names = []
        self._live_dram = {}
        self._final_outputs = set(final_outputs)
        self._uniq = 0

    def din(self, name, shape, dtype=F32):
        if name in self._live_dram:
            return self._live_dram[name]
        t = self.nc.dram_tensor(name, list(shape), dtype, kind="ExternalInput")
        self.input_names.append(name)
        ap = t.ap()
        self._live_dram[name] = ap
        return ap

    def dout(self, name, shape, dtype=F32):
        if self._final_outputs and name not in self._final_outputs:
            t = self.nc.dram_tensor(name, list(shape), dtype, kind="Internal")
        else:
            t = self.nc.dram_tensor(name, list(shape), dtype, kind="ExternalOutput")
            self.output_names.append(name)
        ap = t.ap()
        self._live_dram[name] = ap
        return ap

    def phase_mark(self):
        return len(self._guards)

    def phase_free(self, mark):
        for g in reversed(self._guards[mark:]):
            g.__exit__(None, None, None)
        del self._guards[mark:]
        streams = [self.pe, self.v, self.a, self.qs, self.qg, self.qa]
        for wtr in streams:
            for p in streams:
                if p is not wtr:
                    wtr.wait(p, p.n)

    def dram(self, name, shape, dtype=F32):
        self._uniq += 1
        name = f"{name}_{self._uniq}"
        return self.nc.dram_tensor(name, list(shape), dtype, kind="Internal").ap()

    def sb(self, name, shape, dtype):
        self._uniq += 1
        name = f"{name}_{self._uniq}"
        g = self.nc.sbuf_tensor(name, list(shape), dtype)
        t = g.__enter__()
        self._guards.append(g)
        return t

    def ps(self, name, shape, dtype=F32):
        self._uniq += 1
        name = f"{name}_{self._uniq}"
        g = self.nc.psum_tensor(name, list(shape), dtype)
        t = g.__enter__()
        self._guards.append(g)
        return t

    def barrier(self, waiter, producers=None):
        if producers is None:
            producers = [e for e in (self.pe, self.v, self.a, self.qs, self.qg, self.qa)
                         if e is not waiter]
        for p in producers:
            waiter.wait(p, p.n)

    def finish_outputs(self):
        self.qs.wait(self.qs, self.qs.n)
        self.qg.wait(self.qg, self.qg.n)
        self.qa.wait(self.qa, self.qa.n)

    def build(self):
        nc = self.nc
        streams = [self.pe, self.v, self.a, self.qs, self.qg, self.qa]
        semg = []
        for s in streams:
            g = nc.semaphore("sem_" + s.name)
            s.sem = g.__enter__()
            semg.append(g)
        blockg = nc.Block()
        block = blockg.__enter__()

        def run(dec, obj):
            @dec
            def _(eng):
                for fn in obj.cmds:
                    fn(eng)
        run(block.tensor, self.pe)
        run(block.vector, self.v)
        run(block.scalar, self.a)
        run(block.sync, self.qs)
        run(block.gpsimd, self.qg)
        run(block.scalar, self.qa)
        blockg.__exit__(None, None, None)
        for g in semg:
            g.__exit__(None, None, None)
        for g in reversed(self._guards):
            g.__exit__(None, None, None)
        return nc


def cast_np(a, dtype=np.float32):
    return np.ascontiguousarray(np.asarray(a), dtype=dtype)



def prep_common(w):
    c = {}
    c["W_inT"] = cast_np(w["L0_attn_in_proj_w"].T)
    c["conv_wR"] = cast_np(w["L0_attn_conv_w"][:, 0, :].reshape(80, 128, KCONV).transpose(1, 0, 2))
    c["conv_bR"] = cast_np(w["L0_attn_conv_b"].reshape(80, 128).T)
    c["dt_bias"] = cast_np(w["L0_attn_dt_bias"].reshape(128, 1))
    c["A_log"] = cast_np(w["L0_attn_A_log"].reshape(128, 1))
    c["Dvec"] = cast_np(w["L0_attn_D"].reshape(128, 1))
    c["norm_wR"] = cast_np(w["L0_attn_norm_w"].reshape(64, 128).T)
    c["W_outT"] = cast_np(w["L0_attn_out_proj_w"].T)
    c["F0_WT"] = cast_np(w["L0_ffn_W"].T)
    c["F0_biasR"] = cast_np(w["L0_ffn_bias"].reshape(32, 128).T)
    c["F0_AcatT"] = cast_np(w["L0_ffn_A"].reshape(256, DIM).T)
    c["F0_Bcat"] = cast_np(w["L0_ffn_Bf"].transpose(0, 2, 1).reshape(256, DIM))
    c["F0_coefR"] = cast_np(np.repeat(w["L0_ffn_coeffs"], 16).reshape(2, 128).T)
    c["F0_scalR"] = cast_np(np.repeat(w["L0_ffn_scales"], 16).reshape(2, 128).T)
    c["WqT"] = cast_np(w["L1_attn_Wq"].T)
    c["WkT"] = cast_np(w["L1_attn_Wk"].T)
    c["WvT"] = cast_np(w["L1_attn_Wv"].T)
    c["WoT"] = cast_np(w["L1_attn_Wo"].T)
    c["F1_WT"] = cast_np(w["L1_ffn_W_base"].T)
    c["F1_biasR"] = cast_np(w["L1_ffn_b_base"].reshape(32, 128).T)
    c["F1_WrT"] = cast_np(w["L1_ffn_W_router"].T)
    c["F1_AcatT"] = cast_np(w["L1_ffn_A"].reshape(128, DIM).T)
    c["F1_Bcat"] = cast_np(w["L1_ffn_B"].transpose(0, 2, 1).reshape(128, DIM))
    E8 = np.zeros((8, 128), np.float32)
    for e in range(8):
        E8[e, e * 16:(e + 1) * 16] = 1.0
    c["E8"] = cast_np(E8, BF)
    c["D_B"] = cast_np(np.broadcast_to(w["L0_attn_D"].reshape(1, 128), (128, 128)))
    c["ident_f"] = cast_np(np.eye(128))
    c["onescol"] = cast_np(np.ones((128, 1)), BF)
    c["onesrow"] = cast_np(np.ones((1, 128)), BF)
    c["ident"] = cast_np(np.eye(128), BF)
    c["tri"] = cast_np(np.triu(np.ones((128, 128))), BF)
    return c


def prep_core(ci, w):
    x = w["L0_attn_hidden_states"][0]
    d = {}
    t0 = ci * T
    halo = np.zeros((HALO, DIM), np.float32) if ci == 0 else x[t0 - HALO:t0]
    d["x_in"] = cast_np(np.concatenate([halo, x[t0:t0 + T]], 0))
    cm = np.zeros((1, 8), np.float32)
    cm[0, :ci] = 1.0
    d["carrymask"] = cm
    d["carrymaskR"] = cast_np(np.broadcast_to(cm, (128, 8)))
    pos = np.arange(t0, t0 + T, dtype=np.float64)
    inv = 1.0 / 10000.0 ** (np.arange(0, HD, 2) / HD)
    fr = pos[:, None] * inv[None, :]
    emb = np.concatenate([fr, fr], -1)
    cos = np.cos(emb)
    sin = np.sin(emb)
    sin_signed = np.concatenate([-sin[:, :HD // 2], sin[:, HD // 2:]], -1)
    d["rope_cos"] = cast_np(cos.reshape(8, 128, HD).transpose(1, 0, 2))
    d["rope_sin"] = cast_np(sin_signed.reshape(8, 128, HD).transpose(1, 0, 2))
    m = np.zeros((2, 12, 128, 512), np.float32)
    for qg_ in range(2):
        q0 = qg_ * 512
        for kb in range(12):
            kstart = q0 - 1024 + kb * 128
            kidx = np.arange(128)[:, None] + kstart
            qidx = np.arange(512)[None, :] + q0
            rel = qidx - kidx
            ok = (rel >= 0) & (rel < WINDOW)
            if ci == 0:
                ok = ok & (kidx >= 0)
            m[qg_, kb] = ok
    d["wmask"] = cast_np(m, BF)
    return d



class LnKit:

    def __init__(self, ctx, tag="", slim=False):
        self.ctx = ctx
        self.x = ctx.sb(f"ln{tag}_x", [128, DIM], F32)
        self.sq = ctx.sb(f"ln{tag}_sq", [128, DIM], F32)
        self.u = None if slim else ctx.sb(f"ln{tag}_u", [128, DIM], BF16)
        self.s1 = ctx.sb(f"ln{tag}_s1", [128, 1], F32)
        self.s2 = ctx.sb(f"ln{tag}_s2", [128, 1], F32)
        self.mneg = ctx.sb(f"ln{tag}_m", [128, 1], F32)
        self.msq = ctx.sb(f"ln{tag}_m2", [128, 1], F32)
        self.var = ctx.sb(f"ln{tag}_v", [128, 1], F32)
        self.rstd = ctx.sb(f"ln{tag}_r", [128, 1], F32)
        self.last_tp = 0

    def stats_and_u(self, xt, rows, ubuf):
        ctx = self.ctx
        v, a = ctx.v, ctx.a
        v.op("reduce_sum", self.s1[0:rows, :], xt[0:rows, :], axis=AX.X)
        a.wait(v, v.n - 1)
        a.op("activation", self.sq[0:rows, :], xt[0:rows, :], AF.Square,
             accum_out=self.s2[0:rows, 0:1])
        v.op("tensor_scalar", self.mneg[0:rows, :], self.s1[0:rows, :],
             -1.0 / DIM, None, ALU.mult)
        v.op("tensor_tensor", self.msq[0:rows, :], self.mneg[0:rows, :],
             self.mneg[0:rows, :], ALU.mult)
        v.wait(a, a.n)
        v.op("tensor_scalar", self.var[0:rows, :], self.s2[0:rows, :],
             1.0 / DIM, EPS, ALU.mult, ALU.add)
        v.op("tensor_tensor", self.var[0:rows, :], self.var[0:rows, :],
             self.msq[0:rows, :], ALU.subtract)
        a.wait(v, v.n)
        a.op("activation", self.rstd[0:rows, :], self.var[0:rows, :], AF.Sqrt)
        v.wait(a, a.n)
        v.op("reciprocal", self.rstd[0:rows, :], self.rstd[0:rows, :])
        v.op("tensor_scalar", ubuf[0:rows, :], xt[0:rows, :],
             self.mneg[0:rows, 0:1], self.rstd[0:rows, 0:1], ALU.add, ALU.mult)
        return v.n

    def tile_from_hbm(self, x_h, row0, rows, uT=None, ucol=None, u_keep=None):
        ctx = self.ctx
        xt, ub = self.x, self.u
        ctx.qs.wait(ctx.v, ctx.v.n)
        ctx.qs.wait(ctx.a, ctx.a.n)
        ld = ctx.qs.dma(xt[0:rows, :], x_h[row0:row0 + rows, :])
        ctx.v.wait(ctx.qs, ld)
        ctx.a.wait(ctx.qs, ld)
        if self.last_tp:
            ctx.v.wait(ctx.qs, self.last_tp)
        done = self.stats_and_u(xt, rows, ub)
        if u_keep is not None:
            ctx.v.op("tensor_copy", u_keep[0:rows, :], ub[0:rows, :])
        if uT is not None:
            ctx.qs.wait(ctx.v, done)
            tp = ctx.qs.dmat(uT[:, :, ucol:ucol + rows],
                             ub[0:rows, :].rearrange("t (a r) -> t a r", a=32))
            self.last_tp = tp
        return xt


def emit_weight_gemm(ctx, wT_h, uT, n_k, do_tiles, n_cols, evac, wname,
                     extra_mms=None, pre_tile=None):
    pe, qg, v = ctx.pe, ctx.qg, ctx.v
    wb = [ctx.sb(f"{wname}0", [128, n_k, 128], BF16),
          ctx.sb(f"{wname}1", [128, n_k, 128], BF16)]
    psl = [ctx.ps(f"{wname}_psA", [128, 512], F32),
           ctx.ps(f"{wname}_psB", [128, 512], F32),
           ctx.ps(f"{wname}_psC", [128, 512], F32),
           ctx.ps(f"{wname}_psD", [128, 512], F32)]
    nsplit = [(i, min(512, n_cols - i)) for i in range(0, n_cols, 512)]
    assert len(nsplit) <= 2
    pe_done_at = {}
    v_done_at = {}
    for d in range(do_tiles):
        b = wb[d % 2]
        if d >= 2:
            qg.wait(pe, pe_done_at[d - 2])
        ld = qg.dma(b[:, :, :],
                    wT_h[:, d * 128:(d + 1) * 128].rearrange("(a k) m -> k a m", k=128))
        if pre_tile is not None:
            ld = pre_tile(d) or ld
        pe.raw(lambda eng, ld=ld: eng.wait_ge(qg.sem, ld))
        if d >= 2:
            pe.raw(lambda eng, val=v_done_at[d - 2]: eng.wait_ge(v.sem, val))
        pss = [psl[(2 * d) % 4], psl[(2 * d + 1) % 4]]
        for ci, (c0, cw) in enumerate(nsplit):
            ps = pss[ci]
            for k in range(n_k):
                pe.mm(ps[:, 0:cw], b[:, k, :], uT[:, k, c0:c0 + cw],
                      start=(k == 0), stop=(extra_mms is None and k == n_k - 1))
            if extra_mms is not None:
                extra_mms(d, ps, c0, cw)
        pe.mark()
        pe_done_at[d] = pe.n
        evac(d, pss, nsplit)
        v_done_at[d] = v.n
    return




def emit_post_ln_residual(ctx, ln, src_fm, x_old_h, x_new_h):
    v, a, qs = ctx.v, ctx.a, ctx.qs
    tm = ctx.sb("plr_tm_" + ctx.name, [128, DIM], BF16)
    xold = ctx.sb("plr_xold_" + ctx.name, [128, DIM], F32)
    nrm = ln.sq
    for i in range(8):
        qs.wait(v, v.n)
        tp = qs.dmat(tm[:, :], src_fm[:, i * 128:(i + 1) * 128].opt())
        ld = qs.dma(xold[:, :], x_old_h[i * 128:(i + 1) * 128, :])
        v.wait(qs, tp)
        v.wait(qs, ld)
        a.wait(qs, tp)
        a.wait(v, v.n)
        ln.stats_and_u(tm, 128, nrm)
        v.op("tensor_tensor", nrm[:, :], nrm[:, :], xold[:, :], ALU.add)
        qs.wait(v, v.n)
        qs.dma(x_new_h[i * 128:(i + 1) * 128, :], nrm[:, :])


def build_phaseB(ctx=None):
    if ctx is None:
        ctx = Ctx("phaseB")
    x1 = ctx.din("x1", (T, DIM))
    WT = ctx.din("F0_WT", (DIM, DIM))
    biasR = ctx.din("F0_biasR", (128, 32))
    AcatT = ctx.din("F0_AcatT", (DIM, 256))
    Bcat = ctx.din("F0_Bcat", (256, DIM))
    coefR = ctx.din("F0_coefR", (128, 2))
    scalR = ctx.din("F0_scalR", (128, 2))
    x2 = ctx.dout("x2", (T, DIM))
    ffn_h = ctx.dram("ffn_h", (DIM, T), BF16)

    uT = ctx.sb("uT", [128, 32, T], BF16)
    ln = LnKit(ctx, "b")
    bias_s = ctx.sb("bias_s", [128, 32], F32)
    coef_s = ctx.sb("coef_s", [128, 2], F32)
    scal_s = ctx.sb("scal_s", [128, 2], F32)
    AcT_s = ctx.sb("AcT_s", [128, 32, 256], BF16)
    bcb = [ctx.sb("bcb0", [128, 2, 128], BF16), ctx.sb("bcb1", [128, 2, 128], BF16)]
    xa = ctx.sb("xa", [128, 2, T], BF16)
    stage = ctx.sb("stage", [128, T], BF16)
    psX = ctx.ps("psX", [128, 512], F32)
    psY = ctx.ps("psY", [128, 512], F32)

    pe, v, a, qs, qg = ctx.pe, ctx.v, ctx.a, ctx.qs, ctx.qg

    qg.dma(bias_s[:, :], biasR[:, :])
    qg.dma(coef_s[:, :], coefR[:, :])
    qg.dma(scal_s[:, :], scalR[:, :])
    qg.dma(AcT_s[:, :, :], AcatT[:, :].rearrange("(a k) m -> k a m", k=128))

    for i in range(8):
        ln.tile_from_hbm(x1, i * 128, 128, uT=uT, ucol=i * 128)

    ctx.barrier(pe, [ctx.qs, ctx.v, ctx.qg])
    for er in range(2):
        for ci in range(2):
            ps = psX if ci == 0 else psY
            for k in range(32):
                pe.mm(ps[:, :], AcT_s[:, k, er * 128:(er + 1) * 128],
                      uT[:, k, ci * 512:(ci + 1) * 512], start=(k == 0), stop=(k == 31))
            pe.mark()
            v.wait(pe, pe.n)
            v.op("tensor_scalar", xa[:, er, ci * 512:(ci + 1) * 512],
                 ps[:, :], coef_s[:, er:er + 1], scal_s[:, er:er + 1],
                 ALU.mult, ALU.mult)
    pe.raw(lambda eng, val=v.n: eng.wait_ge(v.sem, val))

    def pre_tile(d):
        return qg.dma(bcb[d % 2][:, :, :],
                      Bcat[:, d * 128:(d + 1) * 128].rearrange("(c k) m -> k c m", k=128))

    def extra(d, ps, c0, cw):
        pe.mm(ps[:, 0:cw], bcb[d % 2][:, 0, :], xa[:, 0, c0:c0 + cw], start=False, stop=False)
        pe.mm(ps[:, 0:cw], bcb[d % 2][:, 1, :], xa[:, 1, c0:c0 + cw], start=False, stop=True)

    def evac(d, pss, nsplit):
        v.wait(pe, pe.n)
        for ci, (c0, cw) in enumerate(nsplit):
            v.op("tensor_scalar", stage[:, c0:c0 + cw], pss[ci][:, 0:cw],
                 bias_s[:, d:d + 1], None, ALU.add)
        qs.wait(v, v.n)
        st = qs.dma(ffn_h[d * 128:(d + 1) * 128, :], stage[:, :])
        v.wait(qs, st)

    emit_weight_gemm(ctx, WT, uT, 32, 32, T, evac, "w0", extra_mms=extra,
                     pre_tile=pre_tile)

    emit_post_ln_residual(ctx, ln, ffn_h, x1, x2)
    ctx.finish_outputs()
    return ctx




def build_phaseD(ctx=None, out_name="x4"):
    if ctx is None:
        ctx = Ctx("phaseD")
    x3 = ctx.din("x3", (T, DIM))
    WT = ctx.din("F1_WT", (DIM, DIM))
    biasR = ctx.din("F1_biasR", (128, 32))
    WrT = ctx.din("F1_WrT", (DIM, 8))
    AcatT = ctx.din("F1_AcatT", (DIM, 128))
    Bcat = ctx.din("F1_Bcat", (128, DIM))
    E8 = ctx.din("E8", (8, 128), BF16)
    ident = ctx.din("ident", (128, 128), BF16)
    x4 = ctx.dout(out_name, (T, DIM))
    ffn_h = ctx.dram("ffn_h", (DIM, T), BF16)

    uT = ctx.sb("uT", [128, 32, T], BF16)
    ln = LnKit(ctx, "d")
    bias_s = ctx.sb("bias_s", [128, 32], F32)
    WrT_s = ctx.sb("WrT_s", [128, 32, 8], BF16)
    AcT_s = ctx.sb("AcT_s", [128, 32, 128], BF16)
    Bc_s = ctx.sb("Bc_s", [128, 32, 128], BF16)
    E8_s = ctx.sb("E8_s", [8, 128], BF16)
    id_s = ctx.sb("id_s", [128, 128], BF16)
    lg = ctx.sb("lg", [128, 8], F32)
    m1 = ctx.sb("m1", [128, 1], F32)
    m2 = ctx.sb("m2", [128, 1], F32)
    ge1 = ctx.sb("ge1", [128, 8], F32)
    lg2 = ctx.sb("lg2", [128, 8], F32)
    ex = ctx.sb("ex", [128, 8], F32)
    den = ctx.sb("den", [128, 1], F32)
    rec = ctx.sb("rec", [128, 1], F32)
    wtm = ctx.sb("wtm", [128, 8], BF16)
    w_fm = ctx.sb("w_fm", [8, T], BF16)
    wex = ctx.sb("wex", [128, T], BF16)
    hpr = ctx.sb("hpr", [128, T], BF16)
    stage = ctx.sb("stage", [128, T], BF16)
    psR = ctx.ps("psR", [128, 8], F32)
    psT = ctx.ps("psT", [8, 128], BF16)
    psX = ctx.ps("psX", [128, 512], F32)
    psY = ctx.ps("psY", [128, 512], F32)

    pe, v, a, qs, qg = ctx.pe, ctx.v, ctx.a, ctx.qs, ctx.qg

    qg.dma(bias_s[:, :], biasR[:, :])
    qg.dma(WrT_s[:, :, :], WrT[:, :].rearrange("(a k) m -> k a m", k=128))
    qg.dma(AcT_s[:, :, :], AcatT[:, :].rearrange("(a k) m -> k a m", k=128))
    qg.dma(Bc_s[:, :, :], Bcat[:, :].rearrange("k (a m) -> k a m", m=128))
    qg.dma(E8_s[:, :], E8[:, :])
    qg.dma(id_s[:, :], ident[:, :])

    for i in range(8):
        ln.tile_from_hbm(x3, i * 128, 128, uT=uT, ucol=i * 128)

    ctx.barrier(pe, [ctx.qs, ctx.v, ctx.qg])
    for i in range(8):
        for k in range(32):
            pe.mm(psR[:, :], uT[:, k, i * 128:(i + 1) * 128], WrT_s[:, k, :],
                  start=(k == 0), stop=(k == 31))
        pe.mark()
        v.wait(pe, pe.n)
        v.op("tensor_copy", lg[:, :], psR[:, :])
        v.op("reduce_max", m1[:, :], lg[:, :], axis=AX.X)
        v.op("tensor_scalar", ge1[:, :], lg[:, :], m1[:, 0:1], None, ALU.is_ge)
        v.op("scalar_tensor_tensor", lg2[:, :], ge1[:, :], -1e30, lg[:, :],
             ALU.mult, ALU.add)
        v.op("reduce_max", m2[:, :], lg2[:, :], axis=AX.X)
        v.op("tensor_scalar", m1[:, :], m1[:, :], -1.0, None, ALU.mult)
        a.wait(v, v.n)
        a.op("activation", ex[:, :], lg[:, :], AF.Exp, bias=m1[:, 0:1])
        v.wait(a, a.n)
        v.op("tensor_scalar", ge1[:, :], lg[:, :], m2[:, 0:1], None, ALU.is_ge)
        v.op("tensor_tensor", ex[:, :], ex[:, :], ge1[:, :], ALU.mult)
        v.op("reduce_sum", den[:, :], ex[:, :], axis=AX.X)
        v.op("reciprocal", rec[:, :], den[:, :])
        v.op("tensor_scalar", wtm[:, :], ex[:, :], rec[:, 0:1], None, ALU.mult)
        pe.raw(lambda eng, val=v.n: eng.wait_ge(v.sem, val))
        pe.op("transpose", psT[:, :], wtm[:, :], id_s[:, :])
        pe.mark()
        v.wait(pe, pe.n)
        v.op("tensor_copy", w_fm[:, i * 128:(i + 1) * 128], psT[0:8, :])
    pe.raw(lambda eng, val=v.n: eng.wait_ge(v.sem, val))
    for ci in range(2):
        ps = psX if ci == 0 else psY
        pe.mm(ps[:, :], E8_s[:, :], w_fm[0:8, ci * 512:(ci + 1) * 512],
              start=True, stop=True)
        pe.mark()
        v.wait(pe, pe.n)
        v.op("tensor_scalar", wex[:, ci * 512:(ci + 1) * 512], ps[:, :],
             2.0, None, ALU.mult)
    for ci in range(2):
        ps = psX if ci == 0 else psY
        pe.raw(lambda eng, val=v.n: eng.wait_ge(v.sem, val))
        for k in range(32):
            pe.mm(ps[:, :], AcT_s[:, k, :], uT[:, k, ci * 512:(ci + 1) * 512],
                  start=(k == 0), stop=(k == 31))
        pe.mark()
        v.wait(pe, pe.n)
        v.op("tensor_tensor", hpr[:, ci * 512:(ci + 1) * 512], ps[:, :],
             wex[:, ci * 512:(ci + 1) * 512], ALU.mult)
    pe.raw(lambda eng, val=v.n: eng.wait_ge(v.sem, val))

    def extra(d, ps, c0, cw):
        pe.mm(ps[:, 0:cw], Bc_s[:, d, :], hpr[:, c0:c0 + cw], start=False, stop=True)

    def evac(d, pss, nsplit):
        v.wait(pe, pe.n)
        for ci, (c0, cw) in enumerate(nsplit):
            v.op("tensor_scalar", stage[:, c0:c0 + cw], pss[ci][:, 0:cw],
                 bias_s[:, d:d + 1], None, ALU.add)
        qs.wait(v, v.n)
        st = qs.dma(ffn_h[d * 128:(d + 1) * 128, :], stage[:, :])
        v.wait(qs, st)

    emit_weight_gemm(ctx, WT, uT, 32, 32, T, evac, "w1", extra_mms=extra)

    emit_post_ln_residual(ctx, ln, ffn_h, x3, x4)
    ctx.finish_outputs()
    return ctx




def emit_roleswap_gemm(ctx, wT_h, uT, out_h, wname, rope=None):
    pe, v, qs, qg = ctx.pe, ctx.v, ctx.qs, ctx.qg
    if not hasattr(ctx, "_rs_bufs"):
        ctx._rs_bufs = dict(
            wb=[ctx.sb("rs_wb0", [128, 32, 512], BF16),
                ctx.sb("rs_wb1", [128, 32, 512], BF16)],
            pss=[ctx.ps("rs_psA", [128, 512], F32),
                 ctx.ps("rs_psB", [128, 512], F32)],
            stage=ctx.sb("rs_st", [128, 512], BF16),
            t1=ctx.sb("rs_t1", [128, 512], F32),
            t2=ctx.sb("rs_t2", [128, 512], F32))
    wb = ctx._rs_bufs["wb"]
    pss = ctx._rs_bufs["pss"]
    stage = ctx._rs_bufs["stage"]
    t1 = ctx._rs_bufs["t1"]
    t2 = ctx._rs_bufs["t2"]
    qg.wait(pe, pe.n)
    qg.wait(v, v.n)
    pe_done = {}
    v_done = {}
    for ci in range(8):
        b = wb[ci % 2]
        if ci >= 2:
            qg.wait(pe, pe_done[ci - 2])
        ld = qg.dma(b[:, :, :],
                    wT_h[:, ci * 512:(ci + 1) * 512].rearrange("(a k) m -> k a m", k=128))
        pe.raw(lambda eng, ld=ld: eng.wait_ge(qg.sem, ld))
        for ti in range(8):
            ps = pss[ti % 2]
            if (ci, ti) != (0, 0) and (ci, ti) != (0, 1):
                prev = v_done.get((ci, ti - 2)) or v_done.get((ci - 1, ti + 6))
                if prev:
                    pe.raw(lambda eng, val=prev: eng.wait_ge(v.sem, val))
            for k in range(32):
                pe.mm(ps[:, :], uT[:, k, ti * 128:(ti + 1) * 128], b[:, k, :],
                      start=(k == 0), stop=(k == 31))
            pe.mark()
            pe_done[ci] = pe.n
            v.wait(pe, pe.n)
            if rope is None:
                v.op("tensor_copy", stage[:, :], ps[:, :])
            else:
                cos_s, sin_s = rope
                pv = ps[:, :].rearrange("t (h v f) -> t h v f", h=4, v=2)
                cl = cos_s[:, ti, 0:64].unsqueeze(1).broadcast_to([128, 4, 64])
                ch = cos_s[:, ti, 64:128].unsqueeze(1).broadcast_to([128, 4, 64])
                sl = sin_s[:, ti, 0:64].unsqueeze(1).broadcast_to([128, 4, 64])
                sh = sin_s[:, ti, 64:128].unsqueeze(1).broadcast_to([128, 4, 64])
                t1v = t1[:, :].rearrange("t (h v f) -> t h v f", h=4, v=2)
                v.op("tensor_tensor", t1v[:, :, 0, :], pv[:, :, 0, :], cl, ALU.mult)
                v.op("tensor_tensor", t1v[:, :, 1, :], pv[:, :, 1, :], ch, ALU.mult)
                t2v = t2[:, :].rearrange("t (h v f) -> t h v f", h=4, v=2)
                v.op("tensor_tensor", t2v[:, :, 0, :], pv[:, :, 1, :], sl, ALU.mult)
                v.op("tensor_tensor", t2v[:, :, 1, :], pv[:, :, 0, :], sh, ALU.mult)
                v.op("tensor_tensor", stage[:, :], t1[:, :], t2[:, :], ALU.add)
            qs.wait(v, v.n)
            st = qs.dma(out_h[ti * 128:(ti + 1) * 128, ci * 512:(ci + 1) * 512],
                        stage[:, :])
            v.wait(qs, st)
            v_done[(ci, ti)] = v.n


def build_phaseC1(ctx=None):
    if ctx is None:
        ctx = Ctx("phaseC1")
    x2 = ctx.din("x2", (T, DIM))
    WqT = ctx.din("WqT", (DIM, DIM))
    WkT = ctx.din("WkT", (DIM, DIM))
    WvT = ctx.din("WvT", (DIM, DIM))
    cosR = ctx.din("rope_cos", (128, 8, HD))
    sinR = ctx.din("rope_sin", (128, 8, HD))
    q_h = ctx.dout("q_tm", (T, DIM), BF16)
    k_h = ctx.dout("k_tm", (T, DIM), BF16)
    v_h = ctx.dout("v_tm", (T, DIM), BF16)

    uT = ctx.sb("uT", [128, 32, T], BF16)
    ln = LnKit(ctx, "c")
    cos_s = ctx.sb("cos_s", [128, 8, HD], F32)
    sin_s = ctx.sb("sin_s", [128, 8, HD], F32)

    qg = ctx.qg
    qg.dma(cos_s[:, :, :], cosR[:, :, :])
    qg.dma(sin_s[:, :, :], sinR[:, :, :])

    for i in range(8):
        ln.tile_from_hbm(x2, i * 128, 128, uT=uT, ucol=i * 128)
    ctx.barrier(ctx.pe, [ctx.qs, ctx.v, ctx.qg])

    emit_roleswap_gemm(ctx, WqT, uT, q_h, "wq", rope=(cos_s, sin_s))
    emit_roleswap_gemm(ctx, WkT, uT, k_h, "wk", rope=(cos_s, sin_s))
    emit_roleswap_gemm(ctx, WvT, uT, v_h, "wv", rope=None)
    ctx.finish_outputs()
    return ctx


def build_phaseC2(ctx=None):
    if ctx is None:
        ctx = Ctx("phaseC2")
    x2 = ctx.din("x2", (T, DIM))
    q_h = ctx.din("q_tm", (T, DIM), BF16)
    k_h = ctx.din("k_tm", (T, DIM), BF16)
    v_h = ctx.din("v_tm", (T, DIM), BF16)
    kp_h = ctx.din("kprev_tm", (T, DIM), BF16)
    vp_h = ctx.din("vprev_tm", (T, DIM), BF16)
    wmask = ctx.din("wmask", (2, 12, 128, 512), BF16)
    WoT = ctx.din("WoT", (DIM, DIM))
    onescol = ctx.din("onescol", (128, 1), BF16)
    onesrow = ctx.din("onesrow", (1, 128), BF16)
    x3 = ctx.dout("x3", (T, DIM))
    attn_h = ctx.dram("attn_h", (DIM, T), BF16)
    wo_h = ctx.dram("wo_h", (DIM, T), BF16)

    ln = LnKit(ctx, "c2", slim=True)
    mask_s = ctx.sb("mask_s", [128, 2, 12, 512], BF16)
    ones_s = ctx.sb("ones_s", [128, 1], BF16)
    onesr_s = ctx.sb("onesr_s", [1, 128], BF16)
    mk_attn = ctx.phase_mark()
    qh_s = [ctx.sb("qh0", [128, T], BF16), ctx.sb("qh1", [128, T], BF16)]
    kw_s = [ctx.sb("kw0", [128, 2048], BF16), ctx.sb("kw1", [128, 2048], BF16)]
    vw_s = [ctx.sb("vw0", [128, 16, 128], BF16), ctx.sb("vw1", [128, 16, 128], BF16)]
    probs = ctx.sb("probs", [128, 12, 512], BF16)
    sum_f = ctx.sb("sum_f", [1, 512], F32)
    rec_f = ctx.sb("rec_f", [1, 512], F32)
    rec_b = ctx.sb("rec_b", [1, 512], BF16)
    recB = ctx.sb("recB", [128, 512], BF16)
    ast = ctx.sb("ast", [128, 512], BF16)
    psS = ctx.ps("psS", [128, 512], F32)
    psSum = ctx.ps("psSum", [1, 512], F32)
    psAV = ctx.ps("psAV", [128, 512], F32)
    psRB = ctx.ps("psRB", [128, 512], F32)

    pe, v, a, qs, qg = ctx.pe, ctx.v, ctx.a, ctx.qs, ctx.qg

    qg.dma(mask_s[:, :, :, :], wmask[:, :, :, :].rearrange("g b k q -> k g b q"))
    qg.dma(ones_s[:, :], onescol[:, :])
    qg.dma(onesr_s[:, :], onesrow[:, :])

    SC = 1.0 / (HD ** 0.5)
    ld_done = {}
    v_done_head = {}
    pe_done_head = {}
    for h in range(32):
        qb, kb_, vb = qh_s[h % 2], kw_s[h % 2], vw_s[h % 2]
        if h >= 2:
            qs.wait(v, v_done_head[h - 2])
            qs.wait(pe, pe_done_head[h - 2])
        l1 = qs.dmat(qb[:, :], q_h[:, h * 128:(h + 1) * 128].opt())
        l2 = qs.dmat(kb_[:, 0:1024], kp_h[:, h * 128:(h + 1) * 128].opt())
        l3 = qs.dmat(kb_[:, 1024:2048], k_h[:, h * 128:(h + 1) * 128].opt())
        l4 = qs.dma(vb[:, 0:8, :], vp_h[:, h * 128:(h + 1) * 128].rearrange("(b r) d -> r b d", r=128))
        l5 = qs.dma(vb[:, 8:16, :], v_h[:, h * 128:(h + 1) * 128].rearrange("(b r) d -> r b d", r=128))
        pe.raw(lambda eng, val=l5: eng.wait_ge(qs.sem, val))
        for qg_i in range(2):
            for kb in range(12):
                koff = qg_i * 512 + kb * 128
                pe.raw(lambda eng, val=a.n: eng.wait_ge(a.sem, val))
                pe.mm(psS[:, :], kb_[:, koff:koff + 128], qb[:, qg_i * 512:(qg_i + 1) * 512],
                      start=True, stop=True)
                pe.mark()
                a.wait(pe, pe.n)
                if kb > 0 or qg_i > 0 or h > 0:
                    a.wait(v, v.n)
                a.op("activation", probs[:, kb, :], psS[:, :], AF.Exp, scale=SC)
                v.wait(a, a.n)
                v.op("tensor_tensor", probs[:, kb, :], probs[:, kb, :],
                     mask_s[:, qg_i, kb, :], ALU.mult)
            pv = v.n
            pe.raw(lambda eng, val=pv: eng.wait_ge(v.sem, val))
            for kb in range(12):
                pe.mm(psSum[:, :], ones_s[:, 0:1],
                      probs[:, kb, :], start=(kb == 0), stop=(kb == 11))
            for kb in range(12):
                koff = qg_i * 512 + kb * 128
                bidx = qg_i * 4 + kb
                pe.mm(psAV[:, :], vb[:, bidx, :],
                      probs[:, kb, :], start=(kb == 0), stop=(kb == 11))
            pe.mark()
            v.wait(pe, pe.n)
            v.op("tensor_copy", sum_f[:, :], psSum[0:1, :])
            v.op("reciprocal", rec_f[:, :], sum_f[:, :])
            v.op("tensor_copy", rec_b[:, :], rec_f[:, :])
            pe.raw(lambda eng, val=v.n: eng.wait_ge(v.sem, val))
            pe.mm(psRB[:, :], onesr_s[0:1, :], rec_b[0:1, :], start=True, stop=True)
            pe.mark()
            v.wait(pe, pe.n)
            v.op("tensor_copy", recB[:, :], psRB[:, :])
            v.op("tensor_tensor", ast[:, :], psAV[:, :], recB[:, :], ALU.mult)
            qs.wait(v, v.n)
            st = qs.dma(attn_h[h * 128:(h + 1) * 128, qg_i * 512:(qg_i + 1) * 512],
                        ast[:, :])
            v.wait(qs, st)
        v_done_head[h] = v.n
        pe_done_head[h] = pe.n

    ctx.phase_free(mk_attn)
    attn_s = ctx.sb("attn_s", [128, 32, T], BF16)
    la = qs.dma(attn_s[:, :, :], attn_h[:, :].rearrange("(a k) t -> k a t", k=128))
    pe.raw(lambda eng, val=la: eng.wait_ge(qs.sem, val))
    stage2 = ctx.sb("stage2", [128, T], BF16)

    def evac(d, pss2, nsplit):
        v.wait(pe, pe.n)
        for ci, (c0, cw) in enumerate(nsplit):
            v.op("tensor_copy", stage2[:, c0:c0 + cw], pss2[ci][:, 0:cw])
        qs.wait(v, v.n)
        st = qs.dma(wo_h[d * 128:(d + 1) * 128, :], stage2[:, :])
        v.wait(qs, st)

    emit_weight_gemm(ctx, WoT, attn_s, 32, 32, T, evac, "wo")
    emit_post_ln_residual(ctx, ln, wo_h, x2, x3)
    ctx.finish_outputs()
    return ctx




def build_phaseA1(ctx=None):
    if ctx is None:
        ctx = Ctx("phaseA1")
    x_in = ctx.din("x_in", (HT, DIM))
    WT = ctx.din("W_inT", (DIM, NPROJ))
    convw = ctx.din("conv_wR", (128, 80, KCONV))
    convb = ctx.din("conv_bR", (128, 80))
    dtb = ctx.din("dt_bias", (128, 1))
    alog = ctx.din("A_log", (128, 1))
    normw = ctx.din("norm_wR", (128, 64))
    ident = ctx.din("ident", (128, 128), BF16)
    onesr = ctx.din("onesrow", (1, 128), BF16)

    xtm_h = ctx.dout("xtm", (T, INTER), BF16)
    gate_h = ctx.dout("gate", (INTER, T), BF16)
    bc_h = ctx.dout("bc", (2 * G * N, T), BF16)
    dt_h = ctx.dout("dt", (128, T))
    Rloc_h = ctx.dout("Rloc", (NCH, 128, INTER))
    Z_h = ctx.dout("Z", (128, INTER))
    cumA_h = ctx.dout("cumA", (128, 1))
    decflat_h = ctx.dram("decflat", (1, 1024))

    uT = ctx.sb("uT", [128, 32, HT], BF16)
    ln = LnKit(ctx, "a")
    wb = [ctx.sb("wb0", [128, 32, 128], BF16), ctx.sb("wb1", [128, 32, 128], BF16)]
    ident_s = ctx.sb("ident_s", [128, 128], BF16)
    onesr_s = ctx.sb("onesr_s", [1, 128], BF16)
    convw_s = ctx.sb("convw_s", [128, 80, KCONV], F32)
    convb_s = ctx.sb("convb_s", [128, 80], F32)
    dtb_s = ctx.sb("dtb_s", [128, 1], F32)
    alog_s = ctx.sb("alog_s", [128, 1], F32)
    normw_s = ctx.sb("normw_s", [128, 64], F32)
    A_col = ctx.sb("A_col", [128, 1], F32)
    hbc_s = ctx.sb("hbc_s", [128, HT], BF16)
    diag_s = ctx.sb("diag_s", [128, 4, 128], BF16)
    act_s = ctx.sb("act_s", [128, T], BF16)
    dt_s = ctx.sb("dt_s", [128, T], F32)
    xb_s = ctx.sb("xb_s", [128, T], F32)
    a_s = ctx.sb("a_s", [128, T], F32)
    Acum_s = ctx.sb("Acum_s", [128, T], F32)
    zeros_s = ctx.sb("zeros_s", [128, CS], F32)
    dec_s = ctx.sb("dec_s", [128, T], F32)
    sp1_s = dec_s
    ddt_s = xb_s
    ddtT_s = ctx.sb("ddtT_s", [128, NCH, 128], F32)
    dec8_s = ctx.sb("dec8_s", [128, NCH], F32)
    decB_s = ctx.sb("decB_s", [128, 1024], F32)
    Btm_s = ctx.sb("Btm_s", [128, NCH, 128], BF16)
    xtmg_s = ctx.sb("xtmg_s", [128, NCH, 8, 128], BF16)
    xsl_s = ctx.sb("xsl_s", [128, NCH, 8, 128], BF16)
    Rg_s = ctx.sb("Rg_s", [128, 1024], F32)
    Rt_s = ctx.sb("Rt_s", [128, 1024], F32)
    Rst_s = ctx.sb("Rst_s", [128, 1024], F32)
    cumA_s = ctx.sb("cumA_s", [128, 1], F32)
    stage_s = None

    psI = [ctx.ps("psIa", [128, 512], F32), ctx.ps("psIb", [128, 512], F32)]
    psI2 = [ctx.ps("psIa2", [128, 512], F32), ctx.ps("psIb2", [128, 512], F32)]
    psIh = ctx.ps("psIh", [128, 32], F32)
    psC0 = ctx.ps("psCa", [128, 512], F32)
    psC = [psC0, psC0]
    psS = ctx.ps("psS", [128, 1024], F32)
    psT = psC0[:, 0:64].bitcast(BF16)

    pe, v, a, qs, qg = ctx.pe, ctx.v, ctx.a, ctx.qs, ctx.qg

    qg.dma(ident_s[:, :], ident[:, :])
    qg.dma(onesr_s[:, :], onesr[:, :])
    qg.dma(convw_s[:, :, :], convw[:, :, :])
    qg.dma(convb_s[:, :], convb[:, :])
    qg.dma(dtb_s[:, :], dtb[:, :])
    qg.dma(alog_s[:, :], alog[:, :])
    lc = qg.dma(normw_s[:, :], normw[:, :])

    a.wait(qg, lc)
    a.op("activation", A_col[:, :], alog_s[:, :], AF.Exp)
    v.wait(a, a.n)
    v.op("tensor_scalar", A_col[:, :], A_col[:, :], -1.0, None, ALU.mult)
    v.op("memset", zeros_s[:, :], 0.0)

    ln.tile_from_hbm(x_in, 0, HALO, uT=uT, ucol=0)
    for i in range(8):
        ln.tile_from_hbm(x_in, HALO + i * 128, 128, uT=uT, ucol=HALO + i * 128)
    ctx.barrier(pe, [ctx.qs, ctx.v, ctx.qg])

    NSsets = [[(32, 512, psI[0]), (544, 512, psI[1]), (0, 32, psIh)],
              [(32, 512, psI2[0]), (544, 512, psI2[1]), (0, 32, psIh)]]
    _ip_state = {"marks": {}, "seq": [], "rdr": {}}

    def inproj_tile(d, halo_too=True):
        seq = _ip_state["seq"]
        if len(seq) >= 2:
            qg.wait(pe, _ip_state["marks"][seq[-2]])
        b = wb[len(seq) % 2]
        ld = qg.dma(b[:, :, :],
                    WT[:, d * 128:(d + 1) * 128].rearrange("(a k) m -> k a m", k=128))
        pe.raw(lambda eng, val=ld: eng.wait_ge(qg.sem, val))
        if len(seq) >= 2:
            pv, pa = _ip_state["rdr"].get(seq[-2], (0, 0))
            pe.raw(lambda eng, val=pv: eng.wait_ge(v.sem, val))
            pe.raw(lambda eng, val=pa: eng.wait_ge(a.sem, val))
        NS = NSsets[len(seq) % 2]
        for (c0, cw, ps) in NS:
            if c0 == 0:
                if not halo_too:
                    continue
                if len(seq) >= 1 and seq[-1] in _ip_state["rdr"]:
                    pv, pa = _ip_state["rdr"][seq[-1]]
                    pe.raw(lambda eng, val=pv: eng.wait_ge(v.sem, val))
            for k in range(32):
                pe.mm(ps[:, 0:cw], b[:, k, :], uT[:, k, c0:c0 + cw],
                      start=(k == 0), stop=(k == 31))
        pe.mark()
        seq.append(d)
        _ip_state["marks"][d] = pe.n
        return NS

    def inproj_done(d):
        _ip_state["rdr"][d] = (v.n, a.n)

    def conv_and_silu(d, cidx):
        for k in range(KCONV):
            v.op("tensor_scalar", diag_s[:, k, :], ident_s[:, :],
                 convw_s[:, cidx:cidx + 1, k:k + 1].rearrange("p a b -> p (a b)"),
                 None, ALU.mult)
        pe.raw(lambda eng, val=v.n: eng.wait_ge(v.sem, val))
        for ci in range(2):
            pe.raw(lambda eng, val=a.n: eng.wait_ge(a.sem, val))
            for k in range(KCONV):
                pe.mm(psC[ci][:, :], diag_s[:, k, :],
                      hbc_s[:, 29 + k + ci * 512: 29 + k + ci * 512 + 512],
                      start=(k == 0), stop=(k == KCONV - 1))
            pe.mark()
            a.wait(pe, pe.n)
            a.op("activation", act_s[:, ci * 512:(ci + 1) * 512], psC[ci][:, :],
                 AF.Silu, bias=convb_s[:, cidx:cidx + 1])

    NS_ = inproj_tile(144, halo_too=False)
    v.wait(pe, pe.n)
    for ci in range(2):
        v.op("tensor_scalar", xb_s[:, ci * 512:(ci + 1) * 512], NS_[ci][2][:, :],
             dtb_s[:, 0:1], None, ALU.add)
    inproj_done(144)
    a.wait(v, v.n)
    a.op("activation", sp1_s[:, :], xb_s[:, :], AF.Abs)
    a.op("activation", sp1_s[:, :], sp1_s[:, :], AF.Exp, scale=-1.0)
    a.op("activation", sp1_s[:, :], sp1_s[:, :], AF.Ln, bias=1.0)
    v.wait(a, a.n)
    v.op("tensor_scalar", dt_s[:, :], xb_s[:, :], 0.0, None, ALU.max)
    v.op("tensor_tensor", dt_s[:, :], dt_s[:, :], sp1_s[:, :], ALU.add)
    qs.wait(v, v.n)
    qs.dma(dt_h[:, :], dt_s[:, :])
    v.op("tensor_scalar", a_s[:, :], dt_s[:, :], A_col[:, 0:1], None, ALU.mult)
    for c in range(NCH):
        v.op("tensor_tensor_scan", Acum_s[:, c * CS:(c + 1) * CS],
             a_s[:, c * CS:(c + 1) * CS], zeros_s[:, :], 0.0, ALU.add, ALU.add)
    v.op("reduce_sum", cumA_s[:, :],
         Acum_s[:, :].rearrange("p (c l) -> p c l", c=NCH)[:, :, CS - 1:CS].squeeze(2),
         axis=AX.X)
    qs.wait(v, v.n)
    qs.dma(cumA_h[:, :], cumA_s[:, :])
    for c in range(NCH):
        v.op("tensor_scalar", dec_s[:, c * CS:(c + 1) * CS],
             Acum_s[:, c * CS:(c + 1) * CS],
             Acum_s[:, c * CS + CS - 1:c * CS + CS], None, ALU.subtract)
    a.wait(v, v.n)
    a.op("activation", dec_s[:, :], dec_s[:, :], AF.Exp, scale=-1.0)
    v.wait(a, a.n)
    v.op("tensor_tensor", ddt_s[:, :], dec_s[:, :], dt_s[:, :], ALU.mult)
    v.op("tensor_copy", act_s[:, :], ddt_s[:, :])
    pe.raw(lambda eng, val=v.n: eng.wait_ge(v.sem, val))
    for c in range(NCH):
        pe.op("transpose", psT, act_s[:, c * CS:(c + 1) * CS], ident_s[:, :])
        pe.mark()
        v.wait(pe, pe.n)
        v.op("tensor_copy", ddtT_s[:, c, :], psT)
        pe.raw(lambda eng, val=v.n: eng.wait_ge(v.sem, val))
    a.op("activation", dec8_s[:, :],
         Acum_s[:, :].rearrange("p (c l) -> p c l", c=NCH)[:, :, CS - 1:CS].squeeze(2),
         AF.Exp)
    qs.wait(a, a.n)
    f1 = qs.dma(decflat_h[0, :].rearrange("(h c) -> h c", h=128), dec8_s[:, :])
    qs.wait(qs, f1)
    f2 = qs.dma(decB_s[:, :], decflat_h[0:1, :].broadcast_to([128, 1024]))
    v.wait(qs, f2)

    for d in range(128, 144):
        NS_ = inproj_tile(d)
        v.wait(pe, pe.n)
        v.op("tensor_copy", hbc_s[:, 0:32], NS_[2][2][:, :])
        v.op("tensor_copy", hbc_s[:, 32:544], NS_[0][2][:, :])
        v.op("tensor_copy", hbc_s[:, 544:1056], NS_[1][2][:, :])
        inproj_done(d)
        cidx = d - 64
        conv_and_silu(d, cidx)
        qs.wait(a, a.n)
        st = qs.dma(bc_h[(d - 128) * 128:(d - 127) * 128, :], act_s[:, 0:T])
        a.wait(qs, st)
        if d < 136:
            pass

    for g in range(8):
        lb = qs.dma(hbc_s[:, 0:T], bc_h[g * 128:(g + 1) * 128, :])
        qs.wait(qs, lb)
        tb = qs.dmat(Btm_s[:, :, :], hbc_s[:, 0:T].rearrange("n (c s) -> n c s", c=NCH))
        pe.raw(lambda eng, val=tb: eng.wait_ge(qs.sem, val))
        for j in range(8):
            d = 64 + g * 8 + j
            NS_ = inproj_tile(d)
            v.wait(pe, pe.n)
            v.op("tensor_copy", hbc_s[:, 0:32], NS_[2][2][:, :])
            v.op("tensor_copy", hbc_s[:, 32:544], NS_[0][2][:, :])
            v.op("tensor_copy", hbc_s[:, 544:1056], NS_[1][2][:, :])
            inproj_done(d)
            conv_and_silu(d, d - 64)
            qs.wait(a, a.n)
            tx = qs.dmat(xtmg_s[:, :, j, :], act_s[:, :].rearrange("r (c t) -> r c t", c=NCH))
            qs.wait(qs, tx)
            st = qs.dma(xtm_h[:, (d - 64) * 128:(d - 63) * 128]
                        .rearrange("(c tl) ch -> tl c ch", c=NCH), xtmg_s[:, :, j, :])
            v.wait(qs, st)
            for c in range(NCH):
                hv = ddtT_s[:, c, g * 16 + j * 2: g * 16 + j * 2 + 2]
                v.op("tensor_tensor", xsl_s[:, c, j, :].rearrange("p (h q) -> p h q", h=2),
                     xtmg_s[:, c, j, :].rearrange("p (h q) -> p h q", h=2),
                     hv.unsqueeze(2).broadcast_to([128, 2, 64]), ALU.mult)
        v.op("memset", Rg_s[:, :], 0.0)
        for c in range(NCH):
            pe.raw(lambda eng, val=v.n: eng.wait_ge(v.sem, val))
            for j in range(8):
                pe.mm(psS[:, j * 128:(j + 1) * 128], Btm_s[:, c, :],
                      xsl_s[:, c, j, :], start=True, stop=True)
            pe.mark()
            v.op("tensor_copy", Rst_s[:, :], Rg_s[:, :])
            qs.wait(v, v.n)
            sr = qs.dma(Rloc_h[c, :, g * 1024:(g + 1) * 1024], Rst_s[:, :])
            v.wait(pe, pe.n)
            dv = decB_s[:, :].rearrange("p (h c) -> p h c", h=128)[:, g * 16:(g + 1) * 16, c:c + 1]
            v.op("tensor_tensor", Rt_s[:, :].rearrange("p (h q) -> p h q", h=16),
                 Rg_s[:, :].rearrange("p (h q) -> p h q", h=16),
                 dv.broadcast_to([128, 16, 64]), ALU.mult)
            v.op("tensor_tensor", Rg_s[:, :], Rt_s[:, :], psS[:, :], ALU.add)
        qs.wait(v, v.n)
        sz = qs.dma(Z_h[:, g * 1024:(g + 1) * 1024], Rg_s[:, :])
        v.wait(qs, sz)

    for d in range(64):
        NS_ = inproj_tile(d, halo_too=False)
        a.wait(pe, pe.n)
        for ci in range(2):
            a.op("activation", act_s[:, ci * 512:(ci + 1) * 512], NS_[ci][2][:, :], AF.Silu)
        v.wait(a, a.n)
        v.op("tensor_scalar", act_s[:, 0:T], act_s[:, 0:T],
             normw_s[:, d:d + 1], None, ALU.mult)
        qs.wait(v, v.n)
        st = qs.dma(gate_h[d * 128:(d + 1) * 128, :], act_s[:, 0:T])
        v.wait(qs, st)
        a.wait(qs, st)
        inproj_done(d)

    ctx.finish_outputs()
    return ctx




def build_phaseA2a(ctx=None):
    if ctx is None:
        ctx = Ctx("phaseA2a")
    xtm_h = ctx.din("xtm", (T, INTER), BF16)
    gate_h = ctx.din("gate", (INTER, T), BF16)
    bc_h = ctx.din("bc", (2 * G * N, T), BF16)
    dt_h = ctx.din("dt", (128, T))
    alog = ctx.din("A_log", (128, 1))
    Rloc_h = ctx.din("Rloc", (NCH, 128, INTER))
    Zall_h = ctx.din("Z_all", (8, 128, INTER))
    cumAall = ctx.din("cumA_all", (128, 8))
    maskR = ctx.din("carrymaskR", (128, 8))
    D_B = ctx.din("D_B", (128, 128))
    tri = ctx.din("tri", (128, 128), BF16)
    ident = ctx.din("ident", (128, 128), BF16)
    identf = ctx.din("ident_f", (128, 128))
    z_h = ctx.dout("z", (INTER, T), BF16)
    acumflat_h = ctx.dram("acumflat", (NCH, 128 * 128))
    wjflat_h = ctx.dram("wjflat", (1, 1024))
    eprflat_h = ctx.dram("eprflat", (1, 1024))

    dt_s = ctx.sb("dt_s", [128, T], F32)
    a_s = ctx.sb("a_s", [128, T], F32)
    Acum_s = ctx.sb("Acum_s", [128, T], F32)
    zeros_s = ctx.sb("zeros_s", [128, CS], F32)
    alog_s = ctx.sb("alog_s", [128, 1], F32)
    A_col = ctx.sb("A_col", [128, 1], F32)
    AcumT_s = ctx.sb("AcumT_s", [128, NCH, 128], F32)
    exppT_s = ctx.sb("exppT_s", [128, NCH, 128], F32)
    dtT_s = ctx.sb("dtT_s", [128, NCH, 128], F32)
    dtb16 = ctx.sb("dtb16", [128, T], BF16)
    B_s = ctx.sb("B_s", [128, G, T], BF16)
    C_s = ctx.sb("C_s", [128, G, T], BF16)
    tri_s = ctx.sb("tri_s", [128, 128], BF16)
    ident_s = ctx.sb("ident_s", [128, 128], BF16)
    identf_s = ctx.sb("identf_s", [128, 128], F32)
    DB_s = ctx.sb("DB_s", [128, 128], F32)
    maskR_s = ctx.sb("maskR_s", [128, 8], F32)
    cumA_s = ctx.sb("cumA_s", [128, 8], F32)
    Tall_s = ctx.sb("Tall_s", [128, 8], F32)
    Tsel_s = ctx.sb("Tsel_s", [128, 1], F32)
    wj_s = ctx.sb("wj_s", [128, 8], F32)
    zer8 = ctx.sb("zer8", [128, 8], F32)
    wjB_s = ctx.sb("wjB_s", [128, 1024], F32)
    EprB_s = ctx.sb("EprB_s", [128, 1024], F32)
    Epr_s = ctx.sb("Epr_s", [128, 8], F32)
    carry_all = ctx.sb("carry_all", [128, G, 1024], BF16)
    carry_f = ctx.sb("carry_f", [128, 1024], F32)
    zj_s = ctx.sb("zj_s", [128, 1024], F32)
    AROW_s2 = [ctx.sb("AROW_s", [128, 2048], F32),
               ctx.sb("AROW_sb", [128, 2048], F32)]
    Lsl_s = ctx.sb("Lsl_s", [128, 2048], BF16)
    gm_s = ctx.sb("gm_s", [128, 128], BF16)
    M_s = ctx.sb("M_s", [128, 2048], BF16)
    xtm_c2 = [ctx.sb("xtm_c", [128, 1024], BF16), ctx.sb("xtm_cb", [128, 1024], BF16)]
    xdt_s = ctx.sb("xdt_s", [128, 1024], BF16)
    Rloc_b2 = [ctx.sb("Rloc_b", [128, 1024], F32), ctx.sb("Rloc_bb", [128, 1024], F32)]
    Rf_s = ctx.sb("Rf_s", [128, 1024], F32)
    rfb_s = ctx.sb("rfb_s", [128, 1024], BF16)
    yt1_s = ctx.sb("yt1_s", [128, 1024], F32)
    y1_s = yt1_s
    ycs = ctx.sb("ycs", [128, G, 1024], BF16)
    sqsc = None
    ssqc = ctx.sb("ssqc", [128, 8], F32)
    ssq = ctx.sb("ssq", [128, 1], F32)
    rstd = ctx.sb("rstd", [128, 1], F32)
    ytn_g = ctx.sb("ytn_g", [128, 1024], BF16)
    yfm_s = ctx.sb("yfm_s", [128, 64, 128], BF16)
    ggc_s = ctx.sb("ggc_s", [128, 64, 128], BF16)
    zfm_s = ctx.sb("zfm_s", [128, 64, 128], BF16)

    psT = ctx.ps("psT", [128, 128], F32)
    psG = ctx.ps("psG", [128, 128], F32)
    psY = ctx.ps("psY", [128, 1024], F32)
    psO = ctx.ps("psO", [128, 1024], F32)

    pe, v, a, qs, qg = ctx.pe, ctx.v, ctx.a, ctx.qs, ctx.qg

    qg.dma(tri_s[:, :], tri[:, :])
    qg.dma(ident_s[:, :], ident[:, :])
    qg.dma(identf_s[:, :], identf[:, :])
    qg.dma(DB_s[:, :], D_B[:, :])
    qg.dma(maskR_s[:, :], maskR[:, :])
    qg.dma(cumA_s[:, :], cumAall[:, :])
    qg.dma(alog_s[:, :], alog[:, :])
    qg.dma(B_s[:, :, :], bc_h[0:G * N, :].rearrange("(g n) t -> n g t", n=128))
    lBC = qg.dma(C_s[:, :, :], bc_h[G * N:2 * G * N, :].rearrange("(g n) t -> n g t", n=128))
    ldt = qs.dma(dt_s[:, :], dt_h[:, :])

    a.wait(qg, lBC)
    a.op("activation", A_col[:, :], alog_s[:, :], AF.Exp)
    v.wait(qs, ldt)
    v.wait(a, a.n)
    v.op("memset", zeros_s[:, :], 0.0)
    v.op("memset", zer8[:, :], 0.0)
    v.op("tensor_scalar", A_col[:, :], A_col[:, :], -1.0, None, ALU.mult)
    v.op("tensor_scalar", a_s[:, :], dt_s[:, :], A_col[:, 0:1], None, ALU.mult)
    for c in range(NCH):
        v.op("tensor_tensor_scan", Acum_s[:, c * CS:(c + 1) * CS],
             a_s[:, c * CS:(c + 1) * CS], zeros_s[:, :], 0.0, ALU.add, ALU.add)
    v.op("tensor_copy", dtb16[:, :], dt_s[:, :])
    pe.raw(lambda eng, val=v.n: eng.wait_ge(v.sem, val))
    for c in range(NCH):
        pe.op("transpose", psT[:, :], Acum_s[:, c * CS:(c + 1) * CS], identf_s[:, :])
        pe.mark()
        v.wait(pe, pe.n)
        v.op("tensor_copy", AcumT_s[:, c, :], psT[:, :])
        pe.raw(lambda eng, val=v.n: eng.wait_ge(v.sem, val))
        pe.op("transpose", psT[:, :], dt_s[:, c * CS:(c + 1) * CS], identf_s[:, :])
        pe.mark()
        v.wait(pe, pe.n)
        v.op("tensor_copy", dtT_s[:, c, :], psT[:, :])
        pe.raw(lambda eng, val=v.n: eng.wait_ge(v.sem, val))
    a.wait(v, v.n)
    for c in range(NCH):
        a.op("activation", exppT_s[:, c, :], AcumT_s[:, c, :], AF.Exp)
    qs.wait(v, v.n)
    fl = 0
    for c in range(NCH):
        fl = qs.dma(acumflat_h[c, :].rearrange("(h l) -> h l", h=128),
                    Acum_s[:, c * CS:(c + 1) * CS])
    qg.wait(qs, fl)

    v.op("tensor_tensor_scan", Tall_s[:, :], cumA_s[:, :], zer8[:, :], 0.0,
         ALU.add, ALU.add)
    v.op("tensor_tensor", wj_s[:, :], cumA_s[:, :], maskR_s[:, :], ALU.mult)
    v.op("reduce_sum", Tsel_s[:, :], wj_s[:, :], axis=AX.X)
    v.op("tensor_scalar", wj_s[:, :], Tall_s[:, :], Tsel_s[:, 0:1], None, ALU.subtract)
    v.op("tensor_scalar", wj_s[:, :], wj_s[:, :], 0.0, None, ALU.max)
    a.wait(v, v.n)
    a.op("activation", wj_s[:, :], wj_s[:, :], AF.Exp, scale=-1.0)
    v.wait(a, a.n)
    v.op("tensor_tensor", wj_s[:, :], wj_s[:, :], maskR_s[:, :], ALU.mult)
    qs.wait(v, v.n)
    f1 = qs.dma(wjflat_h[0, :].rearrange("(h j) -> h j", h=128), wj_s[:, :])
    qs.wait(qs, f1)
    f2 = qs.dma(wjB_s[:, :], wjflat_h[0:1, :].broadcast_to([128, 1024]))
    totv = Acum_s[:, :].rearrange("p (c l) -> p c l", c=NCH)[:, :, CS - 1:CS].squeeze(2)
    v.op("tensor_tensor_scan", Epr_s[:, :], totv, zer8[:, :], 0.0, ALU.add, ALU.add)
    a.wait(v, v.n)
    a.op("activation", Epr_s[:, 1:8], Epr_s[:, 0:7], AF.Exp)
    v.wait(a, a.n)
    v.op("memset", Epr_s[:, 0:1], 1.0)
    qs.wait(v, v.n)
    f3 = qs.dma(eprflat_h[0, :].rearrange("(h c) -> h c", h=128), Epr_s[:, :])
    qs.wait(qs, f3)
    f4 = qs.dma(EprB_s[:, :], eprflat_h[0:1, :].broadcast_to([128, 1024]))
    v.wait(qs, f4)

    for g in range(G):
        v.op("memset", carry_f[:, :], 0.0)
        for j in range(8):
            lz = qs.dma(zj_s[:, :], Zall_h[j, :, g * 1024:(g + 1) * 1024])
            v.wait(qs, lz)
            wv_ = wjB_s[:, :].rearrange("p (h j) -> p h j", h=128)[:, g * 16:(g + 1) * 16, j:j + 1]
            v.op("tensor_tensor", zj_s[:, :].rearrange("p (h q) -> p h q", h=16),
                 zj_s[:, :].rearrange("p (h q) -> p h q", h=16),
                 wv_.broadcast_to([128, 16, 64]), ALU.mult)
            v.op("tensor_tensor", carry_f[:, :], carry_f[:, :], zj_s[:, :], ALU.add)
            qs.wait(v, v.n)
        v.op("tensor_copy", carry_all[:, g, :], carry_f[:, :])

    tiles = [(c, g) for c in range(NCH) for g in range(G)]
    loads = {}
    vdone_t = {}

    def issue_loads(i):
        c, g = tiles[i]
        p = i % 2
        if i >= 2:
            qs.wait(v, vdone_t[i - 2])
            qg.wait(v, vdone_t[i - 2])
        lx = qs.dma(xtm_c2[p][:, :], xtm_h[c * 128:(c + 1) * 128, g * 1024:(g + 1) * 1024])
        lr = qs.dma(Rloc_b2[p][:, :], Rloc_h[c, :, g * 1024:(g + 1) * 1024])
        la = qg.dma(AROW_s2[p][:, :],
                    acumflat_h[c:c + 1, g * 2048:(g + 1) * 2048].broadcast_to([128, 2048]))
        loads[i] = (lx, lr, la)

    issue_loads(0)
    for ti_ in range(len(tiles)):
        c, g = tiles[ti_]
        if True:
            p = ti_ % 2
            xtm_c, Rloc_b, AROW_s = xtm_c2[p], Rloc_b2[p], AROW_s2[p]
            lx, lr, la = loads[ti_]
            if ti_ + 1 < len(tiles):
                issue_loads(ti_ + 1)
            v.wait(qg, la)
            acv = AcumT_s[:, c, g * 16:(g + 1) * 16].unsqueeze(2).broadcast_to([128, 16, 128])
            v.op("tensor_tensor", AROW_s[:, :].rearrange("p (h l) -> p h l", h=16),
                 AROW_s[:, :].rearrange("p (h l) -> p h l", h=16), acv, ALU.subtract)
            v.op("tensor_scalar", AROW_s[:, :], AROW_s[:, :], 0.0, None, ALU.min)
            a.wait(v, v.n)
            a.op("activation", Lsl_s[:, :], AROW_s[:, :], AF.Exp)
            pe.raw(lambda eng, val=v.n: eng.wait_ge(v.sem, val))
            pe.mm(psG[:, :], B_s[:, g, c * CS:(c + 1) * CS],
                  C_s[:, g, c * CS:(c + 1) * CS], start=True, stop=True)
            pe.mark()
            v.wait(pe, pe.n)
            v.op("tensor_tensor", gm_s[:, :], psG[:, :], tri_s[:, :], ALU.mult)
            v.wait(a, a.n)
            gmv = gm_s[:, :].unsqueeze(1).broadcast_to([128, 16, 128])
            v.op("tensor_tensor", M_s[:, :].rearrange("p (h l) -> p h l", h=16),
                 gmv, Lsl_s[:, :].rearrange("p (h l) -> p h l", h=16), ALU.mult)
            v.wait(qs, lx)
            dtv = dtT_s[:, c, g * 16:(g + 1) * 16].unsqueeze(2).broadcast_to([128, 16, 64])
            v.op("tensor_tensor", xdt_s[:, :].rearrange("p (h q) -> p h q", h=16),
                 xtm_c[:, :].rearrange("p (h q) -> p h q", h=16), dtv, ALU.mult)
            v.wait(qs, lr)
            epv = EprB_s[:, :].rearrange("p (h cc) -> p h cc", h=128)[:, g * 16:(g + 1) * 16, c:c + 1]
            v.op("tensor_tensor", Rf_s[:, :].rearrange("p (h q) -> p h q", h=16),
                 carry_all[:, g, :].rearrange("p (h q) -> p h q", h=16),
                 epv.broadcast_to([128, 16, 64]), ALU.mult)
            v.op("tensor_tensor", Rf_s[:, :], Rf_s[:, :], Rloc_b[:, :], ALU.add)
            v.op("tensor_copy", rfb_s[:, :], Rf_s[:, :])
            pe.raw(lambda eng, val=v.n: eng.wait_ge(v.sem, val))
            for hl in range(16):
                pe.mm(psY[:, hl * 64:(hl + 1) * 64], M_s[:, hl * 128:(hl + 1) * 128],
                      xdt_s[:, hl * 64:(hl + 1) * 64], start=True, stop=True)
            for hl in range(16):
                pe.mm(psO[:, hl * 64:(hl + 1) * 64], C_s[:, g, c * CS:(c + 1) * CS],
                      rfb_s[:, hl * 64:(hl + 1) * 64], start=True, stop=True)
            pe.mark()
            v.wait(pe, pe.n)
            epx = exppT_s[:, c, g * 16:(g + 1) * 16].unsqueeze(2).broadcast_to([128, 16, 64])
            v.op("tensor_tensor", yt1_s[:, :].rearrange("p (h q) -> p h q", h=16),
                 psO[:, :].rearrange("p (h q) -> p h q", h=16), epx, ALU.mult)
            v.op("tensor_tensor", y1_s[:, :], yt1_s[:, :], psY[:, :], ALU.add)
            for hl in range(16):
                hglob = g * 16 + hl
                v.op("scalar_tensor_tensor", ycs[:, g, hl * 64:(hl + 1) * 64],
                     xtm_c[:, hl * 64:(hl + 1) * 64], DB_s[:, hglob:hglob + 1],
                     y1_s[:, hl * 64:(hl + 1) * 64], ALU.mult, ALU.add)
            pe.raw(lambda eng, val=v.n: eng.wait_ge(v.sem, val))
            vdone_t[ti_] = v.n
            if g != G - 1:
                continue
        a.wait(v, v.n)
        for g in range(G):
            a.op("activation", yt1_s[:, :], ycs[:, g, :], AF.Square,
                 accum_out=ssqc[:, g:g + 1])
        v.wait(a, a.n)
        v.op("reduce_sum", ssq[:, :], ssqc[:, :], axis=AX.X)
        v.op("tensor_scalar", ssq[:, :], ssq[:, :], 1.0 / INTER, EPS, ALU.mult, ALU.add)
        a.wait(v, v.n)
        a.op("activation", rstd[:, :], ssq[:, :], AF.Sqrt)
        v.wait(a, a.n)
        v.op("reciprocal", rstd[:, :], rstd[:, :])
        qs.wait(v, v.n)
        lg_ = qs.dma(ggc_s[:, :, :],
                     gate_h[:, c * 128:(c + 1) * 128].rearrange("(a r) t -> r a t", r=128))
        tx = 0
        for g in range(G):
            v.op("tensor_scalar", ytn_g[:, :], ycs[:, g, :], rstd[:, 0:1], None, ALU.mult)
            qs.wait(v, v.n)
            tx = qs.dmat(yfm_s[:, g * 8:(g + 1) * 8, :],
                         ytn_g[:, :].rearrange("t (a r) -> t a r", a=8))
            v.wait(qs, tx)
        v.wait(qs, lg_)
        v.op("tensor_tensor", zfm_s[:, :, :], yfm_s[:, :, :], ggc_s[:, :, :], ALU.mult)
        qs.wait(v, v.n)
        sz = qs.dma(z_h[:, c * 128:(c + 1) * 128].rearrange("(a r) t -> r a t", r=128),
                    zfm_s[:, :, :])
        v.wait(qs, sz)

    ctx.finish_outputs()
    return ctx


def build_phaseA2b(ctx=None):
    if ctx is None:
        ctx = Ctx("phaseA2b")
    x_in = ctx.din("x_in", (HT, DIM))
    z_h = ctx.din("z", (INTER, T), BF16)
    WT = ctx.din("W_outT", (INTER, DIM))
    x1 = ctx.dout("x1", (T, DIM))
    wo_h = ctx.dram("wo_h", (DIM, T), BF16)

    zT_s = ctx.sb("zT_s", [128, 64, 512], BF16)
    wb = [ctx.sb("wb0", [128, 64, 128], BF16), ctx.sb("wb1", [128, 64, 128], BF16)]
    stage = ctx.sb("stage", [128, 512], BF16)
    ln = LnKit(ctx, "a2")
    psA = ctx.ps("psA", [128, 512], F32)
    psB = ctx.ps("psB", [128, 512], F32)

    pe, v, a, qs, qg = ctx.pe, ctx.v, ctx.a, ctx.qs, ctx.qg

    marks = {}
    vdone = {}
    for th in range(2):
        lz = qs.dma(zT_s[:, :, :],
                    z_h[:, th * 512:(th + 1) * 512].rearrange("(a k) t -> k a t", k=128))
        qs.wait(pe, pe.n)
        pe.raw(lambda eng, val=lz: eng.wait_ge(qs.sem, val))
        for d in range(32):
            key = (th, d)
            if d >= 2:
                qg.wait(pe, marks[(th, d - 2)])
            ld = qg.dma(wb[d % 2][:, :, :],
                        WT[:, d * 128:(d + 1) * 128].rearrange("(a k) m -> k a m", k=128))
            pe.raw(lambda eng, val=ld: eng.wait_ge(qg.sem, val))
            if d >= 2:
                pe.raw(lambda eng, val=vdone[(th, d - 2)]: eng.wait_ge(v.sem, val))
            ps = psA if d % 2 == 0 else psB
            for k in range(64):
                pe.mm(ps[:, :], wb[d % 2][:, k, :], zT_s[:, k, :],
                      start=(k == 0), stop=(k == 63))
            pe.mark()
            marks[key] = pe.n
            v.wait(pe, pe.n)
            v.op("tensor_copy", stage[:, :], ps[:, :])
            qs.wait(v, v.n)
            st = qs.dma(wo_h[d * 128:(d + 1) * 128, th * 512:(th + 1) * 512], stage[:, :])
            v.wait(qs, st)
            vdone[key] = v.n

    emit_post_ln_residual(ctx, ln, wo_h, x_in[HALO:HALO + T, :], x1)
    ctx.finish_outputs()
    return ctx




def build_mid1():
    ctx = Ctx("mid1", final_outputs={"x1"})
    m = ctx.phase_mark()
    build_phaseA2a(ctx)
    ctx.phase_free(m)
    build_phaseA2b(ctx)
    return ctx


def build_mid2():
    ctx = Ctx("mid2", final_outputs={"x2", "q_tm", "k_tm", "v_tm"})
    m = ctx.phase_mark()
    build_phaseB(ctx)
    ctx.phase_free(m)
    build_phaseC1(ctx)
    return ctx


def build_tail():
    ctx = Ctx("tail", final_outputs={"x4"})
    m = ctx.phase_mark()
    build_phaseC2(ctx)
    ctx.phase_free(m)
    build_phaseD(ctx)
    return ctx



_BUILT = {}


def _get(name):
    if name not in _BUILT:
        fn = globals().get("build_phase" + name) or globals()["build_" + name]
        ctx = fn()
        nc = ctx.build()
        _BUILT[name] = (ctx, nc)
    return _BUILT[name]


def kernel(**inputs):
    com = prep_common(inputs)
    cores = [prep_core(c, inputs) for c in range(NCORE)]
    live = [dict() for _ in range(NCORE)]

    def resolve(ctx, c):
        m = {}
        for n in ctx.input_names:
            if n in live[c]:
                m[n] = live[c][n]
            elif n == "kprev_tm":
                m[n] = live[c - 1]["k_tm"] if c > 0 else np.zeros((T, DIM), BF)
            elif n == "vprev_tm":
                m[n] = live[c - 1]["v_tm"] if c > 0 else np.zeros((T, DIM), BF)
            elif n == "Z_all":
                m[n] = np.ascontiguousarray(np.stack([live[j]["Z"] for j in range(NCORE)], 0))
            elif n == "cumA_all":
                m[n] = np.ascontiguousarray(
                    np.concatenate([live[j]["cumA"] for j in range(NCORE)], 1))
            elif n in com:
                m[n] = com[n]
            else:
                m[n] = cores[c][n]
        return m

    def launch(name, renames=None):
        ctx, nc = _get(name)
        in_maps = [resolve(ctx, c) for c in range(NCORE)]
        res = run_bass_kernel_spmd(nc, in_maps, core_ids=list(range(NCORE)))
        for c in range(NCORE):
            for on in ctx.output_names:
                key = renames.get(on, on) if renames else on
                live[c][key] = np.asarray(res.results[c][on])
        return res

    launch("A1")
    launch("mid1")
    launch("mid2")
    launch("tail")
    out = np.concatenate([live[c]["x4"] for c in range(NCORE)], 0)
    return out.reshape(1, 8192, DIM).astype(np.float32)


# version 05
# speedup vs baseline: 1.1421x; 1.0288x over previous
import numpy as np
import ml_dtypes
import concourse.bass as bass
import concourse.mybir as mybir
from concourse.bass_utils import run_bass_kernel_spmd

F32 = mybir.dt.float32
BF16 = mybir.dt.bfloat16
AF = mybir.ActivationFunctionType
ALU = mybir.AluOpType
AX = mybir.AxisListType

NCORE = 8
T = 1024
HT = 1056
HALO = 32
DIM = 4096
H, P, G, N, CS, KCONV = 128, 64, 8, 128, 128, 4
INTER = 8192
CONV_DIM = INTER + 2 * G * N
NPROJ = INTER + CONV_DIM + H
NCH = T // CS
EPS = 1e-5
NH, HD = 32, 128
WINDOW = 1024
BF = ml_dtypes.bfloat16



class EngT:

    def __init__(self, name):
        self.name = name
        self.n = 0
        self.sem = None
        self.cmds = []

    def op(self, method, *args, **kwargs):
        prev = self.n
        self.n += 1

        def fn(eng, prev=prev, method=method, args=args, kwargs=kwargs):
            if prev > 0:
                eng.wait_ge(self.sem, prev)
            getattr(eng, method)(*args, **kwargs).then_inc(self.sem, 1)
        self.cmds.append(fn)
        return self.n

    def wait(self, other, val):
        if val <= 0:
            return

        def fn(eng, other=other, val=val):
            eng.wait_ge(other.sem, val)
        self.cmds.append(fn)

    def raw(self, fn):
        self.cmds.append(fn)


class PeT(EngT):

    def op(self, method, *args, mark=False, **kwargs):
        if mark:
            self.n += 1

            def fn(eng, method=method, args=args, kwargs=kwargs):
                getattr(eng, method)(*args, **kwargs).then_inc(self.sem, 1)
        else:
            def fn(eng, method=method, args=args, kwargs=kwargs):
                getattr(eng, method)(*args, **kwargs)
        self.cmds.append(fn)
        return self.n

    def mm(self, out, lhsT, rhs, start, stop, mark=False):
        return self.op("matmul", out, lhsT, rhs, start=start, stop=stop, mark=mark)

    def mark(self):
        self.n += 1

        def fn(eng):
            eng.drain().then_inc(self.sem, 1)
        self.cmds.append(fn)
        return self.n


class QT:

    def __init__(self, name):
        self.name = name
        self.n = 0
        self.sem = None
        self.cmds = []

    def dma(self, out, in_, **kw):
        self.n += 16

        def fn(eng, out=out, in_=in_, kw=kw):
            eng.dma_start(out=out, in_=in_, **kw).then_inc(self.sem, 16)
        self.cmds.append(fn)
        return self.n

    def dmat(self, out, in_):
        self.n += 16

        def fn(eng, out=out, in_=in_):
            eng.dma_start_transpose(out, in_).then_inc(self.sem, 16)
        self.cmds.append(fn)
        return self.n

    def wait(self, other, val):
        if val <= 0:
            return

        def fn(eng, other=other, val=val):
            eng.wait_ge(other.sem, val)
        self.cmds.append(fn)

    def raw(self, fn):
        self.cmds.append(fn)


class Ctx:
    def __init__(self, name, final_outputs=()):
        self.name = name
        self.nc = bass.Bass("TRN2", target_bir_lowering=False)
        self.pe = PeT("pe")
        self.v = EngT("v")
        self.a = EngT("a")
        self.qs = QT("qs")
        self.qg = QT("qg")
        self.qa = QT("qa")
        self._guards = []
        self.input_names = []
        self.output_names = []
        self._live_dram = {}
        self._final_outputs = set(final_outputs)
        self._uniq = 0

    def din(self, name, shape, dtype=F32):
        if name in self._live_dram:
            return self._live_dram[name]
        t = self.nc.dram_tensor(name, list(shape), dtype, kind="ExternalInput")
        self.input_names.append(name)
        ap = t.ap()
        self._live_dram[name] = ap
        return ap

    def dout(self, name, shape, dtype=F32):
        if self._final_outputs and name not in self._final_outputs:
            t = self.nc.dram_tensor(name, list(shape), dtype, kind="Internal")
        else:
            t = self.nc.dram_tensor(name, list(shape), dtype, kind="ExternalOutput")
            self.output_names.append(name)
        ap = t.ap()
        self._live_dram[name] = ap
        return ap

    def phase_mark(self):
        return len(self._guards)

    def phase_free(self, mark):
        for g in reversed(self._guards[mark:]):
            g.__exit__(None, None, None)
        del self._guards[mark:]
        streams = [self.pe, self.v, self.a, self.qs, self.qg, self.qa]
        for wtr in streams:
            for p in streams:
                if p is not wtr:
                    wtr.wait(p, p.n)

    def dram(self, name, shape, dtype=F32):
        self._uniq += 1
        name = f"{name}_{self._uniq}"
        return self.nc.dram_tensor(name, list(shape), dtype, kind="Internal").ap()

    def sb(self, name, shape, dtype):
        self._uniq += 1
        name = f"{name}_{self._uniq}"
        g = self.nc.sbuf_tensor(name, list(shape), dtype)
        t = g.__enter__()
        self._guards.append(g)
        return t

    def ps(self, name, shape, dtype=F32):
        self._uniq += 1
        name = f"{name}_{self._uniq}"
        g = self.nc.psum_tensor(name, list(shape), dtype)
        t = g.__enter__()
        self._guards.append(g)
        return t

    def barrier(self, waiter, producers=None):
        if producers is None:
            producers = [e for e in (self.pe, self.v, self.a, self.qs, self.qg, self.qa)
                         if e is not waiter]
        for p in producers:
            waiter.wait(p, p.n)

    def finish_outputs(self):
        self.qs.wait(self.qs, self.qs.n)
        self.qg.wait(self.qg, self.qg.n)
        self.qa.wait(self.qa, self.qa.n)

    def build(self):
        nc = self.nc
        streams = [self.pe, self.v, self.a, self.qs, self.qg, self.qa]
        semg = []
        for s in streams:
            g = nc.semaphore("sem_" + s.name)
            s.sem = g.__enter__()
            semg.append(g)
        blockg = nc.Block()
        block = blockg.__enter__()

        def run(dec, obj):
            @dec
            def _(eng):
                for fn in obj.cmds:
                    fn(eng)
        run(block.tensor, self.pe)
        run(block.vector, self.v)
        run(block.scalar, self.a)
        run(block.sync, self.qs)
        run(block.gpsimd, self.qg)
        run(block.scalar, self.qa)
        blockg.__exit__(None, None, None)
        for g in semg:
            g.__exit__(None, None, None)
        for g in reversed(self._guards):
            g.__exit__(None, None, None)
        return nc


def cast_np(a, dtype=np.float32):
    return np.ascontiguousarray(np.asarray(a), dtype=dtype)



def prep_common(w):
    c = {}
    c["W_inT"] = cast_np(w["L0_attn_in_proj_w"].T)
    c["conv_wR"] = cast_np(w["L0_attn_conv_w"][:, 0, :].reshape(80, 128, KCONV).transpose(1, 0, 2))
    c["conv_bR"] = cast_np(w["L0_attn_conv_b"].reshape(80, 128).T)
    c["dt_bias"] = cast_np(w["L0_attn_dt_bias"].reshape(128, 1))
    c["A_log"] = cast_np(w["L0_attn_A_log"].reshape(128, 1))
    c["Dvec"] = cast_np(w["L0_attn_D"].reshape(128, 1))
    c["norm_wR"] = cast_np(w["L0_attn_norm_w"].reshape(64, 128).T)
    c["W_outT"] = cast_np(w["L0_attn_out_proj_w"].T)
    c["F0_WT"] = cast_np(w["L0_ffn_W"].T)
    c["F0_biasR"] = cast_np(w["L0_ffn_bias"].reshape(32, 128).T)
    c["F0_AcatT"] = cast_np(w["L0_ffn_A"].reshape(256, DIM).T)
    c["F0_Bcat"] = cast_np(w["L0_ffn_Bf"].transpose(0, 2, 1).reshape(256, DIM))
    c["F0_coefR"] = cast_np(np.repeat(w["L0_ffn_coeffs"], 16).reshape(2, 128).T)
    c["F0_scalR"] = cast_np(np.repeat(w["L0_ffn_scales"], 16).reshape(2, 128).T)
    c["WqT"] = cast_np(w["L1_attn_Wq"].T)
    c["WkT"] = cast_np(w["L1_attn_Wk"].T)
    c["WvT"] = cast_np(w["L1_attn_Wv"].T)
    c["WoT"] = cast_np(w["L1_attn_Wo"].T)
    c["F1_WT"] = cast_np(w["L1_ffn_W_base"].T)
    c["F1_biasR"] = cast_np(w["L1_ffn_b_base"].reshape(32, 128).T)
    c["F1_WrT"] = cast_np(w["L1_ffn_W_router"].T)
    c["F1_AcatT"] = cast_np(w["L1_ffn_A"].reshape(128, DIM).T)
    c["F1_Bcat"] = cast_np(w["L1_ffn_B"].transpose(0, 2, 1).reshape(128, DIM))
    E8 = np.zeros((8, 128), np.float32)
    for e in range(8):
        E8[e, e * 16:(e + 1) * 16] = 1.0
    c["E8"] = cast_np(E8, BF)
    c["D_B"] = cast_np(np.broadcast_to(w["L0_attn_D"].reshape(1, 128), (128, 128)))
    c["ident_f"] = cast_np(np.eye(128))
    c["onescol"] = cast_np(np.ones((128, 1)), BF)
    c["onesrow"] = cast_np(np.ones((1, 128)), BF)
    c["ident"] = cast_np(np.eye(128), BF)
    c["tri"] = cast_np(np.triu(np.ones((128, 128))), BF)
    return c


def prep_core(ci, w):
    x = w["L0_attn_hidden_states"][0]
    d = {}
    t0 = ci * T
    halo = np.zeros((HALO, DIM), np.float32) if ci == 0 else x[t0 - HALO:t0]
    d["x_in"] = cast_np(np.concatenate([halo, x[t0:t0 + T]], 0))
    cm = np.zeros((1, 8), np.float32)
    cm[0, :ci] = 1.0
    d["carrymask"] = cm
    d["carrymaskR"] = cast_np(np.broadcast_to(cm, (128, 8)))
    pos = np.arange(t0, t0 + T, dtype=np.float64)
    inv = 1.0 / 10000.0 ** (np.arange(0, HD, 2) / HD)
    fr = pos[:, None] * inv[None, :]
    emb = np.concatenate([fr, fr], -1)
    cos = np.cos(emb)
    sin = np.sin(emb)
    sin_signed = np.concatenate([-sin[:, :HD // 2], sin[:, HD // 2:]], -1)
    d["rope_cos"] = cast_np(cos.reshape(8, 128, HD).transpose(1, 0, 2))
    d["rope_sin"] = cast_np(sin_signed.reshape(8, 128, HD).transpose(1, 0, 2))
    m = np.zeros((2, 12, 128, 512), np.float32)
    for qg_ in range(2):
        q0 = qg_ * 512
        for kb in range(12):
            kstart = q0 - 1024 + kb * 128
            kidx = np.arange(128)[:, None] + kstart
            qidx = np.arange(512)[None, :] + q0
            rel = qidx - kidx
            ok = (rel >= 0) & (rel < WINDOW)
            if ci == 0:
                ok = ok & (kidx >= 0)
            m[qg_, kb] = ok
    d["wmask"] = cast_np(m, BF)
    return d



class LnKit:

    def __init__(self, ctx, tag="", slim=False):
        self.ctx = ctx
        self.x = ctx.sb(f"ln{tag}_x", [128, DIM], F32)
        self.sq = ctx.sb(f"ln{tag}_sq", [128, DIM], F32)
        self.u = None if slim else ctx.sb(f"ln{tag}_u", [128, DIM], BF16)
        self.s1 = ctx.sb(f"ln{tag}_s1", [128, 1], F32)
        self.s2 = ctx.sb(f"ln{tag}_s2", [128, 1], F32)
        self.mneg = ctx.sb(f"ln{tag}_m", [128, 1], F32)
        self.msq = ctx.sb(f"ln{tag}_m2", [128, 1], F32)
        self.var = ctx.sb(f"ln{tag}_v", [128, 1], F32)
        self.rstd = ctx.sb(f"ln{tag}_r", [128, 1], F32)
        self.last_tp = 0

    def stats_and_u(self, xt, rows, ubuf):
        ctx = self.ctx
        v, a = ctx.v, ctx.a
        v.op("reduce_sum", self.s1[0:rows, :], xt[0:rows, :], axis=AX.X)
        a.wait(v, v.n - 1)
        a.op("activation", self.sq[0:rows, :], xt[0:rows, :], AF.Square,
             accum_out=self.s2[0:rows, 0:1])
        v.op("tensor_scalar", self.mneg[0:rows, :], self.s1[0:rows, :],
             -1.0 / DIM, None, ALU.mult)
        v.op("tensor_tensor", self.msq[0:rows, :], self.mneg[0:rows, :],
             self.mneg[0:rows, :], ALU.mult)
        v.wait(a, a.n)
        v.op("tensor_scalar", self.var[0:rows, :], self.s2[0:rows, :],
             1.0 / DIM, EPS, ALU.mult, ALU.add)
        v.op("tensor_tensor", self.var[0:rows, :], self.var[0:rows, :],
             self.msq[0:rows, :], ALU.subtract)
        a.wait(v, v.n)
        a.op("activation", self.rstd[0:rows, :], self.var[0:rows, :], AF.Sqrt)
        v.wait(a, a.n)
        v.op("reciprocal", self.rstd[0:rows, :], self.rstd[0:rows, :])
        v.op("tensor_scalar", ubuf[0:rows, :], xt[0:rows, :],
             self.mneg[0:rows, 0:1], self.rstd[0:rows, 0:1], ALU.add, ALU.mult)
        return v.n

    def tile_from_hbm(self, x_h, row0, rows, uT=None, ucol=None, u_keep=None):
        ctx = self.ctx
        xt, ub = self.x, self.u
        ctx.qs.wait(ctx.v, ctx.v.n)
        ctx.qs.wait(ctx.a, ctx.a.n)
        ld = ctx.qs.dma(xt[0:rows, :], x_h[row0:row0 + rows, :])
        ctx.v.wait(ctx.qs, ld)
        ctx.a.wait(ctx.qs, ld)
        if self.last_tp:
            ctx.v.wait(ctx.qs, self.last_tp)
        done = self.stats_and_u(xt, rows, ub)
        if u_keep is not None:
            ctx.v.op("tensor_copy", u_keep[0:rows, :], ub[0:rows, :])
        if uT is not None:
            ctx.qs.wait(ctx.v, done)
            tp = ctx.qs.dmat(uT[:, :, ucol:ucol + rows],
                             ub[0:rows, :].rearrange("t (a r) -> t a r", a=32))
            self.last_tp = tp
        return xt


def emit_weight_gemm(ctx, wT_h, uT, n_k, do_tiles, n_cols, evac, wname,
                     extra_mms=None, pre_tile=None):
    pe, qg, v = ctx.pe, ctx.qg, ctx.v
    wb = [ctx.sb(f"{wname}0", [128, n_k, 128], BF16),
          ctx.sb(f"{wname}1", [128, n_k, 128], BF16)]
    psl = [ctx.ps(f"{wname}_psA", [128, 512], F32),
           ctx.ps(f"{wname}_psB", [128, 512], F32),
           ctx.ps(f"{wname}_psC", [128, 512], F32),
           ctx.ps(f"{wname}_psD", [128, 512], F32)]
    nsplit = [(i, min(512, n_cols - i)) for i in range(0, n_cols, 512)]
    assert len(nsplit) <= 2
    pe_done_at = {}
    v_done_at = {}
    for d in range(do_tiles):
        b = wb[d % 2]
        if d >= 2:
            qg.wait(pe, pe_done_at[d - 2])
        ld = qg.dma(b[:, :, :],
                    wT_h[:, d * 128:(d + 1) * 128].rearrange("(a k) m -> k a m", k=128))
        if pre_tile is not None:
            ld = pre_tile(d) or ld
        pe.raw(lambda eng, ld=ld: eng.wait_ge(qg.sem, ld))
        if d >= 2:
            pe.raw(lambda eng, val=v_done_at[d - 2]: eng.wait_ge(v.sem, val))
        pss = [psl[(2 * d) % 4], psl[(2 * d + 1) % 4]]
        for ci, (c0, cw) in enumerate(nsplit):
            ps = pss[ci]
            for k in range(n_k):
                pe.mm(ps[:, 0:cw], b[:, k, :], uT[:, k, c0:c0 + cw],
                      start=(k == 0), stop=(extra_mms is None and k == n_k - 1))
            if extra_mms is not None:
                extra_mms(d, ps, c0, cw)
        pe.mark()
        pe_done_at[d] = pe.n
        evac(d, pss, nsplit)
        v_done_at[d] = v.n
    return




def emit_post_ln_residual(ctx, ln, src_fm, x_old_h, x_new_h):
    v, a, qs = ctx.v, ctx.a, ctx.qs
    tm = ctx.sb("plr_tm_" + ctx.name, [128, DIM], BF16)
    xold = ctx.sb("plr_xold_" + ctx.name, [128, DIM], F32)
    nrm = ln.sq
    for i in range(8):
        qs.wait(v, v.n)
        tp = qs.dmat(tm[:, :], src_fm[:, i * 128:(i + 1) * 128].opt())
        ld = qs.dma(xold[:, :], x_old_h[i * 128:(i + 1) * 128, :])
        v.wait(qs, tp)
        v.wait(qs, ld)
        a.wait(qs, tp)
        a.wait(v, v.n)
        ln.stats_and_u(tm, 128, nrm)
        v.op("tensor_tensor", nrm[:, :], nrm[:, :], xold[:, :], ALU.add)
        qs.wait(v, v.n)
        qs.dma(x_new_h[i * 128:(i + 1) * 128, :], nrm[:, :])


def build_phaseB(ctx=None):
    if ctx is None:
        ctx = Ctx("phaseB")
    x1 = ctx.din("x1", (T, DIM))
    WT = ctx.din("F0_WT", (DIM, DIM))
    biasR = ctx.din("F0_biasR", (128, 32))
    AcatT = ctx.din("F0_AcatT", (DIM, 256))
    Bcat = ctx.din("F0_Bcat", (256, DIM))
    coefR = ctx.din("F0_coefR", (128, 2))
    scalR = ctx.din("F0_scalR", (128, 2))
    x2 = ctx.dout("x2", (T, DIM))
    ffn_h = ctx.dram("ffn_h", (DIM, T), BF16)

    uT = ctx.sb("uT", [128, 32, T], BF16)
    ln = LnKit(ctx, "b")
    bias_s = ctx.sb("bias_s", [128, 32], F32)
    coef_s = ctx.sb("coef_s", [128, 2], F32)
    scal_s = ctx.sb("scal_s", [128, 2], F32)
    AcT_s = ctx.sb("AcT_s", [128, 32, 256], BF16)
    bcb = [ctx.sb("bcb0", [128, 2, 128], BF16), ctx.sb("bcb1", [128, 2, 128], BF16)]
    xa = ctx.sb("xa", [128, 2, T], BF16)
    stage = ctx.sb("stage", [128, T], BF16)
    psX = ctx.ps("psX", [128, 512], F32)
    psY = ctx.ps("psY", [128, 512], F32)

    pe, v, a, qs, qg = ctx.pe, ctx.v, ctx.a, ctx.qs, ctx.qg

    qg.dma(bias_s[:, :], biasR[:, :])
    qg.dma(coef_s[:, :], coefR[:, :])
    qg.dma(scal_s[:, :], scalR[:, :])
    qg.dma(AcT_s[:, :, :], AcatT[:, :].rearrange("(a k) m -> k a m", k=128))

    for i in range(8):
        ln.tile_from_hbm(x1, i * 128, 128, uT=uT, ucol=i * 128)

    ctx.barrier(pe, [ctx.qs, ctx.v, ctx.qg])
    for er in range(2):
        for ci in range(2):
            ps = psX if ci == 0 else psY
            for k in range(32):
                pe.mm(ps[:, :], AcT_s[:, k, er * 128:(er + 1) * 128],
                      uT[:, k, ci * 512:(ci + 1) * 512], start=(k == 0), stop=(k == 31))
            pe.mark()
            v.wait(pe, pe.n)
            v.op("tensor_scalar", xa[:, er, ci * 512:(ci + 1) * 512],
                 ps[:, :], coef_s[:, er:er + 1], scal_s[:, er:er + 1],
                 ALU.mult, ALU.mult)
    pe.raw(lambda eng, val=v.n: eng.wait_ge(v.sem, val))

    def pre_tile(d):
        return qg.dma(bcb[d % 2][:, :, :],
                      Bcat[:, d * 128:(d + 1) * 128].rearrange("(c k) m -> k c m", k=128))

    def extra(d, ps, c0, cw):
        pe.mm(ps[:, 0:cw], bcb[d % 2][:, 0, :], xa[:, 0, c0:c0 + cw], start=False, stop=False)
        pe.mm(ps[:, 0:cw], bcb[d % 2][:, 1, :], xa[:, 1, c0:c0 + cw], start=False, stop=True)

    def evac(d, pss, nsplit):
        v.wait(pe, pe.n)
        for ci, (c0, cw) in enumerate(nsplit):
            v.op("tensor_scalar", stage[:, c0:c0 + cw], pss[ci][:, 0:cw],
                 bias_s[:, d:d + 1], None, ALU.add)
        qs.wait(v, v.n)
        st = qs.dma(ffn_h[d * 128:(d + 1) * 128, :], stage[:, :])
        v.wait(qs, st)

    emit_weight_gemm(ctx, WT, uT, 32, 32, T, evac, "w0", extra_mms=extra,
                     pre_tile=pre_tile)

    emit_post_ln_residual(ctx, ln, ffn_h, x1, x2)
    ctx.finish_outputs()
    return ctx




def build_phaseD(ctx=None, out_name="x4"):
    if ctx is None:
        ctx = Ctx("phaseD")
    x3 = ctx.din("x3", (T, DIM))
    WT = ctx.din("F1_WT", (DIM, DIM))
    biasR = ctx.din("F1_biasR", (128, 32))
    WrT = ctx.din("F1_WrT", (DIM, 8))
    AcatT = ctx.din("F1_AcatT", (DIM, 128))
    Bcat = ctx.din("F1_Bcat", (128, DIM))
    E8 = ctx.din("E8", (8, 128), BF16)
    ident = ctx.din("ident", (128, 128), BF16)
    x4 = ctx.dout(out_name, (T, DIM))
    ffn_h = ctx.dram("ffn_h", (DIM, T), BF16)

    uT = ctx.sb("uT", [128, 32, T], BF16)
    ln = LnKit(ctx, "d")
    bias_s = ctx.sb("bias_s", [128, 32], F32)
    WrT_s = ctx.sb("WrT_s", [128, 32, 8], BF16)
    AcT_s = ctx.sb("AcT_s", [128, 32, 128], BF16)
    Bc_s = ctx.sb("Bc_s", [128, 32, 128], BF16)
    E8_s = ctx.sb("E8_s", [8, 128], BF16)
    id_s = ctx.sb("id_s", [128, 128], BF16)
    lg = ctx.sb("lg", [128, 8], F32)
    m1 = ctx.sb("m1", [128, 1], F32)
    m2 = ctx.sb("m2", [128, 1], F32)
    ge1 = ctx.sb("ge1", [128, 8], F32)
    lg2 = ctx.sb("lg2", [128, 8], F32)
    ex = ctx.sb("ex", [128, 8], F32)
    den = ctx.sb("den", [128, 1], F32)
    rec = ctx.sb("rec", [128, 1], F32)
    wtm = ctx.sb("wtm", [128, 8], BF16)
    w_fm = ctx.sb("w_fm", [8, T], BF16)
    wex = ctx.sb("wex", [128, T], BF16)
    hpr = ctx.sb("hpr", [128, T], BF16)
    stage = ctx.sb("stage", [128, T], BF16)
    psR = ctx.ps("psR", [128, 8], F32)
    psT = ctx.ps("psT", [8, 128], BF16)
    psX = ctx.ps("psX", [128, 512], F32)
    psY = ctx.ps("psY", [128, 512], F32)

    pe, v, a, qs, qg = ctx.pe, ctx.v, ctx.a, ctx.qs, ctx.qg

    qg.dma(bias_s[:, :], biasR[:, :])
    qg.dma(WrT_s[:, :, :], WrT[:, :].rearrange("(a k) m -> k a m", k=128))
    qg.dma(AcT_s[:, :, :], AcatT[:, :].rearrange("(a k) m -> k a m", k=128))
    qg.dma(Bc_s[:, :, :], Bcat[:, :].rearrange("k (a m) -> k a m", m=128))
    qg.dma(E8_s[:, :], E8[:, :])
    qg.dma(id_s[:, :], ident[:, :])

    for i in range(8):
        ln.tile_from_hbm(x3, i * 128, 128, uT=uT, ucol=i * 128)

    ctx.barrier(pe, [ctx.qs, ctx.v, ctx.qg])
    for i in range(8):
        for k in range(32):
            pe.mm(psR[:, :], uT[:, k, i * 128:(i + 1) * 128], WrT_s[:, k, :],
                  start=(k == 0), stop=(k == 31))
        pe.mark()
        v.wait(pe, pe.n)
        v.op("tensor_copy", lg[:, :], psR[:, :])
        v.op("reduce_max", m1[:, :], lg[:, :], axis=AX.X)
        v.op("tensor_scalar", ge1[:, :], lg[:, :], m1[:, 0:1], None, ALU.is_ge)
        v.op("scalar_tensor_tensor", lg2[:, :], ge1[:, :], -1e30, lg[:, :],
             ALU.mult, ALU.add)
        v.op("reduce_max", m2[:, :], lg2[:, :], axis=AX.X)
        v.op("tensor_scalar", m1[:, :], m1[:, :], -1.0, None, ALU.mult)
        a.wait(v, v.n)
        a.op("activation", ex[:, :], lg[:, :], AF.Exp, bias=m1[:, 0:1])
        v.wait(a, a.n)
        v.op("tensor_scalar", ge1[:, :], lg[:, :], m2[:, 0:1], None, ALU.is_ge)
        v.op("tensor_tensor", ex[:, :], ex[:, :], ge1[:, :], ALU.mult)
        v.op("reduce_sum", den[:, :], ex[:, :], axis=AX.X)
        v.op("reciprocal", rec[:, :], den[:, :])
        v.op("tensor_scalar", wtm[:, :], ex[:, :], rec[:, 0:1], None, ALU.mult)
        pe.raw(lambda eng, val=v.n: eng.wait_ge(v.sem, val))
        pe.op("transpose", psT[:, :], wtm[:, :], id_s[:, :])
        pe.mark()
        v.wait(pe, pe.n)
        v.op("tensor_copy", w_fm[:, i * 128:(i + 1) * 128], psT[0:8, :])
    pe.raw(lambda eng, val=v.n: eng.wait_ge(v.sem, val))
    for ci in range(2):
        ps = psX if ci == 0 else psY
        pe.mm(ps[:, :], E8_s[:, :], w_fm[0:8, ci * 512:(ci + 1) * 512],
              start=True, stop=True)
        pe.mark()
        v.wait(pe, pe.n)
        v.op("tensor_scalar", wex[:, ci * 512:(ci + 1) * 512], ps[:, :],
             2.0, None, ALU.mult)
    for ci in range(2):
        ps = psX if ci == 0 else psY
        pe.raw(lambda eng, val=v.n: eng.wait_ge(v.sem, val))
        for k in range(32):
            pe.mm(ps[:, :], AcT_s[:, k, :], uT[:, k, ci * 512:(ci + 1) * 512],
                  start=(k == 0), stop=(k == 31))
        pe.mark()
        v.wait(pe, pe.n)
        v.op("tensor_tensor", hpr[:, ci * 512:(ci + 1) * 512], ps[:, :],
             wex[:, ci * 512:(ci + 1) * 512], ALU.mult)
    pe.raw(lambda eng, val=v.n: eng.wait_ge(v.sem, val))

    def extra(d, ps, c0, cw):
        pe.mm(ps[:, 0:cw], Bc_s[:, d, :], hpr[:, c0:c0 + cw], start=False, stop=True)

    def evac(d, pss, nsplit):
        v.wait(pe, pe.n)
        for ci, (c0, cw) in enumerate(nsplit):
            v.op("tensor_scalar", stage[:, c0:c0 + cw], pss[ci][:, 0:cw],
                 bias_s[:, d:d + 1], None, ALU.add)
        qs.wait(v, v.n)
        st = qs.dma(ffn_h[d * 128:(d + 1) * 128, :], stage[:, :])
        v.wait(qs, st)

    emit_weight_gemm(ctx, WT, uT, 32, 32, T, evac, "w1", extra_mms=extra)

    emit_post_ln_residual(ctx, ln, ffn_h, x3, x4)
    ctx.finish_outputs()
    return ctx




def emit_roleswap_gemm(ctx, wT_h, uT, out_h, wname, rope=None):
    pe, v, qs, qg = ctx.pe, ctx.v, ctx.qs, ctx.qg
    if not hasattr(ctx, "_rs_bufs"):
        ctx._rs_bufs = dict(
            wb=[ctx.sb("rs_wb0", [128, 32, 512], BF16),
                ctx.sb("rs_wb1", [128, 32, 512], BF16)],
            pss=[ctx.ps("rs_psA", [128, 512], F32),
                 ctx.ps("rs_psB", [128, 512], F32)],
            stage=ctx.sb("rs_st", [128, 512], BF16),
            t1=ctx.sb("rs_t1", [128, 512], F32),
            t2=ctx.sb("rs_t2", [128, 512], F32))
    wb = ctx._rs_bufs["wb"]
    pss = ctx._rs_bufs["pss"]
    stage = ctx._rs_bufs["stage"]
    t1 = ctx._rs_bufs["t1"]
    t2 = ctx._rs_bufs["t2"]
    qg.wait(pe, pe.n)
    qg.wait(v, v.n)
    pe_done = {}
    v_done = {}
    for ci in range(8):
        b = wb[ci % 2]
        if ci >= 2:
            qg.wait(pe, pe_done[ci - 2])
        ld = qg.dma(b[:, :, :],
                    wT_h[:, ci * 512:(ci + 1) * 512].rearrange("(a k) m -> k a m", k=128))
        pe.raw(lambda eng, ld=ld: eng.wait_ge(qg.sem, ld))
        for ti in range(8):
            ps = pss[ti % 2]
            if (ci, ti) != (0, 0) and (ci, ti) != (0, 1):
                prev = v_done.get((ci, ti - 2)) or v_done.get((ci - 1, ti + 6))
                if prev:
                    pe.raw(lambda eng, val=prev: eng.wait_ge(v.sem, val))
            for k in range(32):
                pe.mm(ps[:, :], uT[:, k, ti * 128:(ti + 1) * 128], b[:, k, :],
                      start=(k == 0), stop=(k == 31))
            pe.mark()
            pe_done[ci] = pe.n
            v.wait(pe, pe.n)
            if rope is None:
                v.op("tensor_copy", stage[:, :], ps[:, :])
            else:
                cos_s, sin_s = rope
                pv = ps[:, :].rearrange("t (h v f) -> t h v f", h=4, v=2)
                cl = cos_s[:, ti, 0:64].unsqueeze(1).broadcast_to([128, 4, 64])
                ch = cos_s[:, ti, 64:128].unsqueeze(1).broadcast_to([128, 4, 64])
                sl = sin_s[:, ti, 0:64].unsqueeze(1).broadcast_to([128, 4, 64])
                sh = sin_s[:, ti, 64:128].unsqueeze(1).broadcast_to([128, 4, 64])
                t1v = t1[:, :].rearrange("t (h v f) -> t h v f", h=4, v=2)
                v.op("tensor_tensor", t1v[:, :, 0, :], pv[:, :, 0, :], cl, ALU.mult)
                v.op("tensor_tensor", t1v[:, :, 1, :], pv[:, :, 1, :], ch, ALU.mult)
                t2v = t2[:, :].rearrange("t (h v f) -> t h v f", h=4, v=2)
                v.op("tensor_tensor", t2v[:, :, 0, :], pv[:, :, 1, :], sl, ALU.mult)
                v.op("tensor_tensor", t2v[:, :, 1, :], pv[:, :, 0, :], sh, ALU.mult)
                v.op("tensor_tensor", stage[:, :], t1[:, :], t2[:, :], ALU.add)
            qs.wait(v, v.n)
            st = qs.dma(out_h[ti * 128:(ti + 1) * 128, ci * 512:(ci + 1) * 512],
                        stage[:, :])
            v.wait(qs, st)
            v_done[(ci, ti)] = v.n


def build_phaseC1(ctx=None):
    if ctx is None:
        ctx = Ctx("phaseC1")
    x2 = ctx.din("x2", (T, DIM))
    WqT = ctx.din("WqT", (DIM, DIM))
    WkT = ctx.din("WkT", (DIM, DIM))
    WvT = ctx.din("WvT", (DIM, DIM))
    cosR = ctx.din("rope_cos", (128, 8, HD))
    sinR = ctx.din("rope_sin", (128, 8, HD))
    q_h = ctx.dout("q_tm", (T, DIM), BF16)
    k_h = ctx.dout("k_tm", (T, DIM), BF16)
    v_h = ctx.dout("v_tm", (T, DIM), BF16)

    uT = ctx.sb("uT", [128, 32, T], BF16)
    ln = LnKit(ctx, "c")
    cos_s = ctx.sb("cos_s", [128, 8, HD], F32)
    sin_s = ctx.sb("sin_s", [128, 8, HD], F32)

    qg = ctx.qg
    qg.dma(cos_s[:, :, :], cosR[:, :, :])
    qg.dma(sin_s[:, :, :], sinR[:, :, :])

    for i in range(8):
        ln.tile_from_hbm(x2, i * 128, 128, uT=uT, ucol=i * 128)
    ctx.barrier(ctx.pe, [ctx.qs, ctx.v, ctx.qg])

    emit_roleswap_gemm(ctx, WqT, uT, q_h, "wq", rope=(cos_s, sin_s))
    emit_roleswap_gemm(ctx, WkT, uT, k_h, "wk", rope=(cos_s, sin_s))
    emit_roleswap_gemm(ctx, WvT, uT, v_h, "wv", rope=None)
    ctx.finish_outputs()
    return ctx


def build_phaseC2(ctx=None):
    if ctx is None:
        ctx = Ctx("phaseC2")
    x2 = ctx.din("x2", (T, DIM))
    q_h = ctx.din("q_tm", (T, DIM), BF16)
    k_h = ctx.din("k_tm", (T, DIM), BF16)
    v_h = ctx.din("v_tm", (T, DIM), BF16)
    kp_h = ctx.din("kprev_tm", (T, DIM), BF16)
    vp_h = ctx.din("vprev_tm", (T, DIM), BF16)
    wmask = ctx.din("wmask", (2, 12, 128, 512), BF16)
    WoT = ctx.din("WoT", (DIM, DIM))
    onescol = ctx.din("onescol", (128, 1), BF16)
    onesrow = ctx.din("onesrow", (1, 128), BF16)
    x3 = ctx.dout("x3", (T, DIM))
    attn_h = ctx.dram("attn_h", (DIM, T), BF16)
    wo_h = ctx.dram("wo_h", (DIM, T), BF16)

    ln = LnKit(ctx, "c2", slim=True)
    mask_s = ctx.sb("mask_s", [128, 2, 12, 512], BF16)
    ones_s = ctx.sb("ones_s", [128, 1], BF16)
    onesr_s = ctx.sb("onesr_s", [1, 128], BF16)
    mk_attn = ctx.phase_mark()
    qh_s = [ctx.sb("qh0", [128, T], BF16), ctx.sb("qh1", [128, T], BF16)]
    kw_s = [ctx.sb("kw0", [128, 2048], BF16), ctx.sb("kw1", [128, 2048], BF16)]
    vw_s = [ctx.sb("vw0", [128, 16, 128], BF16), ctx.sb("vw1", [128, 16, 128], BF16)]
    probs = ctx.sb("probs", [128, 12, 512], BF16)
    sum_f = ctx.sb("sum_f", [1, 512], F32)
    rec_f = ctx.sb("rec_f", [1, 512], F32)
    rec_b = ctx.sb("rec_b", [1, 512], BF16)
    recB = ctx.sb("recB", [128, 512], BF16)
    ast = ctx.sb("ast", [128, 512], BF16)
    psS2 = [ctx.ps("psS", [128, 512], F32), ctx.ps("psSb", [128, 512], F32)]
    psSum = ctx.ps("psSum", [1, 512], F32)
    psAV = ctx.ps("psAV", [128, 512], F32)
    psRB = ctx.ps("psRB", [128, 512], F32)

    pe, v, a, qs, qg = ctx.pe, ctx.v, ctx.a, ctx.qs, ctx.qg

    qg.dma(mask_s[:, :, :, :], wmask[:, :, :, :].rearrange("g b k q -> k g b q"))
    qg.dma(ones_s[:, :], onescol[:, :])
    qg.dma(onesr_s[:, :], onesrow[:, :])

    SC = 1.0 / (HD ** 0.5)
    ld_done = {}
    v_done_head = {}
    pe_done_head = {}
    _aexp = {}
    for h in range(32):
        qb, kb_, vb = qh_s[h % 2], kw_s[h % 2], vw_s[h % 2]
        if h >= 2:
            qs.wait(v, v_done_head[h - 2])
            qs.wait(pe, pe_done_head[h - 2])
        l1 = qs.dmat(qb[:, :], q_h[:, h * 128:(h + 1) * 128].opt())
        l2 = qs.dmat(kb_[:, 0:1024], kp_h[:, h * 128:(h + 1) * 128].opt())
        l3 = qs.dmat(kb_[:, 1024:2048], k_h[:, h * 128:(h + 1) * 128].opt())
        l4 = qs.dma(vb[:, 0:8, :], vp_h[:, h * 128:(h + 1) * 128].rearrange("(b r) d -> r b d", r=128))
        l5 = qs.dma(vb[:, 8:16, :], v_h[:, h * 128:(h + 1) * 128].rearrange("(b r) d -> r b d", r=128))
        pe.raw(lambda eng, val=l5: eng.wait_ge(qs.sem, val))
        for qg_i in range(2):
            for kb in range(12):
                koff = qg_i * 512 + kb * 128
                psS = psS2[kb % 2]
                if _aexp.get(kb % 2):
                    pe.raw(lambda eng, val=_aexp[kb % 2]: eng.wait_ge(a.sem, val))
                pe.mm(psS[:, :], kb_[:, koff:koff + 128], qb[:, qg_i * 512:(qg_i + 1) * 512],
                      start=True, stop=True)
                pe.mark()
                a.wait(pe, pe.n)
                if kb > 0 or qg_i > 0 or h > 0:
                    a.wait(v, v.n)
                a.op("activation", probs[:, kb, :], psS[:, :], AF.Exp, scale=SC)
                _aexp[kb % 2] = a.n
                v.wait(a, a.n)
                v.op("tensor_tensor", probs[:, kb, :], probs[:, kb, :],
                     mask_s[:, qg_i, kb, :], ALU.mult)
            pv = v.n
            pe.raw(lambda eng, val=pv: eng.wait_ge(v.sem, val))
            for kb in range(12):
                pe.mm(psSum[:, :], ones_s[:, 0:1],
                      probs[:, kb, :], start=(kb == 0), stop=(kb == 11))
            for kb in range(12):
                koff = qg_i * 512 + kb * 128
                bidx = qg_i * 4 + kb
                pe.mm(psAV[:, :], vb[:, bidx, :],
                      probs[:, kb, :], start=(kb == 0), stop=(kb == 11))
            pe.mark()
            v.wait(pe, pe.n)
            v.op("tensor_copy", sum_f[:, :], psSum[0:1, :])
            v.op("reciprocal", rec_f[:, :], sum_f[:, :])
            v.op("tensor_copy", rec_b[:, :], rec_f[:, :])
            pe.raw(lambda eng, val=v.n: eng.wait_ge(v.sem, val))
            pe.mm(psRB[:, :], onesr_s[0:1, :], rec_b[0:1, :], start=True, stop=True)
            pe.mark()
            v.wait(pe, pe.n)
            v.op("tensor_copy", recB[:, :], psRB[:, :])
            v.op("tensor_tensor", ast[:, :], psAV[:, :], recB[:, :], ALU.mult)
            qs.wait(v, v.n)
            st = qs.dma(attn_h[h * 128:(h + 1) * 128, qg_i * 512:(qg_i + 1) * 512],
                        ast[:, :])
            v.wait(qs, st)
        v_done_head[h] = v.n
        pe_done_head[h] = pe.n

    ctx.phase_free(mk_attn)
    attn_s = ctx.sb("attn_s", [128, 32, T], BF16)
    la = qs.dma(attn_s[:, :, :], attn_h[:, :].rearrange("(a k) t -> k a t", k=128))
    pe.raw(lambda eng, val=la: eng.wait_ge(qs.sem, val))
    stage2 = ctx.sb("stage2", [128, T], BF16)

    def evac(d, pss2, nsplit):
        v.wait(pe, pe.n)
        for ci, (c0, cw) in enumerate(nsplit):
            v.op("tensor_copy", stage2[:, c0:c0 + cw], pss2[ci][:, 0:cw])
        qs.wait(v, v.n)
        st = qs.dma(wo_h[d * 128:(d + 1) * 128, :], stage2[:, :])
        v.wait(qs, st)

    emit_weight_gemm(ctx, WoT, attn_s, 32, 32, T, evac, "wo")
    emit_post_ln_residual(ctx, ln, wo_h, x2, x3)
    ctx.finish_outputs()
    return ctx




def build_phaseA1(ctx=None):
    if ctx is None:
        ctx = Ctx("phaseA1")
    x_in = ctx.din("x_in", (HT, DIM))
    WT = ctx.din("W_inT", (DIM, NPROJ))
    convw = ctx.din("conv_wR", (128, 80, KCONV))
    convb = ctx.din("conv_bR", (128, 80))
    dtb = ctx.din("dt_bias", (128, 1))
    alog = ctx.din("A_log", (128, 1))
    normw = ctx.din("norm_wR", (128, 64))
    ident = ctx.din("ident", (128, 128), BF16)
    onesr = ctx.din("onesrow", (1, 128), BF16)

    xtm_h = ctx.dout("xtm", (T, INTER), BF16)
    gate_h = ctx.dout("gate", (INTER, T), BF16)
    bc_h = ctx.dout("bc", (2 * G * N, T), BF16)
    dt_h = ctx.dout("dt", (128, T))
    Rloc_h = ctx.dout("Rloc", (NCH, 128, INTER))
    Z_h = ctx.dout("Z", (128, INTER))
    cumA_h = ctx.dout("cumA", (128, 1))
    decflat_h = ctx.dram("decflat", (1, 1024))

    uT = ctx.sb("uT", [128, 32, HT], BF16)
    ln = LnKit(ctx, "a")
    wb = [ctx.sb("wb0", [128, 32, 128], BF16), ctx.sb("wb1", [128, 32, 128], BF16)]
    ident_s = ctx.sb("ident_s", [128, 128], BF16)
    onesr_s = ctx.sb("onesr_s", [1, 128], BF16)
    convw_s = ctx.sb("convw_s", [128, 80, KCONV], F32)
    convb_s = ctx.sb("convb_s", [128, 80], F32)
    dtb_s = ctx.sb("dtb_s", [128, 1], F32)
    alog_s = ctx.sb("alog_s", [128, 1], F32)
    normw_s = ctx.sb("normw_s", [128, 64], F32)
    A_col = ctx.sb("A_col", [128, 1], F32)
    hbc_s = ctx.sb("hbc_s", [128, HT], BF16)
    diag_s = ctx.sb("diag_s", [128, 4, 128], BF16)
    act_s = ctx.sb("act_s", [128, T], BF16)
    dt_s = ctx.sb("dt_s", [128, T], F32)
    xb_s = ctx.sb("xb_s", [128, T], F32)
    a_s = ctx.sb("a_s", [128, T], F32)
    Acum_s = ctx.sb("Acum_s", [128, T], F32)
    zeros_s = ctx.sb("zeros_s", [128, CS], F32)
    dec_s = ctx.sb("dec_s", [128, T], F32)
    sp1_s = dec_s
    ddt_s = xb_s
    ddtT_s = ctx.sb("ddtT_s", [128, NCH, 128], F32)
    dec8_s = ctx.sb("dec8_s", [128, NCH], F32)
    decB_s = ctx.sb("decB_s", [128, 1024], F32)
    Btm_s = ctx.sb("Btm_s", [128, NCH, 128], BF16)
    xtmg_s = ctx.sb("xtmg_s", [128, NCH, 8, 128], BF16)
    xsl_s = ctx.sb("xsl_s", [128, NCH, 8, 128], BF16)
    Rg_s = ctx.sb("Rg_s", [128, 1024], F32)
    Rt_s = ctx.sb("Rt_s", [128, 1024], F32)
    Rst_s = ctx.sb("Rst_s", [128, 1024], F32)
    cumA_s = ctx.sb("cumA_s", [128, 1], F32)
    stage_s = None

    psI = [ctx.ps("psIa", [128, 512], F32), ctx.ps("psIb", [128, 512], F32)]
    psI2 = [ctx.ps("psIa2", [128, 512], F32), ctx.ps("psIb2", [128, 512], F32)]
    psIh = ctx.ps("psIh", [128, 32], F32)
    psC0 = ctx.ps("psCa", [128, 512], F32)
    psC = [psC0, psC0]
    psS = ctx.ps("psS", [128, 1024], F32)
    psT = psC0[:, 0:64].bitcast(BF16)

    pe, v, a, qs, qg = ctx.pe, ctx.v, ctx.a, ctx.qs, ctx.qg

    qg.dma(ident_s[:, :], ident[:, :])
    qg.dma(onesr_s[:, :], onesr[:, :])
    qg.dma(convw_s[:, :, :], convw[:, :, :])
    qg.dma(convb_s[:, :], convb[:, :])
    qg.dma(dtb_s[:, :], dtb[:, :])
    qg.dma(alog_s[:, :], alog[:, :])
    lc = qg.dma(normw_s[:, :], normw[:, :])

    a.wait(qg, lc)
    a.op("activation", A_col[:, :], alog_s[:, :], AF.Exp)
    v.wait(a, a.n)
    v.op("tensor_scalar", A_col[:, :], A_col[:, :], -1.0, None, ALU.mult)
    v.op("memset", zeros_s[:, :], 0.0)

    ln.tile_from_hbm(x_in, 0, HALO, uT=uT, ucol=0)
    for i in range(8):
        ln.tile_from_hbm(x_in, HALO + i * 128, 128, uT=uT, ucol=HALO + i * 128)
    ctx.barrier(pe, [ctx.qs, ctx.v, ctx.qg])

    NSsets = [[(32, 512, psI[0]), (544, 512, psI[1]), (0, 32, psIh)],
              [(32, 512, psI2[0]), (544, 512, psI2[1]), (0, 32, psIh)]]
    _ip_state = {"marks": {}, "seq": [], "rdr": {}}

    def inproj_tile(d, halo_too=True):
        seq = _ip_state["seq"]
        if len(seq) >= 2:
            qg.wait(pe, _ip_state["marks"][seq[-2]])
        b = wb[len(seq) % 2]
        ld = qg.dma(b[:, :, :],
                    WT[:, d * 128:(d + 1) * 128].rearrange("(a k) m -> k a m", k=128))
        pe.raw(lambda eng, val=ld: eng.wait_ge(qg.sem, val))
        if len(seq) >= 2:
            pv, pa = _ip_state["rdr"].get(seq[-2], (0, 0))
            pe.raw(lambda eng, val=pv: eng.wait_ge(v.sem, val))
            pe.raw(lambda eng, val=pa: eng.wait_ge(a.sem, val))
        NS = NSsets[len(seq) % 2]
        for (c0, cw, ps) in NS:
            if c0 == 0:
                if not halo_too:
                    continue
                if len(seq) >= 1 and seq[-1] in _ip_state["rdr"]:
                    pv, pa = _ip_state["rdr"][seq[-1]]
                    pe.raw(lambda eng, val=pv: eng.wait_ge(v.sem, val))
            for k in range(32):
                pe.mm(ps[:, 0:cw], b[:, k, :], uT[:, k, c0:c0 + cw],
                      start=(k == 0), stop=(k == 31))
        pe.mark()
        seq.append(d)
        _ip_state["marks"][d] = pe.n
        return NS

    def inproj_done(d):
        _ip_state["rdr"][d] = (v.n, a.n)

    def conv_and_silu(d, cidx):
        for k in range(KCONV):
            v.op("tensor_scalar", diag_s[:, k, :], ident_s[:, :],
                 convw_s[:, cidx:cidx + 1, k:k + 1].rearrange("p a b -> p (a b)"),
                 None, ALU.mult)
        pe.raw(lambda eng, val=v.n: eng.wait_ge(v.sem, val))
        for ci in range(2):
            pe.raw(lambda eng, val=a.n: eng.wait_ge(a.sem, val))
            for k in range(KCONV):
                pe.mm(psC[ci][:, :], diag_s[:, k, :],
                      hbc_s[:, 29 + k + ci * 512: 29 + k + ci * 512 + 512],
                      start=(k == 0), stop=(k == KCONV - 1))
            pe.mark()
            a.wait(pe, pe.n)
            a.op("activation", act_s[:, ci * 512:(ci + 1) * 512], psC[ci][:, :],
                 AF.Silu, bias=convb_s[:, cidx:cidx + 1])

    NS_ = inproj_tile(144, halo_too=False)
    v.wait(pe, pe.n)
    for ci in range(2):
        v.op("tensor_scalar", xb_s[:, ci * 512:(ci + 1) * 512], NS_[ci][2][:, :],
             dtb_s[:, 0:1], None, ALU.add)
    inproj_done(144)
    a.wait(v, v.n)
    a.op("activation", sp1_s[:, :], xb_s[:, :], AF.Abs)
    a.op("activation", sp1_s[:, :], sp1_s[:, :], AF.Exp, scale=-1.0)
    a.op("activation", sp1_s[:, :], sp1_s[:, :], AF.Ln, bias=1.0)
    v.wait(a, a.n)
    v.op("tensor_scalar", dt_s[:, :], xb_s[:, :], 0.0, None, ALU.max)
    v.op("tensor_tensor", dt_s[:, :], dt_s[:, :], sp1_s[:, :], ALU.add)
    qs.wait(v, v.n)
    qs.dma(dt_h[:, :], dt_s[:, :])
    v.op("tensor_scalar", a_s[:, :], dt_s[:, :], A_col[:, 0:1], None, ALU.mult)
    for c in range(NCH):
        v.op("tensor_tensor_scan", Acum_s[:, c * CS:(c + 1) * CS],
             a_s[:, c * CS:(c + 1) * CS], zeros_s[:, :], 0.0, ALU.add, ALU.add)
    v.op("reduce_sum", cumA_s[:, :],
         Acum_s[:, :].rearrange("p (c l) -> p c l", c=NCH)[:, :, CS - 1:CS].squeeze(2),
         axis=AX.X)
    qs.wait(v, v.n)
    qs.dma(cumA_h[:, :], cumA_s[:, :])
    for c in range(NCH):
        v.op("tensor_scalar", dec_s[:, c * CS:(c + 1) * CS],
             Acum_s[:, c * CS:(c + 1) * CS],
             Acum_s[:, c * CS + CS - 1:c * CS + CS], None, ALU.subtract)
    a.wait(v, v.n)
    a.op("activation", dec_s[:, :], dec_s[:, :], AF.Exp, scale=-1.0)
    v.wait(a, a.n)
    v.op("tensor_tensor", ddt_s[:, :], dec_s[:, :], dt_s[:, :], ALU.mult)
    v.op("tensor_copy", act_s[:, :], ddt_s[:, :])
    pe.raw(lambda eng, val=v.n: eng.wait_ge(v.sem, val))
    for c in range(NCH):
        pe.op("transpose", psT, act_s[:, c * CS:(c + 1) * CS], ident_s[:, :])
        pe.mark()
        v.wait(pe, pe.n)
        v.op("tensor_copy", ddtT_s[:, c, :], psT)
        pe.raw(lambda eng, val=v.n: eng.wait_ge(v.sem, val))
    a.op("activation", dec8_s[:, :],
         Acum_s[:, :].rearrange("p (c l) -> p c l", c=NCH)[:, :, CS - 1:CS].squeeze(2),
         AF.Exp)
    qs.wait(a, a.n)
    f1 = qs.dma(decflat_h[0, :].rearrange("(h c) -> h c", h=128), dec8_s[:, :])
    qs.wait(qs, f1)
    f2 = qs.dma(decB_s[:, :], decflat_h[0:1, :].broadcast_to([128, 1024]))
    v.wait(qs, f2)

    for d in range(128, 144):
        NS_ = inproj_tile(d)
        v.wait(pe, pe.n)
        v.op("tensor_copy", hbc_s[:, 0:32], NS_[2][2][:, :])
        v.op("tensor_copy", hbc_s[:, 32:544], NS_[0][2][:, :])
        v.op("tensor_copy", hbc_s[:, 544:1056], NS_[1][2][:, :])
        inproj_done(d)
        cidx = d - 64
        conv_and_silu(d, cidx)
        qs.wait(a, a.n)
        st = qs.dma(bc_h[(d - 128) * 128:(d - 127) * 128, :], act_s[:, 0:T])
        a.wait(qs, st)
        if d < 136:
            pass

    for g in range(8):
        lb = qs.dma(hbc_s[:, 0:T], bc_h[g * 128:(g + 1) * 128, :])
        qs.wait(qs, lb)
        tb = qs.dmat(Btm_s[:, :, :], hbc_s[:, 0:T].rearrange("n (c s) -> n c s", c=NCH))
        pe.raw(lambda eng, val=tb: eng.wait_ge(qs.sem, val))
        for j in range(8):
            d = 64 + g * 8 + j
            NS_ = inproj_tile(d)
            v.wait(pe, pe.n)
            v.op("tensor_copy", hbc_s[:, 0:32], NS_[2][2][:, :])
            v.op("tensor_copy", hbc_s[:, 32:544], NS_[0][2][:, :])
            v.op("tensor_copy", hbc_s[:, 544:1056], NS_[1][2][:, :])
            inproj_done(d)
            conv_and_silu(d, d - 64)
            qs.wait(a, a.n)
            tx = qs.dmat(xtmg_s[:, :, j, :], act_s[:, :].rearrange("r (c t) -> r c t", c=NCH))
            qs.wait(qs, tx)
            st = qs.dma(xtm_h[:, (d - 64) * 128:(d - 63) * 128]
                        .rearrange("(c tl) ch -> tl c ch", c=NCH), xtmg_s[:, :, j, :])
            v.wait(qs, st)
            for c in range(NCH):
                hv = ddtT_s[:, c, g * 16 + j * 2: g * 16 + j * 2 + 2]
                v.op("tensor_tensor", xsl_s[:, c, j, :].rearrange("p (h q) -> p h q", h=2),
                     xtmg_s[:, c, j, :].rearrange("p (h q) -> p h q", h=2),
                     hv.unsqueeze(2).broadcast_to([128, 2, 64]), ALU.mult)
        v.op("memset", Rg_s[:, :], 0.0)
        for c in range(NCH):
            pe.raw(lambda eng, val=v.n: eng.wait_ge(v.sem, val))
            for j in range(8):
                pe.mm(psS[:, j * 128:(j + 1) * 128], Btm_s[:, c, :],
                      xsl_s[:, c, j, :], start=True, stop=True)
            pe.mark()
            v.op("tensor_copy", Rst_s[:, :], Rg_s[:, :])
            qs.wait(v, v.n)
            sr = qs.dma(Rloc_h[c, :, g * 1024:(g + 1) * 1024], Rst_s[:, :])
            v.wait(pe, pe.n)
            dv = decB_s[:, :].rearrange("p (h c) -> p h c", h=128)[:, g * 16:(g + 1) * 16, c:c + 1]
            v.op("tensor_tensor", Rt_s[:, :].rearrange("p (h q) -> p h q", h=16),
                 Rg_s[:, :].rearrange("p (h q) -> p h q", h=16),
                 dv.broadcast_to([128, 16, 64]), ALU.mult)
            v.op("tensor_tensor", Rg_s[:, :], Rt_s[:, :], psS[:, :], ALU.add)
        qs.wait(v, v.n)
        sz = qs.dma(Z_h[:, g * 1024:(g + 1) * 1024], Rg_s[:, :])
        v.wait(qs, sz)

    for d in range(64):
        NS_ = inproj_tile(d, halo_too=False)
        a.wait(pe, pe.n)
        for ci in range(2):
            a.op("activation", act_s[:, ci * 512:(ci + 1) * 512], NS_[ci][2][:, :], AF.Silu)
        v.wait(a, a.n)
        v.op("tensor_scalar", act_s[:, 0:T], act_s[:, 0:T],
             normw_s[:, d:d + 1], None, ALU.mult)
        qs.wait(v, v.n)
        st = qs.dma(gate_h[d * 128:(d + 1) * 128, :], act_s[:, 0:T])
        v.wait(qs, st)
        a.wait(qs, st)
        inproj_done(d)

    ctx.finish_outputs()
    return ctx




def build_phaseA2a(ctx=None):
    if ctx is None:
        ctx = Ctx("phaseA2a")
    xtm_h = ctx.din("xtm", (T, INTER), BF16)
    gate_h = ctx.din("gate", (INTER, T), BF16)
    bc_h = ctx.din("bc", (2 * G * N, T), BF16)
    dt_h = ctx.din("dt", (128, T))
    alog = ctx.din("A_log", (128, 1))
    Rloc_h = ctx.din("Rloc", (NCH, 128, INTER))
    Zall_h = ctx.din("Z_all", (8, 128, INTER))
    cumAall = ctx.din("cumA_all", (128, 8))
    maskR = ctx.din("carrymaskR", (128, 8))
    D_B = ctx.din("D_B", (128, 128))
    tri = ctx.din("tri", (128, 128), BF16)
    ident = ctx.din("ident", (128, 128), BF16)
    identf = ctx.din("ident_f", (128, 128))
    z_h = ctx.dout("z", (INTER, T), BF16)
    acumflat_h = ctx.dram("acumflat", (NCH, 128 * 128))
    wjflat_h = ctx.dram("wjflat", (1, 1024))
    eprflat_h = ctx.dram("eprflat", (1, 1024))

    dt_s = ctx.sb("dt_s", [128, T], F32)
    a_s = ctx.sb("a_s", [128, T], F32)
    Acum_s = ctx.sb("Acum_s", [128, T], F32)
    zeros_s = ctx.sb("zeros_s", [128, CS], F32)
    alog_s = ctx.sb("alog_s", [128, 1], F32)
    A_col = ctx.sb("A_col", [128, 1], F32)
    AcumT_s = ctx.sb("AcumT_s", [128, NCH, 128], F32)
    exppT_s = ctx.sb("exppT_s", [128, NCH, 128], F32)
    dtT_s = ctx.sb("dtT_s", [128, NCH, 128], F32)
    dtb16 = ctx.sb("dtb16", [128, T], BF16)
    B_s = ctx.sb("B_s", [128, G, T], BF16)
    C_s = ctx.sb("C_s", [128, G, T], BF16)
    tri_s = ctx.sb("tri_s", [128, 128], BF16)
    ident_s = ctx.sb("ident_s", [128, 128], BF16)
    identf_s = ctx.sb("identf_s", [128, 128], F32)
    DB_s = ctx.sb("DB_s", [128, 128], F32)
    maskR_s = ctx.sb("maskR_s", [128, 8], F32)
    cumA_s = ctx.sb("cumA_s", [128, 8], F32)
    Tall_s = ctx.sb("Tall_s", [128, 8], F32)
    Tsel_s = ctx.sb("Tsel_s", [128, 1], F32)
    wj_s = ctx.sb("wj_s", [128, 8], F32)
    zer8 = ctx.sb("zer8", [128, 8], F32)
    wjB_s = ctx.sb("wjB_s", [128, 1024], F32)
    EprB_s = ctx.sb("EprB_s", [128, 1024], F32)
    Epr_s = ctx.sb("Epr_s", [128, 8], F32)
    carry_all = ctx.sb("carry_all", [128, G, 1024], BF16)
    carry_f = ctx.sb("carry_f", [128, 1024], F32)
    zj_s = ctx.sb("zj_s", [128, 1024], F32)
    AROW_s2 = [ctx.sb("AROW_s", [128, 2048], F32),
               ctx.sb("AROW_sb", [128, 2048], F32)]
    Lsl_s = ctx.sb("Lsl_s", [128, 2048], BF16)
    gm_s = ctx.sb("gm_s", [128, 128], BF16)
    M_s = ctx.sb("M_s", [128, 2048], BF16)
    xtm_c2 = [ctx.sb("xtm_c", [128, 1024], BF16), ctx.sb("xtm_cb", [128, 1024], BF16)]
    xdt_s = ctx.sb("xdt_s", [128, 1024], BF16)
    Rloc_b2 = [ctx.sb("Rloc_b", [128, 1024], F32), ctx.sb("Rloc_bb", [128, 1024], F32)]
    Rf_s = ctx.sb("Rf_s", [128, 1024], F32)
    rfb_s = ctx.sb("rfb_s", [128, 1024], BF16)
    yt1_s = ctx.sb("yt1_s", [128, 1024], F32)
    y1_s = yt1_s
    ycs = ctx.sb("ycs", [128, G, 1024], BF16)
    sqsc = None
    ssqc = ctx.sb("ssqc", [128, 8], F32)
    ssq = ctx.sb("ssq", [128, 1], F32)
    rstd = ctx.sb("rstd", [128, 1], F32)
    ytn_g = ctx.sb("ytn_g", [128, 1024], BF16)
    yfm_s = ctx.sb("yfm_s", [128, 64, 128], BF16)
    ggc_s = ctx.sb("ggc_s", [128, 64, 128], BF16)
    zfm_s = ctx.sb("zfm_s", [128, 64, 128], BF16)

    psT = ctx.ps("psT", [128, 128], F32)
    psG = ctx.ps("psG", [128, 128], F32)
    psY = ctx.ps("psY", [128, 1024], F32)
    psO = ctx.ps("psO", [128, 1024], F32)

    pe, v, a, qs, qg = ctx.pe, ctx.v, ctx.a, ctx.qs, ctx.qg

    qg.dma(tri_s[:, :], tri[:, :])
    qg.dma(ident_s[:, :], ident[:, :])
    qg.dma(identf_s[:, :], identf[:, :])
    qg.dma(DB_s[:, :], D_B[:, :])
    qg.dma(maskR_s[:, :], maskR[:, :])
    qg.dma(cumA_s[:, :], cumAall[:, :])
    qg.dma(alog_s[:, :], alog[:, :])
    qg.dma(B_s[:, :, :], bc_h[0:G * N, :].rearrange("(g n) t -> n g t", n=128))
    lBC = qg.dma(C_s[:, :, :], bc_h[G * N:2 * G * N, :].rearrange("(g n) t -> n g t", n=128))
    ldt = qs.dma(dt_s[:, :], dt_h[:, :])

    a.wait(qg, lBC)
    a.op("activation", A_col[:, :], alog_s[:, :], AF.Exp)
    v.wait(qs, ldt)
    v.wait(a, a.n)
    v.op("memset", zeros_s[:, :], 0.0)
    v.op("memset", zer8[:, :], 0.0)
    v.op("tensor_scalar", A_col[:, :], A_col[:, :], -1.0, None, ALU.mult)
    v.op("tensor_scalar", a_s[:, :], dt_s[:, :], A_col[:, 0:1], None, ALU.mult)
    for c in range(NCH):
        v.op("tensor_tensor_scan", Acum_s[:, c * CS:(c + 1) * CS],
             a_s[:, c * CS:(c + 1) * CS], zeros_s[:, :], 0.0, ALU.add, ALU.add)
    v.op("tensor_copy", dtb16[:, :], dt_s[:, :])
    pe.raw(lambda eng, val=v.n: eng.wait_ge(v.sem, val))
    for c in range(NCH):
        pe.op("transpose", psT[:, :], Acum_s[:, c * CS:(c + 1) * CS], identf_s[:, :])
        pe.mark()
        v.wait(pe, pe.n)
        v.op("tensor_copy", AcumT_s[:, c, :], psT[:, :])
        pe.raw(lambda eng, val=v.n: eng.wait_ge(v.sem, val))
        pe.op("transpose", psT[:, :], dt_s[:, c * CS:(c + 1) * CS], identf_s[:, :])
        pe.mark()
        v.wait(pe, pe.n)
        v.op("tensor_copy", dtT_s[:, c, :], psT[:, :])
        pe.raw(lambda eng, val=v.n: eng.wait_ge(v.sem, val))
    a.wait(v, v.n)
    for c in range(NCH):
        a.op("activation", exppT_s[:, c, :], AcumT_s[:, c, :], AF.Exp)
    qs.wait(v, v.n)
    fl = 0
    for c in range(NCH):
        fl = qs.dma(acumflat_h[c, :].rearrange("(h l) -> h l", h=128),
                    Acum_s[:, c * CS:(c + 1) * CS])
    qg.wait(qs, fl)

    v.op("tensor_tensor_scan", Tall_s[:, :], cumA_s[:, :], zer8[:, :], 0.0,
         ALU.add, ALU.add)
    v.op("tensor_tensor", wj_s[:, :], cumA_s[:, :], maskR_s[:, :], ALU.mult)
    v.op("reduce_sum", Tsel_s[:, :], wj_s[:, :], axis=AX.X)
    v.op("tensor_scalar", wj_s[:, :], Tall_s[:, :], Tsel_s[:, 0:1], None, ALU.subtract)
    v.op("tensor_scalar", wj_s[:, :], wj_s[:, :], 0.0, None, ALU.max)
    a.wait(v, v.n)
    a.op("activation", wj_s[:, :], wj_s[:, :], AF.Exp, scale=-1.0)
    v.wait(a, a.n)
    v.op("tensor_tensor", wj_s[:, :], wj_s[:, :], maskR_s[:, :], ALU.mult)
    qs.wait(v, v.n)
    f1 = qs.dma(wjflat_h[0, :].rearrange("(h j) -> h j", h=128), wj_s[:, :])
    qs.wait(qs, f1)
    f2 = qs.dma(wjB_s[:, :], wjflat_h[0:1, :].broadcast_to([128, 1024]))
    totv = Acum_s[:, :].rearrange("p (c l) -> p c l", c=NCH)[:, :, CS - 1:CS].squeeze(2)
    v.op("tensor_tensor_scan", Epr_s[:, :], totv, zer8[:, :], 0.0, ALU.add, ALU.add)
    a.wait(v, v.n)
    a.op("activation", Epr_s[:, 1:8], Epr_s[:, 0:7], AF.Exp)
    v.wait(a, a.n)
    v.op("memset", Epr_s[:, 0:1], 1.0)
    qs.wait(v, v.n)
    f3 = qs.dma(eprflat_h[0, :].rearrange("(h c) -> h c", h=128), Epr_s[:, :])
    qs.wait(qs, f3)
    f4 = qs.dma(EprB_s[:, :], eprflat_h[0:1, :].broadcast_to([128, 1024]))
    v.wait(qs, f4)

    for g in range(G):
        v.op("memset", carry_f[:, :], 0.0)
        for j in range(8):
            lz = qs.dma(zj_s[:, :], Zall_h[j, :, g * 1024:(g + 1) * 1024])
            v.wait(qs, lz)
            wv_ = wjB_s[:, :].rearrange("p (h j) -> p h j", h=128)[:, g * 16:(g + 1) * 16, j:j + 1]
            v.op("tensor_tensor", zj_s[:, :].rearrange("p (h q) -> p h q", h=16),
                 zj_s[:, :].rearrange("p (h q) -> p h q", h=16),
                 wv_.broadcast_to([128, 16, 64]), ALU.mult)
            v.op("tensor_tensor", carry_f[:, :], carry_f[:, :], zj_s[:, :], ALU.add)
            qs.wait(v, v.n)
        v.op("tensor_copy", carry_all[:, g, :], carry_f[:, :])

    tiles = [(c, g) for c in range(NCH) for g in range(G)]
    loads = {}
    vdone_t = {}
    _g_read = [0]
    _yo_read = [0]

    def issue_loads(i):
        c, g = tiles[i]
        p = i % 2
        if i >= 2:
            qs.wait(v, vdone_t[i - 2])
            qg.wait(v, vdone_t[i - 2])
        lx = qs.dma(xtm_c2[p][:, :], xtm_h[c * 128:(c + 1) * 128, g * 1024:(g + 1) * 1024])
        lr = qs.dma(Rloc_b2[p][:, :], Rloc_h[c, :, g * 1024:(g + 1) * 1024])
        la = qg.dma(AROW_s2[p][:, :],
                    acumflat_h[c:c + 1, g * 2048:(g + 1) * 2048].broadcast_to([128, 2048]))
        loads[i] = (lx, lr, la)

    issue_loads(0)
    for ti_ in range(len(tiles)):
        c, g = tiles[ti_]
        if True:
            p = ti_ % 2
            xtm_c, Rloc_b, AROW_s = xtm_c2[p], Rloc_b2[p], AROW_s2[p]
            lx, lr, la = loads[ti_]
            if ti_ + 1 < len(tiles):
                issue_loads(ti_ + 1)
            v.wait(qg, la)
            acv = AcumT_s[:, c, g * 16:(g + 1) * 16].unsqueeze(2).broadcast_to([128, 16, 128])
            v.op("tensor_tensor", AROW_s[:, :].rearrange("p (h l) -> p h l", h=16),
                 AROW_s[:, :].rearrange("p (h l) -> p h l", h=16), acv, ALU.subtract)
            v.op("tensor_scalar", AROW_s[:, :], AROW_s[:, :], 0.0, None, ALU.min)
            a.wait(v, v.n)
            a.op("activation", Lsl_s[:, :], AROW_s[:, :], AF.Exp)
            if _g_read[0]:
                pe.raw(lambda eng, val=_g_read[0]: eng.wait_ge(v.sem, val))
            pe.mm(psG[:, :], B_s[:, g, c * CS:(c + 1) * CS],
                  C_s[:, g, c * CS:(c + 1) * CS], start=True, stop=True)
            pe.mark()
            v.wait(pe, pe.n)
            v.op("tensor_tensor", gm_s[:, :], psG[:, :], tri_s[:, :], ALU.mult)
            _g_read[0] = v.n
            v.wait(a, a.n)
            gmv = gm_s[:, :].unsqueeze(1).broadcast_to([128, 16, 128])
            v.op("tensor_tensor", M_s[:, :].rearrange("p (h l) -> p h l", h=16),
                 gmv, Lsl_s[:, :].rearrange("p (h l) -> p h l", h=16), ALU.mult)
            v.wait(qs, lx)
            dtv = dtT_s[:, c, g * 16:(g + 1) * 16].unsqueeze(2).broadcast_to([128, 16, 64])
            v.op("tensor_tensor", xdt_s[:, :].rearrange("p (h q) -> p h q", h=16),
                 xtm_c[:, :].rearrange("p (h q) -> p h q", h=16), dtv, ALU.mult)
            v.wait(qs, lr)
            epv = EprB_s[:, :].rearrange("p (h cc) -> p h cc", h=128)[:, g * 16:(g + 1) * 16, c:c + 1]
            v.op("tensor_tensor", Rf_s[:, :].rearrange("p (h q) -> p h q", h=16),
                 carry_all[:, g, :].rearrange("p (h q) -> p h q", h=16),
                 epv.broadcast_to([128, 16, 64]), ALU.mult)
            v.op("tensor_tensor", Rf_s[:, :], Rf_s[:, :], Rloc_b[:, :], ALU.add)
            v.op("tensor_copy", rfb_s[:, :], Rf_s[:, :])
            pe.raw(lambda eng, val=v.n: eng.wait_ge(v.sem, val))
            for hl in range(16):
                pe.mm(psY[:, hl * 64:(hl + 1) * 64], M_s[:, hl * 128:(hl + 1) * 128],
                      xdt_s[:, hl * 64:(hl + 1) * 64], start=True, stop=True)
            for hl in range(16):
                pe.mm(psO[:, hl * 64:(hl + 1) * 64], C_s[:, g, c * CS:(c + 1) * CS],
                      rfb_s[:, hl * 64:(hl + 1) * 64], start=True, stop=True)
            pe.mark()
            v.wait(pe, pe.n)
            epx = exppT_s[:, c, g * 16:(g + 1) * 16].unsqueeze(2).broadcast_to([128, 16, 64])
            v.op("tensor_tensor", yt1_s[:, :].rearrange("p (h q) -> p h q", h=16),
                 psO[:, :].rearrange("p (h q) -> p h q", h=16), epx, ALU.mult)
            v.op("tensor_tensor", y1_s[:, :], yt1_s[:, :], psY[:, :], ALU.add)
            _yo_read[0] = v.n
            for hl in range(16):
                hglob = g * 16 + hl
                v.op("scalar_tensor_tensor", ycs[:, g, hl * 64:(hl + 1) * 64],
                     xtm_c[:, hl * 64:(hl + 1) * 64], DB_s[:, hglob:hglob + 1],
                     y1_s[:, hl * 64:(hl + 1) * 64], ALU.mult, ALU.add)
            pe.raw(lambda eng, val=_yo_read[0]: eng.wait_ge(v.sem, val))
            vdone_t[ti_] = v.n
            if g != G - 1:
                continue
        a.wait(v, v.n)
        for g in range(G):
            a.op("activation", yt1_s[:, :], ycs[:, g, :], AF.Square,
                 accum_out=ssqc[:, g:g + 1])
        v.wait(a, a.n)
        v.op("reduce_sum", ssq[:, :], ssqc[:, :], axis=AX.X)
        v.op("tensor_scalar", ssq[:, :], ssq[:, :], 1.0 / INTER, EPS, ALU.mult, ALU.add)
        a.wait(v, v.n)
        a.op("activation", rstd[:, :], ssq[:, :], AF.Sqrt)
        v.wait(a, a.n)
        v.op("reciprocal", rstd[:, :], rstd[:, :])
        qs.wait(v, v.n)
        lg_ = qs.dma(ggc_s[:, :, :],
                     gate_h[:, c * 128:(c + 1) * 128].rearrange("(a r) t -> r a t", r=128))
        tx = 0
        for g in range(G):
            v.op("tensor_scalar", ytn_g[:, :], ycs[:, g, :], rstd[:, 0:1], None, ALU.mult)
            qs.wait(v, v.n)
            tx = qs.dmat(yfm_s[:, g * 8:(g + 1) * 8, :],
                         ytn_g[:, :].rearrange("t (a r) -> t a r", a=8))
            v.wait(qs, tx)
        v.wait(qs, lg_)
        v.op("tensor_tensor", zfm_s[:, :, :], yfm_s[:, :, :], ggc_s[:, :, :], ALU.mult)
        qs.wait(v, v.n)
        sz = qs.dma(z_h[:, c * 128:(c + 1) * 128].rearrange("(a r) t -> r a t", r=128),
                    zfm_s[:, :, :])
        v.wait(qs, sz)

    ctx.finish_outputs()
    return ctx


def build_phaseA2b(ctx=None):
    if ctx is None:
        ctx = Ctx("phaseA2b")
    x_in = ctx.din("x_in", (HT, DIM))
    z_h = ctx.din("z", (INTER, T), BF16)
    WT = ctx.din("W_outT", (INTER, DIM))
    x1 = ctx.dout("x1", (T, DIM))
    wo_h = ctx.dram("wo_h", (DIM, T), BF16)

    zT_s = ctx.sb("zT_s", [128, 64, 512], BF16)
    wb = [ctx.sb("wb0", [128, 64, 128], BF16), ctx.sb("wb1", [128, 64, 128], BF16)]
    stage = ctx.sb("stage", [128, 512], BF16)
    ln = LnKit(ctx, "a2")
    psA = ctx.ps("psA", [128, 512], F32)
    psB = ctx.ps("psB", [128, 512], F32)

    pe, v, a, qs, qg = ctx.pe, ctx.v, ctx.a, ctx.qs, ctx.qg

    marks = {}
    vdone = {}
    for th in range(2):
        lz = qs.dma(zT_s[:, :, :],
                    z_h[:, th * 512:(th + 1) * 512].rearrange("(a k) t -> k a t", k=128))
        qs.wait(pe, pe.n)
        pe.raw(lambda eng, val=lz: eng.wait_ge(qs.sem, val))
        for d in range(32):
            key = (th, d)
            if d >= 2:
                qg.wait(pe, marks[(th, d - 2)])
            ld = qg.dma(wb[d % 2][:, :, :],
                        WT[:, d * 128:(d + 1) * 128].rearrange("(a k) m -> k a m", k=128))
            pe.raw(lambda eng, val=ld: eng.wait_ge(qg.sem, val))
            if d >= 2:
                pe.raw(lambda eng, val=vdone[(th, d - 2)]: eng.wait_ge(v.sem, val))
            ps = psA if d % 2 == 0 else psB
            for k in range(64):
                pe.mm(ps[:, :], wb[d % 2][:, k, :], zT_s[:, k, :],
                      start=(k == 0), stop=(k == 63))
            pe.mark()
            marks[key] = pe.n
            v.wait(pe, pe.n)
            v.op("tensor_copy", stage[:, :], ps[:, :])
            qs.wait(v, v.n)
            st = qs.dma(wo_h[d * 128:(d + 1) * 128, th * 512:(th + 1) * 512], stage[:, :])
            v.wait(qs, st)
            vdone[key] = v.n

    emit_post_ln_residual(ctx, ln, wo_h, x_in[HALO:HALO + T, :], x1)
    ctx.finish_outputs()
    return ctx




def build_mid1():
    ctx = Ctx("mid1", final_outputs={"x1"})
    m = ctx.phase_mark()
    build_phaseA2a(ctx)
    ctx.phase_free(m)
    build_phaseA2b(ctx)
    return ctx


def build_mid2():
    ctx = Ctx("mid2", final_outputs={"x2", "q_tm", "k_tm", "v_tm"})
    m = ctx.phase_mark()
    build_phaseB(ctx)
    ctx.phase_free(m)
    build_phaseC1(ctx)
    return ctx


def build_tail():
    ctx = Ctx("tail", final_outputs={"x4"})
    m = ctx.phase_mark()
    build_phaseC2(ctx)
    ctx.phase_free(m)
    build_phaseD(ctx)
    return ctx



_BUILT = {}


def _get(name):
    if name not in _BUILT:
        fn = globals().get("build_phase" + name) or globals()["build_" + name]
        ctx = fn()
        nc = ctx.build()
        _BUILT[name] = (ctx, nc)
    return _BUILT[name]


def kernel(**inputs):
    com = prep_common(inputs)
    cores = [prep_core(c, inputs) for c in range(NCORE)]
    live = [dict() for _ in range(NCORE)]

    def resolve(ctx, c):
        m = {}
        for n in ctx.input_names:
            if n in live[c]:
                m[n] = live[c][n]
            elif n == "kprev_tm":
                m[n] = live[c - 1]["k_tm"] if c > 0 else np.zeros((T, DIM), BF)
            elif n == "vprev_tm":
                m[n] = live[c - 1]["v_tm"] if c > 0 else np.zeros((T, DIM), BF)
            elif n == "Z_all":
                m[n] = np.ascontiguousarray(np.stack([live[j]["Z"] for j in range(NCORE)], 0))
            elif n == "cumA_all":
                m[n] = np.ascontiguousarray(
                    np.concatenate([live[j]["cumA"] for j in range(NCORE)], 1))
            elif n in com:
                m[n] = com[n]
            else:
                m[n] = cores[c][n]
        return m

    def launch(name, renames=None):
        ctx, nc = _get(name)
        in_maps = [resolve(ctx, c) for c in range(NCORE)]
        res = run_bass_kernel_spmd(nc, in_maps, core_ids=list(range(NCORE)))
        for c in range(NCORE):
            for on in ctx.output_names:
                key = renames.get(on, on) if renames else on
                live[c][key] = np.asarray(res.results[c][on])
        return res

    launch("A1")
    launch("mid1")
    launch("mid2")
    launch("tail")
    out = np.concatenate([live[c]["x4"] for c in range(NCORE)], 0)
    return out.reshape(1, 8192, DIM).astype(np.float32)
